# version 16
# baseline (speedup 1.0000x reference)
import numpy as np
import ml_dtypes
import concourse.bass as bass
import concourse.mybir as mybir
from concourse.bass_utils import run_bass_kernel_spmd
from contextlib import ExitStack

F32 = mybir.dt.float32
BF16 = mybir.dt.bfloat16
U8 = mybir.dt.uint8
I32 = mybir.dt.int32
AF = mybir.ActivationFunctionType
ALU = mybir.AluOpType
AX = mybir.AxisListType

NB = 2
SEQ = 2048
D = 1024
NH = 16
DH = 64
NK = 8
CH = 128
NCH = NB * (SEQ // CH)
QSCALE = 0.125
EPS = 1e-5
NEGBIG = 1.0e9
I_FF = 2048
NM = 16
NE = 8
SWTOK = 1024
NSW = NB * SEQ // SWTOK

PHASE_A = True
PHASE_B = True
SPARSE = True
DEBUG_SPARSE = False
SPARSE_STAGE = 4
TN = 15


def make_consts():
    ident_bf = np.eye(128, dtype=ml_dtypes.bfloat16)
    ident_f32 = np.eye(128, dtype=np.float32)
    utri = np.triu(np.ones((128, 128), dtype=np.float32))
    onesk1 = np.ones((1, 128), dtype=np.float32)
    oh127 = np.zeros((128, 1), dtype=np.float32)
    oh127[127, 0] = 1.0
    iota8 = np.broadcast_to(np.arange(8, dtype=np.float32), (128, 8)).copy()
    big8 = iota8 - 256.0
    negtri = np.where(np.triu(np.ones((128, 128), dtype=bool)), 0.0, NEGBIG).astype(np.float32)
    utri_strict = np.triu(np.ones((128, 128), dtype=np.float32), 1)
    iota_row = np.arange(128, dtype=np.float32).reshape(1, 128)
    iota_col = np.arange(128, dtype=np.float32).reshape(128, 1)
    onescol = np.ones((128, 1), dtype=np.float32)
    actc = np.zeros((128, 8), dtype=np.float32)
    actc[:] = np.array([1e-5, 1.0, -1.0, -0.5, 0.125, -0.125, 1.0 / 1024.0, 0.0], dtype=np.float32)
    return dict(c_actconst=actc, c_utri_strict=utri_strict, c_iota_row=iota_row,
                c_iota_col=iota_col,c_ident_bf=ident_bf, c_ident_f32=ident_f32, c_utri=utri,
                c_onesk1=onesk1, c_oh127=oh127, c_iota8=iota8, c_big8=big8,
                c_negtri=negtri, c_onescol=onescol)


class Arena:
    def __init__(self, nc, name, nbytes):
        self.t = nc.alloc_sbuf_tensor(name, [128, nbytes], U8)
        self.off = 0
        self.marks = []

    def take(self, shape, dtype, name=""):
        esz = {F32: 4, BF16: 2, U8: 1}[dtype]
        n = int(np.prod(shape[1:])) * esz
        off = (self.off + 127) // 128 * 128
        self.off = off + n
        assert self.off <= self.t.shape[1], f"arena overflow at {name}: {self.off}"
        ap = self.t.ap()[:, off:off + n].bitcast(dtype)
        if len(shape) > 2:
            dims = " ".join(f"d{i}" for i in range(1, len(shape)))
            ap = ap.rearrange(f"p ({dims}) -> p {dims}",
                              **{f"d{i}": shape[i] for i in range(1, len(shape) - 1)})
        if shape[0] != 128:
            ap = ap[:shape[0]]
        return ap


def build_nc():
    nc = bass.Bass("TRN2", target_bir_lowering=False, debug=False)

    x_d = nc.dram_tensor("x", [NB, SEQ, D], F32, kind="ExternalInput").ap()
    wq_d = nc.dram_tensor("wq", [D, D], F32, kind="ExternalInput").ap()
    wk_d = nc.dram_tensor("wk", [D, D], F32, kind="ExternalInput").ap()
    wv_d = nc.dram_tensor("wv", [D, D], F32, kind="ExternalInput").ap()
    wo_d = nc.dram_tensor("wo", [D, D], F32, kind="ExternalInput").ap()
    wg_d = nc.dram_tensor("wgate", [NH, D], F32, kind="ExternalInput").ap()
    bg_d = nc.dram_tensor("bgate", [1, NH], F32, kind="ExternalInput").ap()
    gw_d = nc.dram_tensor("gate_w", [D, NE], F32, kind="ExternalInput").ap()
    shg_d = nc.dram_tensor("shg", [D, I_FF], F32, kind="ExternalInput").ap()
    shu_d = nc.dram_tensor("shu", [D, I_FF], F32, kind="ExternalInput").ap()
    shd_d = nc.dram_tensor("shd", [I_FF, D], F32, kind="ExternalInput").ap()
    rtg_d = nc.dram_tensor("rtg", [NE, D, I_FF], F32, kind="ExternalInput").ap()
    rtu_d = nc.dram_tensor("rtu", [NE, D, I_FF], F32, kind="ExternalInput").ap()
    rtd_d = nc.dram_tensor("rtd", [NE, I_FF, D], F32, kind="ExternalInput").ap()
    cvals = make_consts()
    cst = {k: nc.dram_tensor(k, list(v.shape), F32 if v.dtype == np.float32 else BF16,
                             kind="ExternalInput").ap() for k, v in cvals.items()}
    out_d = nc.dram_tensor("out", [NB, SEQ, D], F32, kind="ExternalOutput").ap()
    if DEBUG_SPARSE:
        dbg1 = nc.dram_tensor("dbg1", [128, 32], F32, kind="ExternalOutput").ap()
        dbg2 = nc.dram_tensor("dbg2", [1, 128], F32, kind="ExternalOutput").ap()
        dbg3 = nc.dram_tensor("dbg3", [512, D], F32, kind="ExternalOutput").ap()

    xsT_scr = nc.dram_tensor("xsT_scr", [NCH, 128, NK, CH], BF16, kind="Internal").ap()
    lnT_scr = nc.dram_tensor("lnT_scr", [NCH, 128, NK, CH], BF16, kind="Internal").ap()
    msk_scr = nc.dram_tensor("msk_scr", [NCH, 128, NE], F32, kind="Internal").ap()
    L_scr = nc.dram_tensor("L_scr", [NCH, NH, CH], F32, kind="Internal").ap()
    xs_scr = nc.dram_tensor("xs_scr", [NCH, 128, D], BF16, kind="Internal").ap()
    sorted_scr = nc.dram_tensor("sorted_scr", [TN * 512, D], BF16, kind="Internal").ap()
    sout_scr = nc.dram_tensor("sout_scr", [TN * 512, D], F32, kind="Internal").ap()
    srtA = nc.dram_tensor("srtA", [1, 256], F32, kind="Internal").ap()
    srtB = nc.dram_tensor("srtB", [32, 8], F32, kind="Internal").ap()
    srtC = nc.dram_tensor("srtC", [1, 128], F32, kind="Internal").ap()

    with ExitStack() as stk:
        e = stk.enter_context

        identb = nc.alloc_sbuf_tensor("identb", [128, 128], BF16)
        identf = nc.alloc_sbuf_tensor("identf", [128, 128], F32)
        utri = nc.alloc_sbuf_tensor("utri", [128, 128], F32)
        onesk1 = nc.alloc_sbuf_tensor("onesk1", [1, 128], F32)
        oh127 = nc.alloc_sbuf_tensor("oh127", [128, 1], F32)
        iota8 = nc.alloc_sbuf_tensor("iota8", [128, 8], F32)
        big8 = nc.alloc_sbuf_tensor("big8", [128, 8], F32)
        negtri = nc.alloc_sbuf_tensor("negtri", [128, 128], F32)
        onescol = nc.alloc_sbuf_tensor("onescol", [128, 1], F32)
        bgB = nc.alloc_sbuf_tensor("bgB", [128, NH], F32)
        bgrow = nc.alloc_sbuf_tensor("bgrow", [1, NH], F32)
        wgT = nc.alloc_sbuf_tensor("wgT", [128, NK, NH], BF16)
        gwS = nc.alloc_sbuf_tensor("gwS", [128, NK, NE], BF16)
        wgtmp = nc.alloc_sbuf_tensor("wgtmp", [16, D], BF16)
        actc = nc.alloc_sbuf_tensor("actc", [128, 8], F32)
        utriS = nc.alloc_sbuf_tensor("utriS", [128, 128], F32)
        iotar = nc.alloc_sbuf_tensor("iotar", [1, 128], F32)
        iotac = nc.alloc_sbuf_tensor("iotac", [128, 1], F32)
        mk_s = nc.alloc_sbuf_tensor("mk_s", [128, 32, 8], F32)
        rk_s = nc.alloc_sbuf_tensor("rk_s", [128, 32, 8], F32)
        srow = nc.alloc_sbuf_tensor("srow", [1, 256], F32)
        car_s = nc.alloc_sbuf_tensor("car_s", [32, 8], F32)
        rows_s = nc.alloc_sbuf_tensor("rows_s", [1, 128], F32)
        rowi_s = nc.alloc_sbuf_tensor("rowi_s", [1, 16], I32)
        carb_s = nc.alloc_sbuf_tensor("carb_s", [128, 32, 8], F32)
        rowb_s = nc.alloc_sbuf_tensor("rowb_s", [128, 128], F32)
        pos_s = nc.alloc_sbuf_tensor("pos_s", [128, 32], F32)
        posi_s = nc.alloc_sbuf_tensor("posi_s", [128, 32], I32)
        widxQ = nc.alloc_sbuf_tensor("widxQ", [128, 16], I32)
        widxD = nc.alloc_sbuf_tensor("widxD", [128, 16], I32)
        zt_s = nc.alloc_sbuf_tensor("zt_s", [128, 1024], BF16)
        for _i, _v in enumerate([1e-5, 1.0, -1.0, -0.5, 0.125, -0.125, 1.0 / 1024.0, 256.0]):
            nc.const_aps.register(F32, _v, actc[:, _i:_i + 1])

        OVB = 194 * 1024
        ar = Arena(nc, "ov", OVB)

        off0 = ar.off
        wqT = ar.take([128, NK, D], BF16, "wqT")
        wkT = ar.take([128, NK, D], BF16, "wkT")
        wvT = ar.take([128, NK, D], BF16, "wvT")
        woT = ar.take([128, NK, D], BF16, "woT")
        wtmp = ar.take([128, NK, D], BF16, "wtmp")
        x_in = [ar.take([128, D], F32, "x0"), ar.take([128, D], F32, "x1")]
        r1 = ar.take([128, D], F32, "r1")
        e1 = ar.take([128, D], F32, "e1")
        e1a = ar.take([128, 512], F32, "e1a")
        def _pp2(shape, dt, nm):
            return [ar.take(shape, dt, nm + "0"), ar.take(shape, dt, nm + "1")]
        T_bc = ar.take([128, NH * CH], F32, "T_bc")
        LTs = ar.take([16, 128], F32, "LTs")
        LCrow = ar.take([1, NH], F32, "LCrow")
        ACrow = ar.take([1, NH], F32, "ACrow")
        def _pp(shape, dt, nm):
            return [ar.take(shape, dt, nm + "0"), ar.take(shape, dt, nm + "1")]
        ln1 = _pp([128, D], BF16, "ln1")
        ln1T = _pp([128, NK, CH], BF16, "ln1T")
        qf0 = ar.take([128, D], BF16, "qf")
        qs0 = ar.take([128, D], BF16, "qs")
        kf0 = ar.take([128, D], BF16, "kf")
        qf, qs, kf = [qf0, qf0], [qs0, qs0], [kf0, kf0]
        dmp = wtmp[:, 0, :].bitcast(BF16) if False else wtmp.rearrange("p k d -> p (k d)")[:, 0:D]
        kh = _pp([128, NH, DH], BF16, "kh")
        vv = _pp([128, NH, DH], BF16, "vv")
        qfT = _pp([128, NK, CH], BF16, "qfT")
        qsT = _pp([128, NK, CH], BF16, "qsT")
        kfT = _pp([128, NK, CH], BF16, "kfT")
        zb = _pp([128, NH], F32, "zb")
        za = _pp([128, NH], F32, "za")
        zl = _pp([128, NH], F32, "zl")
        zm = _pp([128, NH], F32, "zm")
        lcol = _pp([128, NH], F32, "lcol")
        Lm = _pp([128, NH], F32, "Lm")
        Ecol = _pp([128, NH], F32, "Ecol")
        Acol = _pp([128, NH], F32, "Acol")
        ACcol = _pp([128, NH], F32, "ACcol")
        Dbuf = _pp([128, NH, CH], BF16, "Dbuf")
        Pt = _pp([128, NH, CH], BF16, "Pt")
        yT16 = _pp([128, NK, CH], BF16, "yT16")
        Sf = _pp([128, NK, DH], F32, "Sf")
        S16 = _pp([128, NK, DH], BF16, "S16")
        x_new = _pp([128, D], F32, "x_new")
        ln2 = _pp([128, D], BF16, "ln2")
        ln2T = _pp([128, NK, CH], BF16, "ln2T")
        xs = _pp([128, D], BF16, "xs")
        eqm = _pp([128, NE], F32, "eqm")
        cand = _pp([128, NE], F32, "cand")
        masks8 = _pp([128, NE], F32, "masks8")
        scol = _pp([128, 16], F32, "scol")
        A_BYTES = ar.off
        ar.off = off0

        (SC_SUM, SC_M, SC_M2, SC_VAR, SC_LNV, SC_RSTD, SC_SQ, SC_MAX,
         SC_ESC, SC_DEN, SC_SCR, SC_SR2, SC_SEL, SC_NMR) = range(14)

        xsT_s = ar.take([128, NK, SWTOK], BF16, "xsT_s")
        lnT_s = ar.take([128, NK, SWTOK], BF16, "lnT_s")
        acc = ar.take([128, SWTOK // 128, D], F32, "acc")
        actT = ar.take([128, NM, SWTOK], BF16, "actT")
        wgu = [[ar.take([128, NK, 256], BF16, f"wg{i}"), ar.take([128, NK, 256], BF16, f"wu{i}")]
               for i in range(2)]
        wdb = [ar.take([128, NM, 512], BF16, f"wd{i}") for i in range(2)]
        sgb = [ar.take([128, 512], F32, f"sg{i}") for i in range(2)]
        masks_s = ar.take([128, SWTOK // 128, NE], F32, "masks_s")
        B_BYTES = ar.off
        ar.off = off0
        sx_tm = [ar.take([128, 4, D], BF16, "sx0"), ar.take([128, 4, D], BF16, "sx1")]
        sxT = [ar.take([128, NK, 512], BF16, "sxT0"), ar.take([128, NK, 512], BF16, "sxT1")]
        wgr = [ar.take([128, NK, 1024], BF16, "wgr0"), ar.take([128, NK, 1024], BF16, "wgr1")]
        wur = [ar.take([128, NK, 1024], BF16, "wur0"), ar.take([128, NK, 1024], BF16, "wur1")]
        wdr = ar.take([128, NM, 1024], BF16, "wdr")
        actT_r = ar.take([128, NM, 512], BF16, "actT_r")
        dstg = ar.take([128, 4, D], F32, "dstg")
        rst_s = ar.take([128, D], F32, "rst_s")
        rst_s2 = ar.take([128, D], F32, "rst_s2")
        sgb_r = [ar.take([128, 512], F32, "sgr0"), ar.take([128, 512], F32, "sgr1")]
        C_BYTES = ar.off
        assert max(A_BYTES, B_BYTES, C_BYTES) <= OVB, (A_BYTES, B_BYTES, C_BYTES)

        ps = [nc.alloc_psum_tensor(f"ps{i}", [128, 512], F32) for i in range(3)] + [None, None, None]
        ps[4] = nc.alloc_psum_tensor("ps4", [128, 512], F32)
        ps.append(nc.alloc_psum_tensor("ps6", [128, 512], F32))
        small1 = nc.alloc_psum_tensor("small1", [128, 512], F32)
        ps[3] = small1
        pbs = [nc.alloc_psum_tensor("pbt0", [128, 1024], BF16),
               nc.alloc_psum_tensor("pbt1", [128, 1024], BF16)]
        SMALLs = [ps[6], small1]

        sems = {s: e(nc.semaphore(s)) for s in
                ("PE", "VE", "SE", "GP", "DSC", "DXL", "DST", "DSS", "DSB",
                 "DGP", "DGL", "DGW", "DGD", "DGS", "DGA")}
        block = e(nc.Block())

        def emit(who, eng):
            cnt = {k: 0 for k in sems}
            lastw = {}
            DMA_OWNER = {"DSC": "sync", "DXL": "sync", "DST": "sync", "DSS": "sync",
                         "DSB": "sync", "DGP": "gpsimd", "DGL": "gpsimd",
                         "DGW": "gpsimd", "DGD": "gpsimd", "DGS": "gpsimd", "DGA": "gpsimd"}

            def inc(instr, s_name, n=1):
                cnt[s_name] += n
                if instr is not None:
                    instr.then_inc(sems[s_name], n)
                return (s_name, cnt[s_name])

            def wait(ev):
                if ev is None:
                    return
                s_name, v = ev
                if v <= 0 or lastw.get(s_name, 0) >= v:
                    return
                lastw[s_name] = v
                eng.wait_ge(sems[s_name], v)

            def run(owner, s_name, deps, f, dn=1):
                if who == owner:
                    for d_ in deps:
                        wait(d_)
                    return inc(f(), s_name, dn)
                return inc(None, s_name, dn)

            def tens(deps, f):
                return run("tensor", "PE", deps, f)

            def vec(deps, f):
                return run("vector", "VE", deps, f)

            def sca(deps, f):
                return run("scalar", "SE", deps, f)

            def gp(deps, f):
                return run("gpsimd", "GP", deps, f)

            def dma(sem, deps, fs):
                if not isinstance(fs, (list, tuple)):
                    fs = [fs]
                if who == DMA_OWNER[sem]:
                    wait((sem, cnt[sem]))
                    for d_ in deps:
                        wait(d_)
                    for f_ in fs:
                        inc(f_(), sem, 16)
                else:
                    for f_ in fs:
                        inc(None, sem, 16)
                return (sem, cnt[sem])

            ev_c = []
            for tile, src in ((identb, cst["c_ident_bf"]), (identf, cst["c_ident_f32"]),
                              (utri, cst["c_utri"]), (onesk1, cst["c_onesk1"]),
                              (oh127, cst["c_oh127"]), (iota8, cst["c_iota8"]),
                              (big8, cst["c_big8"]), (negtri, cst["c_negtri"]),
                              (onescol, cst["c_onescol"]), (actc, cst["c_actconst"]),
                              (utriS, cst["c_utri_strict"]), (iotar, cst["c_iota_row"]),
                              (iotac, cst["c_iota_col"]), (bgrow, bg_d)):
                ev_c.append(dma('DSC', [], lambda t_=tile, s_=src: eng.dma_start(out=t_[:], in_=s_)))
            ev_iota = ev_c[5]
            ev_big = ev_c[6]
            ev_cst_all = ev_c[-1]
            if who in ("scalar", "vector", "tensor"):
                wait(ev_cst_all)
            ev_gw = dma('DGP', [], lambda: eng.dma_start(out=gwS[:], in_=gw_d.rearrange("(k p) e -> p k e", p=128)))
            ev_wgl = dma('DGP', [], lambda: eng.dma_start(out=wgtmp[:], in_=wg_d))
            ev_bgb_mm = tens([ev_cst_all], lambda: eng.matmul(out=ps[0][:, 0:NH], lhsT=onesk1[:],
                                                              rhs=bgrow[:], start=True, stop=True))
            ev_bgb = vec([ev_bgb_mm], lambda: eng.tensor_copy(out=bgB[:], in_=ps[0][:, 0:NH]))

            wT_tiles = {"wq": wqT, "wk": wkT, "wv": wvT, "wo": woT}
            w_srcs = {"wq": wq_d, "wk": wk_d, "wv": wv_d, "wo": wo_d}
            ev_wT = {}
            ev_wtmp_free = None
            ev_pb_free = None
            for wn in ("wq", "wk", "wv", "wo"):
                ev_l = dma('DGP', [ev_wtmp_free] if ev_wtmp_free else [],
                             lambda wn_=wn: eng.dma_start(out=wtmp[:],
                                                          in_=w_srcs[wn_].rearrange("(nk p) d -> p nk d", p=128)))
                last_ev = None
                for g in range(8):
                    def tf(g_=g):
                        last = None
                        for dk in range(8):
                            last = eng.transpose(out=pbs[0][:, dk * 128:(dk + 1) * 128],
                                                 in_=wtmp[:, g_, dk * 128:(dk + 1) * 128],
                                                 identity=identb[:])
                        return last
                    ev_t = tens([ev_l, ev_c[0], ev_pb_free], tf)

                    def vf(wn_=wn, g_=g):
                        return eng.tensor_copy(
                            out=wT_tiles[wn_][:, :, g_ * 128:(g_ + 1) * 128],
                            in_=pbs[0][:].rearrange("p (k t) -> p k t", k=8))
                    ev_pb_free = vec([ev_t], vf)
                    last_ev = ev_pb_free
                ev_wT[wn] = last_ev
                ev_wtmp_free = last_ev
            ev_wgT_t = tens([ev_wgl, ev_pb_free],
                            lambda: [eng.transpose(out=pbs[0][:, dk * 16:(dk + 1) * 16],
                                                   in_=wgtmp[0:16, dk * 128:(dk + 1) * 128],
                                                   identity=identb[0:16, 0:16]) for dk in range(8)][-1])
            ev_wgT = vec([ev_wgT_t], lambda: eng.tensor_copy(
                out=wgT[:].rearrange("p k h -> p (k h)"), in_=pbs[0][:, 0:128]))
            ev_pb_free = ev_wgT

            ev_xfree = [None, None]
            pc = {}
            ev_Sready = [None, None]
            ev_small_free = [ev_bgb, ev_bgb]
            ev_pA = [ev_bgb, None]
            ev_dmp = [None]
            ev_y = None
            evd_scr = None
            ev_Lw_prev = None
            ev_pbf = [ev_pb_free, None]
            pbi = 0

            if PHASE_A:
                stH = {}
                exl_pend = [None, None]

                def chunk_h1a(ci):
                    nonlocal pbi, evd_scr, ev_Lw_prev, ev_y
                    st = {}
                    c, b = ci // 2, ci % 2
                    cc = b * (SEQ // CH) + c
                    par = ci % 2
                    x_ = x_in[par]
                    ln1_, ln1T_, qf_, qs_, kf_, kh_, vv_ = (ln1[par], ln1T[par], qf[par],
                                                            qs[par], kf[par], kh[par], vv[par])
                    qfT_, qsT_, kfT_ = qfT[par], qsT[par], kfT[par]
                    zb_, za_, zl_, zm_, lcol_, Lm_ = zb[par], za[par], zl[par], zm[par], lcol[par], Lm[par]
                    Ecol_, Acol_, ACcol_ = Ecol[par], Acol[par], ACcol[par]
                    Dbuf_, Pt_, yT16_ = Dbuf[par], Pt[par], yT16[par]
                    x_new_, ln2_, ln2T_, xs_ = x_new[par], ln2[par], ln2T[par], xs[par]
                    eqm_, cand_, masks8_, scol_ = eqm[par], cand[par], masks8[par], scol[par]
                    Sf_, S16_ = Sf[b], S16[b]
                    SMALL = SMALLs[par]
                    if c == 0:
                        ev_Sready[b] = vec([], lambda Sf__=Sf_, S16__=S16_: (
                            eng.memset(Sf__[:].rearrange("p k j -> p (k j)"), 0.0),
                            eng.memset(S16__[:].rearrange("p k j -> p (k j)"), 0.0))[-1])
                    exl = exl_pend[par]
                    ev_ms = sca([exl, pc.get(("scol", par)), ev_dmp[0],
                                 ev_wT["wo"] if ci == 0 else None],
                                lambda x__=x_, s_=scol_: eng.activation(
                                    out=dmp[:], in_=x__[:], func=AF.Identity,
                                    accum_out=s_[:, SC_SUM:SC_SUM + 1]))
                    ev_sq = sca([exl, pc.get(("scolg", par)), ev_ms],
                                lambda x__=x_, s_=scol_: eng.activation(
                                    out=dmp[:], in_=x__[:], func=AF.Square,
                                    accum_out=s_[:, SC_SQ:SC_SQ + 1]))
                    ev_dmp[0] = ev_sq
                    ev_m = vec([ev_ms], lambda s_=scol_: eng.tensor_scalar(
                        out=s_[:, SC_M:SC_M + 1], in0=s_[:, SC_SUM:SC_SUM + 1],
                        scalar1=1.0 / D, scalar2=None, op0=ALU.mult))
                    ev_m2 = vec([ev_m], lambda s_=scol_: eng.tensor_tensor(
                        out=s_[:, SC_M2:SC_M2 + 1], in0=s_[:, SC_M:SC_M + 1],
                        in1=s_[:, SC_M:SC_M + 1], op=ALU.mult))
                    ev_var = vec([ev_m2, ev_sq, pc.get(("sc_sca", par))],
                                 lambda s_=scol_: eng.scalar_tensor_tensor(
                                     out=s_[:, SC_VAR:SC_VAR + 1], in0=s_[:, SC_SQ:SC_SQ + 1],
                                     scalar=1.0 / D, in1=s_[:, SC_M2:SC_M2 + 1],
                                     op0=ALU.mult, op1=ALU.subtract))
                    ev_lnv = sca([ev_var], lambda s_=scol_: eng.activation(
                        out=s_[:, SC_LNV:SC_LNV + 1], in_=s_[:, SC_VAR:SC_VAR + 1], func=AF.Ln, bias=EPS))
                    ev_rstd = sca([ev_lnv], lambda s_=scol_: eng.activation(
                        out=s_[:, SC_RSTD:SC_RSTD + 1], in_=s_[:, SC_LNV:SC_LNV + 1], func=AF.Exp, scale=-0.5))
                    ev_nmr = vec([ev_m, ev_rstd], lambda s_=scol_: eng.scalar_tensor_tensor(
                        out=s_[:, SC_NMR:SC_NMR + 1], in0=s_[:, SC_M:SC_M + 1], scalar=-1.0,
                        in1=s_[:, SC_RSTD:SC_RSTD + 1], op0=ALU.mult, op1=ALU.mult))
                    ev_ln1 = sca([ev_nmr, pc.get(("ln1", par))],
                                 lambda x__=x_, l_=ln1_, s_=scol_: eng.activation(
                                     out=l_[:], in_=x__[:], func=AF.Identity,
                                     scale=s_[:, SC_RSTD:SC_RSTD + 1], bias=s_[:, SC_NMR:SC_NMR + 1]))
                    ev_xcp = gp([exl, pc.get(("x_new", par)), pc.get(("x_new_dma", par)), pc.get(("ln2", par))],
                                lambda x__=x_, xn=x_new[par]: eng.tensor_copy(out=xn[:], in_=x__[:]))
                    st["ev_xcp"] = ev_xcp
                    ev_xfree[par] = [ev_xcp, ev_ln1, ev_sq, ev_ms]
                    if ci + 1 < NCH:
                        b2, c2, p2 = (ci + 1) % 2, (ci + 1) // 2, (ci + 1) % 2
                        exl_pend[p2] = dma('DXL', list(ev_xfree[p2] or []),
                                           lambda b_=b2, c_=c2, p_=p2: eng.dma_start(
                                               out=x_in[p_][:], in_=x_d[b_, c_ * CH:(c_ + 1) * CH, :]))
                    ev_t = tens([ev_ln1, ev_pbf[pbi]],
                                lambda l_=ln1_, pbx=pbs[pbi]: [eng.transpose(
                                    out=pbx[:, j * 128:(j + 1) * 128], in_=l_[:, j * 128:(j + 1) * 128],
                                    identity=identb[:]) for j in range(8)][-1])
                    ev_ln1T = vec([ev_t, pc.get(("ln1T", par))],
                                  lambda d_=ln1T_, pbx=pbs[pbi]: eng.tensor_copy(
                                      out=d_[:].rearrange("p k t -> p (k t)"), in_=pbx[:]))
                    ev_pbf[pbi] = ev_ln1T
                    pbi ^= 1

                    ev_zmm = tens([ev_ln1T, ev_small_free[par], ev_wgT],
                                  lambda l_=ln1T_: [eng.matmul(out=SMALL[:, 0:NH], lhsT=l_[:, k, :],
                                                               rhs=wgT[:, k, :],
                                                               start=(k == 0), stop=(k == NK - 1))
                                                    for k in range(NK)][-1])
                    ev_zb = vec([ev_zmm, ev_bgb, pc.get(("zb", par)), pc.get(("za", par))],
                                lambda z_=zb_: eng.tensor_tensor(out=z_[:], in0=SMALL[:, 0:NH],
                                                                 in1=bgB[:], op=ALU.add))
                    ev_za = sca([ev_zb, pc.get(("za", par))],
                                lambda z_=zb_, a_=za_: eng.activation(out=a_[:], in_=z_[:], func=AF.Abs))
                    ev_ze = sca([ev_za], lambda a_=za_: eng.activation(out=a_[:], in_=a_[:], func=AF.Exp, scale=-1.0))
                    ev_zl = sca([ev_ze], lambda a_=za_, l_=zl_: eng.activation(out=l_[:], in_=a_[:],
                                                                               func=AF.Ln, bias=1.0))
                    ev_zm = vec([ev_zb, pc.get(("zm", par))],
                                lambda z_=zb_, m_=zm_: eng.tensor_scalar(out=m_[:], in0=z_[:], scalar1=0.0,
                                                                         scalar2=None, op0=ALU.min))
                    ev_l = vec([ev_zm, ev_zl, pc.get(("lcol", par))],
                               lambda m_=zm_, l_=zl_, o_=lcol_: eng.tensor_tensor(out=o_[:], in0=m_[:],
                                                                                  in1=l_[:], op=ALU.subtract))
                    ev_Lmm = tens([ev_l, ev_zb, ev_c[2]],
                                  lambda o_=lcol_: eng.matmul(out=SMALL[:, 16:16 + NH], lhsT=utri[:],
                                                              rhs=o_[:], start=True, stop=True))
                    ev_L = vec([ev_Lmm, pc.get(("Lm", par))],
                               lambda L__=Lm_: eng.tensor_copy(out=L__[:], in_=SMALL[:, 16:16 + NH]))
                    ev_LCmm = tens([ev_L], lambda L__=Lm_: eng.matmul(out=SMALL[0:1, 32:32 + NH], lhsT=oh127[:],
                                                                      rhs=L__[:], start=True, stop=True))
                    ev_LC = vec([ev_LCmm, pc.get(("LCrow", 0))],
                                lambda: eng.tensor_copy(out=LCrow[:], in_=SMALL[0:1, 32:32 + NH]))
                    ev_LCb = tens([ev_LC], lambda: eng.matmul(out=SMALL[:, 48:48 + NH], lhsT=onesk1[:],
                                                              rhs=LCrow[:], start=True, stop=True))
                    ev_Ep = vec([ev_LCb, pc.get(("Ecol", par))],
                                lambda L__=Lm_, E_=Ecol_: eng.tensor_tensor(
                                    out=E_[:], in0=SMALL[:, 48:48 + NH], in1=L__[:], op=ALU.subtract))
                    ev_E = sca([ev_Ep], lambda E_=Ecol_: eng.activation(out=E_[:], in_=E_[:], func=AF.Exp))
                    ev_A = sca([ev_L, pc.get(("Acol", par))],
                               lambda L__=Lm_, A__=Acol_: eng.activation(out=A__[:], in_=L__[:], func=AF.Exp))
                    ev_ACr = sca([ev_LC], lambda: eng.activation(out=ACrow[:], in_=LCrow[:], func=AF.Exp))
                    ev_ACb = tens([ev_ACr, ev_Ep], lambda: eng.matmul(out=SMALL[:, 64:64 + NH], lhsT=onesk1[:],
                                                                      rhs=ACrow[:], start=True, stop=True))
                    ev_AC = vec([ev_ACb, pc.get(("ACcol", par))],
                                lambda A__=ACcol_: eng.tensor_copy(out=A__[:], in_=SMALL[:, 64:64 + NH]))
                    ev_LTt = tens([ev_L, ev_AC, ev_c[1]],
                                  lambda L__=Lm_: eng.transpose(out=SMALL[:, 80:208][0:16], in_=L__[:],
                                                                identity=identf[:]))
                    ev_LTs = vec([ev_LTt, ev_Lw_prev],
                                 lambda: eng.tensor_copy(out=LTs[:], in_=SMALL[0:16, 80:208]))
                    ev_small_free[par] = ev_LTs
                    ev_Lw = dma('DGL', [ev_LTs], lambda cc_=cc: eng.dma_start(out=L_scr[cc_], in_=LTs[:]))
                    ev_Lw_prev = ev_Lw
                    ev_Lr = dma('DGL', [evd_scr],
                                lambda cc_=cc: eng.dma_start(
                                    out=T_bc[:],
                                    in_=L_scr[cc_].rearrange("h t -> (h t)").unsqueeze(0).broadcast_to([128, NH * CH])))
                    ev_d1 = gp([ev_Lr, ev_L],
                               lambda L__=Lm_: eng.tensor_tensor(
                                   out=T_bc[:].rearrange("p (h t) -> p h t", h=NH),
                                   in0=T_bc[:].rearrange("p (h t) -> p h t", h=NH),
                                   in1=L__[:].unsqueeze(2).broadcast_to([128, NH, 128]),
                                   op=ALU.subtract))
                    ev_d2 = gp([ev_d1, ev_c[7]], lambda: eng.tensor_tensor(
                        out=T_bc[:].rearrange("p (h t) -> p h t", h=NH),
                        in0=T_bc[:].rearrange("p (h t) -> p h t", h=NH),
                        in1=negtri[:].unsqueeze(1).broadcast_to([128, NH, 128]), op=ALU.subtract))
                    ev_Dx = sca([ev_d2, pc.get(("Dbuf", par))],
                                lambda D__=Dbuf_: eng.activation(
                                    out=D__[:].rearrange("p h t -> p (h t)"),
                                    in_=T_bc[:], func=AF.Exp))
                    ev_D_piece = [ev_Dx, ev_Dx, ev_Dx, ev_Dx]
                    evd_scr = ev_Dx

                    st.update(dict(exl=exl, ev_ln1T=ev_ln1T, ev_E=ev_E, ev_A=ev_A,
                                   ev_AC=ev_AC, ev_sq=ev_sq, ev_t=ev_t, ev_l=ev_l, ev_zl=ev_zl,
                                   ev_Lmm=ev_Lmm, ev_ACr=ev_ACr, ev_rstd=ev_rstd,
                                   ev_D_piece=ev_D_piece))
                    return st

                def chunk_h1b(ci, st):
                    nonlocal pbi, evd_scr, ev_Lw_prev, ev_y
                    c, b = ci // 2, ci % 2
                    cc = b * (SEQ // CH) + c
                    par = ci % 2
                    x_ = x_in[par]
                    ln1_, ln1T_, qf_, qs_, kf_, kh_, vv_ = (ln1[par], ln1T[par], qf[par],
                                                            qs[par], kf[par], kh[par], vv[par])
                    qfT_, qsT_, kfT_ = qfT[par], qsT[par], kfT[par]
                    Ecol_, Acol_ = Ecol[par], Acol[par]
                    ev_ln1T = st["ev_ln1T"]
                    ev_E = st["ev_E"]
                    ev_A = st["ev_A"]
                    ev_sq = st["ev_sq"]
                    def proj(wT_t, ev_w, scale_pm, out_f, out_scaled3, scale_ap, ev_dep_scale,
                             war_r1, war_f, war_scaled):
                        ev_half = [None, None]
                        for hf in range(2):
                            bank = ps[hf]
                            ev_mm = tens([ev_ln1T, ev_w, ev_pA[hf]],
                                         lambda hf_=hf, bank_=bank, w_=wT_t, l_=ln1T_: [
                                             eng.matmul(out=bank_[:], lhsT=l_[:, k, :],
                                                        rhs=w_[:, k, hf_ * 512:(hf_ + 1) * 512],
                                                        start=(k == 0), stop=(k == NK - 1)) for k in range(NK)][-1])
                            ev_r = sca([ev_mm] + ([war_r1] if hf == 0 else []),
                                       lambda hf_=hf, bank_=bank: eng.activation(
                                           out=r1[:, hf_ * 512:(hf_ + 1) * 512], in_=bank_[:],
                                           func=AF.Relu, scale=scale_pm))
                            ev_n = sca([ev_r], lambda bank_=bank: eng.activation(out=e1a[:], in_=bank_[:],
                                                                                 func=AF.Relu, scale=-scale_pm))
                            ev_e = sca([ev_n], lambda hf_=hf: eng.activation(
                                out=e1[:, hf_ * 512:(hf_ + 1) * 512], in_=e1a[:], func=AF.Exp, scale=-1.0))
                            ev_pA[hf] = ev_e
                            ev_half[hf] = ev_e
                        ev_f = gp([ev_half[0], ev_half[1], war_f],
                                  lambda o_=out_f: eng.tensor_tensor(out=o_[:], in0=r1[:], in1=e1[:], op=ALU.add))
                        ev_s2 = gp([ev_f, ev_dep_scale, war_scaled],
                                   lambda o_=out_f, os_=out_scaled3, sap=scale_ap: eng.tensor_tensor(
                                       out=os_, in0=o_[:].rearrange("p (h j) -> p h j", h=NH),
                                       in1=sap, op=ALU.mult))
                        return ev_f, ev_s2

                    A_ap = Acol_[:].unsqueeze(2).broadcast_to([128, NH, DH])
                    E_ap = Ecol_[:].unsqueeze(2).broadcast_to([128, NH, DH])
                    ev_qf, ev_qs = proj(wqT, ev_wT["wq"], QSCALE, qf_,
                                        qs_[:].rearrange("p (h j) -> p h j", h=NH), A_ap, ev_A,
                                        pc.get(("r1", 1 - par)), pc.get(("qf", 1 - par)), pc.get(("qs", 1 - par)))
                    ev_kf, ev_kh = proj(wkT, ev_wT["wk"], 1.0, kf_, kh_[:], E_ap, ev_E,
                                        ev_qf, pc.get(("kf", par)), pc.get(("kh", par)))
                    ev_v = None
                    for hf in range(2):
                        bank = ps[hf]
                        ev_mm = tens([ev_ln1T, ev_wT["wv"], ev_pA[hf]],
                                     lambda hf_=hf, bank_=bank, l_=ln1T_: [
                                         eng.matmul(out=bank_[:], lhsT=l_[:, k, :],
                                                    rhs=wvT[:, k, hf_ * 512:(hf_ + 1) * 512],
                                                    start=(k == 0), stop=(k == NK - 1)) for k in range(NK)][-1])
                        ev_v = vec([ev_mm] + ([pc.get(("vv", par))] if hf == 0 else []),
                                   lambda hf_=hf, bank_=bank, v_=vv_: eng.tensor_copy(
                                       out=v_[:].rearrange("p h j -> p (h j)")[:, hf_ * 512:(hf_ + 1) * 512],
                                       in_=bank_[:]))
                        ev_pA[hf] = ev_v
                    ev_tr = {}
                    for src_t, dst, evp, nm in ((qf_, qfT_, ev_qf, "qfT"), (qs_, qsT_, ev_qs, "qsT"),
                                                (kf_, kfT_, ev_kf, "kfT")):
                        ev_t2 = tens([evp, ev_pbf[pbi]],
                                     lambda s_=src_t, pbx=pbs[pbi]: [eng.transpose(
                                         out=pbx[:, j * 128:(j + 1) * 128],
                                         in_=s_[:, j * 128:(j + 1) * 128],
                                         identity=identb[:]) for j in range(8)][-1])
                        ev_pbf[pbi] = vec([ev_t2, pc.get((nm, par))],
                                          lambda d_=dst, pbx=pbs[pbi]: eng.tensor_copy(
                                              out=d_[:].rearrange("p k t -> p (k t)"), in_=pbx[:]))
                        ev_tr[nm] = ev_pbf[pbi]
                        pbi ^= 1
                    st.update(dict(ev_tr=ev_tr, ev_v=ev_v, ev_kh=ev_kh,
                                   ev_qf=ev_qf, ev_qs=ev_qs, ev_kf=ev_kf))
                    return st

                    st.update(dict(ev_tr=ev_tr, ev_v=ev_v, ev_kh=ev_kh,
                                   ev_qf=ev_qf, ev_qs=ev_qs, ev_kf=ev_kf))
                    return st

                def chunk_h2a(ci, st):
                    nonlocal pbi, evd_scr, ev_Lw_prev, ev_y
                    c, b = ci // 2, ci % 2
                    cc = b * (SEQ // CH) + c
                    par = ci % 2
                    qfT_, qsT_, kfT_ = qfT[par], qsT[par], kfT[par]
                    kh_, vv_ = kh[par], vv[par]
                    Dbuf_, Pt_, yT16_ = Dbuf[par], Pt[par], yT16[par]
                    ACcol_ = ACcol[par]
                    Sf_, S16_ = Sf[b], S16[b]
                    SMALL = SMALLs[par]
                    ev_tr = st["ev_tr"]
                    ev_v = st["ev_v"]
                    ev_kh = st["ev_kh"]
                    ev_D_piece = st["ev_D_piece"]
                    ev_AC = st["ev_AC"]
                    ev_Pe_last = None
                    ev_Pmm_last = None
                    ev_Sd = None
                    ev_yT = None
                    ev_ymm_w = None
                    ev_Pe_hist = {}
                    for wv_i in range(2):
                        for hh in range(8):
                            h = wv_i * 8 + hh
                            bank = ps[2]
                            ev_Pmm = tens([ev_tr["kfT"], ev_tr["qfT"],
                                           ev_Pe_hist.get(h - 4, pc.get(("Pbank", par)) if h < 4 else None)],
                                          lambda h_=h, bank_=bank, kT=kfT_, qT=qfT_: eng.matmul(
                                              out=bank_[:, 128 * (h_ % 4):128 * (h_ % 4) + 128],
                                              lhsT=kT[64 * (h_ % 2):64 * (h_ % 2) + 64, h_ // 2, :],
                                              rhs=qT[64 * (h_ % 2):64 * (h_ % 2) + 64, h_ // 2, :],
                                              start=True, stop=True))
                            ev_Pmm_last = ev_Pmm
                            ev_Pe = vec([ev_Pmm, ev_D_piece[h // 4]] + ([pc.get(("Pt", par))] if h == 0 else []),
                                        lambda h_=h, bank_=bank, P_=Pt_, D__=Dbuf_: eng.tensor_tensor(
                                            out=P_[:, h_, :], in0=bank_[:, 128 * (h_ % 4):128 * (h_ % 4) + 128],
                                            in1=D__[:, h_, :], op=ALU.mult))
                            ev_Pe_last = ev_Pe
                            ev_Pe_hist[h] = ev_Pe

                            def ymm(h_=h, S16__=S16_, qT=qsT_, v_=vv_, P_=Pt_):
                                base_ = 64 * (h_ % 2)
                                kc_ = h_ // 2
                                yoff_ = 128 * ((h_ // 2) % 4)
                                o = ps[4][base_:base_ + 64, yoff_:yoff_ + 128]
                                eng.matmul(out=o, lhsT=S16__[base_:base_ + 64, kc_, :],
                                           rhs=qT[base_:base_ + 64, kc_, :], start=True, stop=False,
                                           tile_position=(base_, base_))
                                return eng.matmul(out=o, lhsT=v_[:, h_, :], rhs=P_[:, h_, :],
                                                  start=False, stop=True, tile_position=(0, base_))
                            ev_ymm_w = tens([ev_Pe, ev_tr["qsT"], ev_v, ev_Sready[b], ev_y], ymm)

                            def sdmm(h_=h, k_=kh_, v_=vv_, SM=SMALL):
                                base_ = 64 * (h_ % 2)
                                return eng.matmul(out=SM[base_:base_ + 64, 64 * (h_ // 2):64 * (h_ // 2) + 64],
                                                  lhsT=k_[:, h_, :], rhs=v_[:, h_, :], start=True, stop=True,
                                                  tile_position=(0, base_))
                            ev_Sd = tens([ev_kh, ev_v] + ([ev_small_free[par], ev_Sready[b]] if h == 0 else []), sdmm)
                        ev_yT = vec([ev_ymm_w, pc.get(("yT16", par)) if wv_i == 0 else None],
                                    lambda g_=wv_i, y_=yT16_: eng.tensor_copy(
                                        out=y_[:, 4 * g_:4 * g_ + 4, :].rearrange("p k t -> p (k t)"),
                                        in_=ps[4][:]))
                        ev_y = ev_yT
                    st.update(dict(ev_Pe_last=ev_Pe_last, ev_Pmm_last=ev_Pmm_last,
                                   ev_Sd=ev_Sd, ev_yT=ev_yT, ev_ymm_w=ev_ymm_w))
                    return st

                def chunk_h2b(ci, st):
                    nonlocal pbi, evd_scr, ev_Lw_prev, ev_y
                    c, b = ci // 2, ci % 2
                    cc = b * (SEQ // CH) + c
                    par = ci % 2
                    x_ = x_in[par]
                    yT16_ = yT16[par]
                    x_new_, ln2_, ln2T_, xs_ = x_new[par], ln2[par], ln2T[par], xs[par]
                    eqm_, cand_, masks8_, scol_ = eqm[par], cand[par], masks8[par], scol[par]
                    Sf_, S16_ = Sf[b], S16[b]
                    ACcol_ = ACcol[par]
                    SMALL = SMALLs[par]
                    ev_AC = st["ev_AC"]
                    exl = st["exl"]
                    ev_tr = st["ev_tr"]
                    ev_v = st["ev_v"]
                    ev_kh = st["ev_kh"]
                    ev_qf = st["ev_qf"]
                    ev_qs = st["ev_qs"]
                    ev_kf = st["ev_kf"]
                    ev_t = st["ev_t"]
                    ev_l = st["ev_l"]
                    ev_zl = st["ev_zl"]
                    ev_Lmm = st["ev_Lmm"]
                    ev_A = st["ev_A"]
                    ev_ACr = st["ev_ACr"]
                    ev_rstd = st["ev_rstd"]
                    ev_sq = st["ev_sq"]
                    ev_xcp = st["ev_xcp"]
                    ev_Pe_last = st["ev_Pe_last"]
                    ev_Pmm_last = st["ev_Pmm_last"]
                    ev_Sd = st["ev_Sd"]
                    ev_yT = st["ev_yT"]
                    ev_ymm_w = st["ev_ymm_w"]
                    ev_Su = None
                    for h in range(NH):
                        ev_Su = vec([ev_Sd, ev_AC] + ([ev_yT] if h == 0 else []),
                                    lambda h_=h, Sf__=Sf_, A__=ACcol_, SM=SMALL: eng.scalar_tensor_tensor(
                                        out=Sf__[64 * (h_ % 2):64 * (h_ % 2) + 64, h_ // 2, :],
                                        in0=Sf__[64 * (h_ % 2):64 * (h_ % 2) + 64, h_ // 2, :],
                                        scalar=A__[64 * (h_ % 2):64 * (h_ % 2) + 64, h_:h_ + 1],
                                        in1=SM[64 * (h_ % 2):64 * (h_ % 2) + 64,
                                               64 * (h_ // 2):64 * (h_ // 2) + 64],
                                        op0=ALU.mult, op1=ALU.add))
                    ev_small_free[par] = ev_Su
                    ev_S16 = vec([ev_Su], lambda Sf__=Sf_, S16__=S16_: eng.tensor_copy(
                        out=S16__[:].rearrange("p k j -> p (k j)"),
                        in_=Sf__[:].rearrange("p k j -> p (k j)")))
                    ev_Sready[b] = ev_S16
                    ev_res = None
                    for hf in range(2):
                        bank = ps[hf]
                        ev_mm = tens([ev_yT, ev_wT["wo"], ev_pA[hf]],
                                     lambda hf_=hf, bank_=bank, y_=yT16_: [
                                         eng.matmul(out=bank_[:], lhsT=y_[:, k, :],
                                                    rhs=woT[:, k, hf_ * 512:(hf_ + 1) * 512],
                                                    start=(k == 0), stop=(k == NK - 1)) for k in range(NK)][-1])
                        ev_omm_last = ev_mm
                        ev_res = vec([ev_mm] + ([ev_xcp] if hf == 0 else []),
                                     lambda hf_=hf, bank_=bank, xn=x_new_: eng.tensor_tensor(
                                         out=xn[:, hf_ * 512:(hf_ + 1) * 512], in0=bank_[:],
                                         in1=xn[:, hf_ * 512:(hf_ + 1) * 512], op=ALU.add))
                        ev_pA[hf] = ev_res
                    ev_xn_st = dma('DST', [ev_res], lambda b_=b, c_=c, xn=x_new_: eng.dma_start(
                        out=out_d[b_, c_ * CH:(c_ + 1) * CH, :], in_=xn[:]))
                    ev_ms2 = sca([ev_res, ev_dmp[0]], lambda xn=x_new_, s_=scol_: eng.activation(
                        out=dmp[:], in_=xn[:], func=AF.Identity,
                        accum_out=s_[:, SC_SUM:SC_SUM + 1]))
                    ev_sq2 = sca([ev_res, ev_ms2], lambda xn=x_new_, s_=scol_: eng.activation(
                        out=dmp[:], in_=xn[:], func=AF.Square,
                        accum_out=s_[:, SC_SQ:SC_SQ + 1]))
                    ev_dmp[0] = ev_sq2
                    ev_m_2 = vec([ev_ms2], lambda s_=scol_: eng.tensor_scalar(
                        out=s_[:, SC_M:SC_M + 1], in0=s_[:, SC_SUM:SC_SUM + 1],
                        scalar1=1.0 / D, scalar2=None, op0=ALU.mult))
                    ev_m22 = vec([ev_m_2], lambda s_=scol_: eng.tensor_tensor(
                        out=s_[:, SC_M2:SC_M2 + 1], in0=s_[:, SC_M:SC_M + 1],
                        in1=s_[:, SC_M:SC_M + 1], op=ALU.mult))
                    ev_var2 = vec([ev_m22, ev_sq2, ev_rstd],
                                  lambda s_=scol_: eng.scalar_tensor_tensor(
                                      out=s_[:, SC_VAR:SC_VAR + 1], in0=s_[:, SC_SQ:SC_SQ + 1],
                                      scalar=1.0 / D, in1=s_[:, SC_M2:SC_M2 + 1],
                                      op0=ALU.mult, op1=ALU.subtract))
                    ev_lnv2 = sca([ev_var2], lambda s_=scol_: eng.activation(
                        out=s_[:, SC_LNV:SC_LNV + 1], in_=s_[:, SC_VAR:SC_VAR + 1], func=AF.Ln, bias=EPS))
                    ev_rstd2 = sca([ev_lnv2], lambda s_=scol_: eng.activation(
                        out=s_[:, SC_RSTD:SC_RSTD + 1], in_=s_[:, SC_LNV:SC_LNV + 1], func=AF.Exp, scale=-0.5))
                    ev_nmr2 = vec([ev_m_2, ev_rstd2], lambda s_=scol_: eng.scalar_tensor_tensor(
                        out=s_[:, SC_NMR:SC_NMR + 1], in0=s_[:, SC_M:SC_M + 1], scalar=-1.0,
                        in1=s_[:, SC_RSTD:SC_RSTD + 1], op0=ALU.mult, op1=ALU.mult))
                    ev_ln2 = sca([ev_nmr2, pc.get(("ln2", par))],
                                 lambda xn=x_new_, l_=ln2_, s_=scol_: eng.activation(
                                     out=l_[:], in_=xn[:], func=AF.Identity,
                                     scale=s_[:, SC_RSTD:SC_RSTD + 1], bias=s_[:, SC_NMR:SC_NMR + 1]))
                    ev_t3 = tens([ev_ln2, ev_pbf[pbi]],
                                 lambda l_=ln2_, pbx=pbs[pbi]: [eng.transpose(
                                     out=pbx[:, j * 128:(j + 1) * 128], in_=l_[:, j * 128:(j + 1) * 128],
                                     identity=identb[:]) for j in range(8)][-1])
                    ev_ln2T = vec([ev_t3, pc.get(("ln2T", par)), pc.get(("ln2T_pe", par))],
                                  lambda d_=ln2T_, pbx=pbs[pbi]: eng.tensor_copy(
                                      out=d_[:].rearrange("p k t -> p (k t)"), in_=pbx[:]))
                    ev_pbf[pbi] = ev_ln2T
                    pbi ^= 1
                    ev_lg = tens([ev_ln2T, ev_gw, ev_small_free[par]],
                                 lambda l_=ln2T_: [eng.matmul(out=SMALL[:, 0:NE], lhsT=l_[:, k, :],
                                                              rhs=gwS[:, k, :],
                                                              start=(k == 0), stop=(k == NK - 1))
                                                   for k in range(NK)][-1])
                    ev_mx = vec([ev_lg], lambda s_=scol_: eng.reduce_max(
                        out=s_[:, SC_MAX:SC_MAX + 1], in_=SMALL[:, 0:NE], axis=AX.X))
                    ev_eq = vec([ev_mx, pc.get(("eqm", par))],
                                lambda e_=eqm_, s_=scol_: eng.tensor_scalar(
                                    out=e_[:], in0=SMALL[:, 0:NE], scalar1=s_[:, SC_MAX:SC_MAX + 1],
                                    scalar2=None, op0=ALU.is_ge))
                    ev_small_free[par] = ev_eq
                    ev_cd = vec([ev_eq, ev_big, pc.get(("cand", par))],
                                lambda e_=eqm_, c_=cand_: eng.tensor_tensor(out=c_[:], in0=e_[:],
                                                                            in1=big8[:], op=ALU.mult))
                    ev_cd2 = vec([ev_cd], lambda c_=cand_: eng.tensor_scalar(
                        out=c_[:], in0=c_[:], scalar1=256.0, scalar2=None, op0=ALU.add))
                    ev_sel = vec([ev_cd2], lambda c_=cand_, s_=scol_: eng.tensor_reduce(
                        out=s_[:, SC_SEL:SC_SEL + 1], in_=c_[:], axis=AX.X, op=ALU.min))
                    ev_msk = vec([ev_sel, pc.get(("masks8", par))],
                                 lambda m_=masks8_, s_=scol_: eng.tensor_tensor(
                                     out=m_[:], in0=s_[:, SC_SEL:SC_SEL + 1].broadcast_to([128, NE]),
                                     in1=iota8[:], op=ALU.is_equal))
                    ev_esc = sca([ev_mx], lambda s_=scol_: eng.activation(
                        out=s_[:, SC_ESC:SC_ESC + 1], in_=s_[:, SC_MAX:SC_MAX + 1], func=AF.Exp, scale=-1.0))
                    ev_den = vec([ev_esc], lambda s_=scol_: eng.tensor_scalar(
                        out=s_[:, SC_DEN:SC_DEN + 1], in0=s_[:, SC_ESC:SC_ESC + 1],
                        scalar1=1.0, scalar2=None, op0=ALU.add))
                    ev_scr = vec([ev_den], lambda s_=scol_: eng.reciprocal(
                        out=s_[:, SC_SCR:SC_SCR + 1], in_=s_[:, SC_DEN:SC_DEN + 1]))
                    ev_sr2 = vec([ev_scr, ev_rstd2], lambda s_=scol_: eng.tensor_tensor(
                        out=s_[:, SC_SR2:SC_SR2 + 1], in0=s_[:, SC_SCR:SC_SCR + 1],
                        in1=s_[:, SC_RSTD:SC_RSTD + 1], op=ALU.mult))
                    ev_xs = vec([ev_sr2, pc.get(("xs", par))],
                                lambda xn=x_new_, x2=xs_, s_=scol_: eng.tensor_scalar(
                                    out=x2[:], in0=xn[:], scalar1=s_[:, SC_M:SC_M + 1],
                                    scalar2=s_[:, SC_SR2:SC_SR2 + 1], op0=ALU.subtract, op1=ALU.mult))
                    ev_st1 = dma('DSS', [ev_ln2T], lambda cc_=cc, d_=ln2T_: eng.dma_start(out=lnT_scr[cc_], in_=d_[:]))
                    ev_st2 = dma('DSS', [ev_xs], lambda cc_=cc, d_=xs_: eng.dma_start(out=xs_scr[cc_], in_=d_[:]))
                    ev_st3 = dma('DSS', [ev_msk], lambda cc_=cc, m_=masks8_: eng.dma_start(out=msk_scr[cc_], in_=m_[:]))

                    for k_, v_ in {
                        "scol": ev_xs, "scolg": ev_sq2, "sc_sca": ev_rstd2, "r1": ev_qf, "ln1": ev_t,
                        "ln1T": ev_v, "zb": ev_l, "za": ev_zl, "zm": ev_l, "lcol": ev_Lmm,
                        "Lm": ev_A, "LCrow": ev_ACr, "Ecol": ev_kh, "Acol": ev_qs, "ACcol": ev_Su,
                        "Dbuf": ev_Pe_last,
                        "qf": ev_tr["qfT"], "qs": ev_tr["qsT"], "kf": ev_tr["kfT"],
                        "kh": ev_Sd, "vv": ev_Sd, "Pt": ev_yT, "yT16": ev_omm_last,
                        "x_new": ev_xs, "x_new_dma": ev_xn_st, "ln2": ev_t3, "ln2T": ev_st1,
                        "xs": ev_st2, "eqm": ev_cd, "cand": ev_msk,
                        "masks8": ev_st3, "qfT": ev_Pmm_last, "qsT": ev_ymm_w,
                        "kfT": ev_Pmm_last, "ln2T_pe": ev_lg, "Pbank": ev_Pe_last,
                    }.items():
                        pc[(k_, par)] = v_



                exl_pend[0] = dma('DXL', [], lambda: eng.dma_start(out=x_in[0][:], in_=x_d[0, 0:CH, :]))
                for ci in range(NCH):
                    stH[ci] = chunk_h1a(ci)
                    if ci >= 1:
                        chunk_h2a(ci - 1, stH[ci - 1])
                    chunk_h1b(ci, stH[ci])
                    if ci >= 1:
                        chunk_h2b(ci - 1, stH.pop(ci - 1))
                chunk_h2a(NCH - 1, stH[NCH - 1])
                chunk_h2b(NCH - 1, stH.pop(NCH - 1))

            evA_done = [(k, cnt[k]) for k in ("DST", "DSS", "DGL", "VE", "SE", "PE", "GP")]

            if PHASE_B and SPARSE:
                ev_mk = dma('DSB', evA_done, lambda: eng.dma_start(
                    out=mk_s[:], in_=msk_scr[:].rearrange("j p e -> p j e")))
                ev_rz = vec([], lambda: eng.memset(rows_s[0:1, :], 0.0))
                ev_rkmm = tens([ev_mk, ev_c[2]] + evA_done,
                               lambda: [eng.matmul(out=ps[2][:, j * 8:(j + 1) * 8], lhsT=utri[:],
                                                   rhs=mk_s[:, j, :], start=True, stop=True)
                                        for j in range(32)][-1])
                ev_rk = vec([ev_rkmm], lambda: eng.tensor_copy(
                    out=rk_s[:].rearrange("p j e -> p (j e)"), in_=ps[2][:, 0:256]))
                ev_totmm = tens([ev_rk], lambda: eng.matmul(
                    out=ps[3][0:1, 0:256], lhsT=oh127[:], rhs=rk_s[:].rearrange("p j e -> p (j e)"),
                    start=True, stop=True))
                ev_tot = vec([ev_totmm], lambda: eng.tensor_copy(out=srow[:], in_=ps[3][0:1, 0:256]))
                ev_totw = dma('DGL', [ev_tot], lambda: eng.dma_start(out=srtA, in_=srow[:]))
                ev_totr = dma('DGL', [ev_totw], lambda: eng.dma_start(
                    out=car_s[:], in_=srtA.rearrange("a (j e) -> (a j) e", e=8)))
                ev_carmm = tens([ev_totr], lambda: eng.matmul(
                    out=ps[3][0:32, 256:264], lhsT=utriS[0:32, 0:32], rhs=car_s[:],
                    start=True, stop=True))
                ev_cntmm = tens([ev_totr], lambda: eng.matmul(
                    out=ps[3][0:1, 264:272], lhsT=utri[0:32, 127:128], rhs=car_s[:],
                    start=True, stop=True))
                ev_car = vec([ev_carmm, ev_cntmm], lambda: eng.tensor_copy(out=car_s[:], in_=ps[3][0:32, 256:264]))
                ev_cnt = vec([ev_cntmm, ev_car, ev_rz], lambda: eng.tensor_copy(out=rows_s[0:1, 72:80],
                                                                         in_=ps[3][0:1, 264:272]))
                ev_q1 = vec([ev_cnt], lambda: eng.memset(rows_s[0:1, 64:72], 0.0))
                ev_ctf = ev_q1
                for m8 in range(8):
                    ev_ctf = vec([ev_ctf], lambda m_=m8: eng.scalar_tensor_tensor(
                        out=rows_s[0:1, 64:72], in0=rows_s[0:1, 72:80], scalar=512.0 * m_,
                        in1=rows_s[0:1, 64:72], op0=ALU.is_gt, op1=ALU.add))
                ev_cc1 = vec([ev_ctf], lambda: eng.tensor_copy(out=rows_s[0:1, 8:16], in_=rows_s[0:1, 64:72]))
                ev_cs = ev_cc1
                for dlev in (1, 2, 4):
                    ev_cs = vec([ev_cs], lambda d_=dlev: eng.tensor_tensor(
                        out=rows_s[0:1, 8 + d_:16], in0=rows_s[0:1, 8 + d_:16],
                        in1=rows_s[0:1, 8:16 - d_], op=ALU.add))
                ev_px1 = vec([ev_cs], lambda: eng.tensor_tensor(out=rows_s[0:1, 0:8],
                                                                in0=rows_s[0:1, 8:16],
                                                                in1=rows_s[0:1, 64:72], op=ALU.subtract))
                ev_px = vec([ev_px1], lambda: eng.tensor_scalar(out=rows_s[0:1, 0:8], in0=rows_s[0:1, 0:8],
                                                                scalar1=512.0, scalar2=-1.0,
                                                                op0=ALU.mult, op1=ALU.add))
                ev_t2z = vec([ev_px], lambda: eng.memset(rows_s[0:1, 16:32], 0.0))
                ev_t2 = ev_t2z
                for e8 in range(8):
                    ev_t2 = vec([ev_t2], lambda e_=e8: eng.scalar_tensor_tensor(
                        out=rows_s[0:1, 16:32], in0=iotar[0:1, 0:16],
                        scalar=rows_s[0:1, 8 + e_:9 + e_], in1=rows_s[0:1, 16:32],
                        op0=ALU.is_ge, op1=ALU.add))
                ev_t2c = vec([ev_t2], lambda: eng.tensor_scalar(out=rows_s[0:1, 16:32], in0=rows_s[0:1, 16:32],
                                                                scalar1=7.0, scalar2=None, op0=ALU.min))
                ev_woq = vec([ev_t2c], lambda: eng.tensor_scalar(out=rows_s[0:1, 32:48], in0=rows_s[0:1, 16:32],
                                                                 scalar1=1024.0, scalar2=None, op0=ALU.mult))
                ev_wod = vec([ev_woq], lambda: eng.tensor_scalar(out=rows_s[0:1, 48:64], in0=rows_s[0:1, 16:32],
                                                                 scalar1=2048.0, scalar2=None, op0=ALU.mult))
                ev_rw1 = dma('DGL', [ev_car], lambda: eng.dma_start(out=srtB, in_=car_s[:]))
                ev_rw2 = dma('DGL', [ev_wod, ev_px], lambda: eng.dma_start(out=srtC, in_=rows_s[:]))
                ev_rb1 = dma('DGL', [ev_rw1], lambda: eng.dma_start(
                    out=carb_s[:].rearrange("p j e -> p (j e)"),
                    in_=srtB.rearrange("j e -> (j e)").unsqueeze(0).broadcast_to([128, 256])))
                ev_rb2 = dma('DGL', [ev_rw2], lambda: eng.dma_start(
                    out=rowb_s[:],
                    in_=srtC.rearrange("r c -> (r c)").unsqueeze(0).broadcast_to([128, 128])))
                ev_wq1 = vec([ev_rb2], lambda: eng.tensor_tensor(
                    out=pos_s[:, 0:16], in0=rowb_s[:, 32:48],
                    in1=iotac[:, 0:1].broadcast_to([128, 16]), op=ALU.add))
                ev_wq2 = vec([ev_wq1], lambda: eng.tensor_copy(out=widxQ[:], in_=pos_s[:, 0:16]))
                ev_wd1 = vec([ev_wq2], lambda: eng.tensor_tensor(
                    out=pos_s[:, 0:16], in0=rowb_s[:, 48:64],
                    in1=iotac[:, 0:1].broadcast_to([128, 16]), op=ALU.add))
                ev_wd2 = vec([ev_wd1], lambda: eng.tensor_copy(out=widxD[:], in_=pos_s[:, 0:16]))
                ev_p1 = vec([ev_rb1, ev_rk, ev_wd2], lambda: eng.tensor_tensor(
                    out=rk_s[:].rearrange("p j e -> p (j e)"),
                    in0=rk_s[:].rearrange("p j e -> p (j e)"),
                    in1=carb_s[:].rearrange("p j e -> p (j e)"), op=ALU.add))
                ev_p2 = vec([ev_p1, ev_rb2], lambda: eng.tensor_tensor(
                    out=rk_s[:], in0=rk_s[:],
                    in1=rowb_s[:, 0:8].unsqueeze(1).broadcast_to([128, 32, 8]), op=ALU.add))
                ev_pos = ev_p2
                for j32 in range(32):
                    ev_pos = vec([ev_pos], lambda j_=j32: eng.scalar_tensor_tensor(
                        out=carb_s[:, j_, :], in0=rk_s[:, j_, :], scalar=1.0,
                        in1=mk_s[:, j_, :], op0=ALU.mult, op1=ALU.mult,
                        accum_out=pos_s[:, j_:j_ + 1]))
                ev_posi = vec([ev_pos], lambda: eng.tensor_copy(out=posi_s[:], in_=pos_s[:]))

                do_scatter = SPARSE_STAGE >= 2
                do_routed = SPARSE_STAGE >= 3
                do_return = SPARSE_STAGE >= 4
                ev_zz = vec([], lambda: eng.memset(zt_s[:], 0.0))
                scat_state = {"ev": None}

                def scatter_gen():
                    ev_z = None
                    for a0 in range(TN * 512 // 128):
                        ev_z = dma('DGS', [ev_zz] if a0 == 0 else [],
                                   lambda a_=a0: eng.dma_start(
                                       out=sorted_scr[a_ * 128:(a_ + 1) * 128, :], in_=zt_s[:]))
                        if a0 % 4 == 3:
                            yield
                    ev_sc = ev_z
                    for cc2 in range(NCH if do_scatter else 0):
                        ev_ldx = dma('DXL', [ev_sc, ev_zz],
                                     lambda c_=cc2: eng.dma_start(out=zt_s[:], in_=xs_scr[c_]))
                        ev_sc = dma('DGS', [ev_ldx, ev_posi, ev_z],
                                    lambda c_=cc2: eng.indirect_dma_start(
                                        out=sorted_scr,
                                        out_offset=bass.IndirectOffsetOnAxis(posi_s[:, c_:c_ + 1], 0),
                                        in_=zt_s[:], in_offset=None))
                        scat_state["ev"] = ev_sc
                        yield
                    while True:
                        yield

                scat_pump = scatter_gen()
                scat_units = [0]

                def pump_scatter(n):
                    for _ in range(n):
                        next(scat_pump)
                        scat_units[0] += 1

            if PHASE_B:
                NTT = SWTOK // 128
                NT5 = SWTOK // 512
                ev_wgu_free = [None, None]
                ev_wgu_ld = [None, None]
                ev_wd_free = [None, None]
                ev_wd_ld = [None, None]
                ev_acc_st = None
                ev_in_free = None
                gidx = 0
                didx = 0
                widx = 0
                ev_gu_grp = {}
                ev_act_grp = {}
                ev_sil_grp = {}
                ev_down_grp = {}
                ev_stt_grp = {}
                ev_actT_free = None

                passes = ["sh"] if SPARSE else (["sh"] + list(range(NE)))
                for s in range(NSW):
                    b, half = s // 2, s % 2
                    cc0 = b * (SEQ // CH) + half * NTT
                    t0 = half * SWTOK
                    if not SPARSE:
                        ev_xsl = dma('DSB', evA_done + [ev_in_free],
                                     lambda cc0_=cc0: eng.dma_start(
                                         out=xsT_s[:].rearrange("p k (j t) -> p k j t", j=NTT),
                                         in_=xsT_scr[cc0_:cc0_ + NTT].rearrange("j p k t -> p k j t")))
                        ev_mkl = dma('DSB', [ev_acc_st],
                                     lambda cc0_=cc0: eng.dma_start(
                                         out=masks_s[:], in_=msk_scr[cc0_:cc0_ + NTT].rearrange("j p e -> p j e")))
                    else:
                        ev_xsl = None
                        ev_mkl = None
                    ev_lnl = dma('DSB', evA_done + [ev_in_free],
                                   lambda cc0_=cc0: eng.dma_start(
                                       out=lnT_s[:].rearrange("p k (j t) -> p k j t", j=NTT),
                                       in_=lnT_scr[cc0_:cc0_ + NTT].rearrange("j p k t -> p k j t")))
                    ev_accl = dma('DSB', [ev_acc_st],
                                    lambda b_=b, t0_=t0: eng.dma_start(
                                        out=acc[:].rearrange("p j d -> p j d"),
                                        in_=out_d[b_, t0_:t0_ + SWTOK, :].rearrange("(j p) d -> p j d", p=128)))
                    ev_stt_last = None
                    for ex in passes:
                        inT = lnT_s if ex == "sh" else xsT_s
                        ev_inT = ev_lnl if ex == "sh" else ev_xsl
                        g_src = shg_d if ex == "sh" else rtg_d[ex]
                        u_src = shu_d if ex == "sh" else rtu_d[ex]
                        d_src = shd_d if ex == "sh" else rtd_d[ex]
                        for j in range(NK):
                            jp = j % 2
                            ev_wgu_ld[jp] = dma(
                                'DGW', [ev_wgu_free[jp]] + (evA_done if gidx == 0 else []),
                                [lambda j_=j, jp_=jp, g_s=g_src: eng.dma_start(
                                    out=wgu[jp_][0][:],
                                    in_=g_s.rearrange("(k p) i -> p k i", p=128)[:, :, 256 * j_:256 * j_ + 256]),
                                 lambda j_=j, jp_=jp, u_s=u_src: eng.dma_start(
                                    out=wgu[jp_][1][:],
                                    in_=u_s.rearrange("(k p) i -> p k i", p=128)[:, :, 256 * j_:256 * j_ + 256])])
                            for m_loc in range(2):
                                m = 2 * j + m_loc
                                for tt in range(NT5):
                                    gp = gidx % 2
                                    pg_, pu_ = ps[0 + 2 * gp], ps[1 + 2 * gp]
                                    extra = (evA_done + ([ev_rk, ev_car, ev_cnt] if SPARSE else [])) if gidx < 2 else []
                                    ev_mm = tens([ev_wgu_ld[jp], ev_inT,
                                                  ev_act_grp.get(gidx - 2), ev_sil_grp.get(gidx - 2)] + extra,
                                                 lambda m_=m_loc, tt_=tt, jp_=jp, pg__=pg_, pu__=pu_, inT_=inT: (
                                                     [eng.matmul(out=pg__[:],
                                                                 lhsT=wgu[jp_][0][:, k, m_ * 128:m_ * 128 + 128],
                                                                 rhs=inT_[:, k, tt_ * 512:tt_ * 512 + 512],
                                                                 start=(k == 0), stop=(k == NK - 1)) for k in range(NK)],
                                                     [eng.matmul(out=pu__[:],
                                                                 lhsT=wgu[jp_][1][:, k, m_ * 128:m_ * 128 + 128],
                                                                 rhs=inT_[:, k, tt_ * 512:tt_ * 512 + 512],
                                                                 start=(k == 0), stop=(k == NK - 1)) for k in range(NK)])[-1][-1])
                                    ev_gu_grp[gidx] = ev_mm
                                    ev_sil = sca([ev_mm, ev_act_grp.get(gidx - 2)] + (evA_done if gidx < 2 else []),
                                                 lambda gp_=gp, pg__=pg_: eng.activation(out=sgb[gp_][:], in_=pg__[:],
                                                                                         func=AF.Silu))
                                    ev_sil_grp[gidx] = ev_sil
                                    ev_act = vec([ev_sil, ev_actT_free] + (evA_done if gidx < 2 else []),
                                                 lambda m__=m, tt_=tt, gp_=gp, pu__=pu_: eng.tensor_tensor(
                                                     out=actT[:, m__, tt_ * 512:tt_ * 512 + 512],
                                                     in0=sgb[gp_][:], in1=pu__[:], op=ALU.mult))
                                    ev_act_grp[gidx] = ev_act
                                    gidx += 1
                            ev_wgu_free[jp] = ev_gu_grp[gidx - 1]
                            if SPARSE and (s >= 1 or scat_units[0] < 15):
                                pump_scatter(2)
                        ev_all_act = ev_act_grp[gidx - 1]
                        for hh in range(2):
                            wp = widx % 2
                            ev_wd_ld[wp] = dma('DGD', [ev_wd_free[wp]] + (evA_done if widx < 2 else []),
                                                 lambda hh_=hh, wp_=wp, d_s=d_src: eng.dma_start(
                                                     out=wdb[wp_][:],
                                                     in_=d_s.rearrange("(m p) h -> p m h", p=128)[:, :, hh_ * 512:hh_ * 512 + 512]))
                            for ct in range(NTT):
                                dp = didx % 2
                                pd_ = (ps[4], ps[6])[dp]
                                extra = evA_done if didx < 2 else []
                                ev_dmm = tens([ev_wd_ld[wp], ev_all_act, ev_stt_grp.get(didx - 2)] + extra,
                                              lambda ct_=ct, wp_=wp, pd__=pd_: [
                                                  eng.matmul(out=pd__[:], lhsT=actT[:, m, ct_ * 128:ct_ * 128 + 128],
                                                             rhs=wdb[wp_][:, m, :], start=(m == 0), stop=(m == NM - 1))
                                                  for m in range(NM)][-1])
                                ev_down_grp[didx] = ev_dmm
                                mc = onescol[:, 0:1] if ex == "sh" else masks_s[:, ct, ex:ex + 1]
                                ev_stt = vec([ev_dmm, ev_accl, ev_mkl],
                                             lambda ct_=ct, hh_=hh, pd__=pd_, mc_=mc: eng.scalar_tensor_tensor(
                                                 out=acc[:, ct_, hh_ * 512:hh_ * 512 + 512],
                                                 in0=pd__[:], scalar=mc_,
                                                 in1=acc[:, ct_, hh_ * 512:hh_ * 512 + 512],
                                                 op0=ALU.mult, op1=ALU.add))
                                ev_stt_grp[didx] = ev_stt
                                ev_stt_last = ev_stt
                                didx += 1
                            ev_wd_free[wp] = ev_down_grp[didx - 1]
                            widx += 1
                        ev_actT_free = ev_down_grp[didx - 1]
                    ev_acc_st = dma('DST', [ev_stt_last],
                                      lambda b_=b, t0_=t0: eng.dma_start(
                                          out=out_d[b_, t0_:t0_ + SWTOK, :].rearrange("(j p) d -> p j d", p=128),
                                          in_=acc[:]))
                    ev_in_free = ev_gu_grp[gidx - 1]

                pump_scatter(15 + NCH + 4 - scat_units[0])
                ev_scatter_done = scat_state["ev"]
                ev_sh_pe = ev_gu_grp[gidx - 1]
                ev_sh_sil = ev_sil_grp[gidx - 1]
                ev_sh_act = ev_act_grp[gidx - 1]
                ev_sh_stt = ev_stt_grp[didx - 1]
                ev_sh_st = ev_acc_st
                rtg_flat = rtg_d.rearrange("e h i -> (e h) i")
                rtu_flat = rtu_d.rearrange("e h i -> (e h) i")
                rtd_flat = rtd_d.rearrange("e i h -> (e i) h")
                ev_sxld = [None, None]
                ev_sxT = [None, None]
                ev_gu_ld = [[None, None], [None, None]]
                ev_wd_ldr = None
                ev_guf = [[ev_sh_pe, ev_sh_st], [ev_sh_pe, ev_sh_st]]
                ev_wdf = [ev_sh_pe, ev_sh_st]
                ev_act_r = {}
                ev_sil_r = {}
                ev_gu_r = {}
                ev_dn_r = {}
                ev_ev_r = {}
                ev_st_r = {}
                gi = 0
                di = 0
                ev_pbfr = [ev_pbf[0], ev_pbf[1]]
                ev_sxfree = [ev_sh_pe, ev_sh_pe]
                for j in range(TN if do_routed else 0):
                    jb = j % 2
                    ev_sxld[jb] = dma('DSB', [ev_scatter_done, ev_sh_st] + ([ev_sxfree[jb]] if not isinstance(ev_sxfree[jb], list) else ev_sxfree[jb]),
                                      lambda j_=j, jb_=jb: eng.dma_start(
                                          out=sx_tm[jb_][:],
                                          in_=sorted_scr[512 * j_:512 * (j_ + 1), :].rearrange(
                                              "(a p) d -> p a d", p=128)))
                    ev_tx = None
                    for a4 in range(4):
                        ev_tt = tens([ev_sxld[jb], ev_pbfr[a4 % 2]],
                                     lambda jb_=jb, a_=a4, pbx=pbs[a4 % 2]: [
                                         eng.transpose(out=pbx[:, jj * 128:(jj + 1) * 128],
                                                       in_=sx_tm[jb_][:, a_, jj * 128:(jj + 1) * 128],
                                                       identity=identb[:]) for jj in range(8)][-1])
                        ev_tx = vec([ev_tt] + ([ev_sxT[jb]] if a4 == 0 else []),
                                    lambda jb_=jb, a_=a4, pbx=pbs[a4 % 2]: eng.tensor_copy(
                                        out=sxT[jb_][:, :, a_ * 128:(a_ + 1) * 128],
                                        in_=pbx[:].rearrange("p (k t) -> p k t", k=8)))
                        ev_pbfr[a4 % 2] = ev_tx
                    ev_sxT[jb] = ev_tx
                    for ih in range(2):
                        wb_ = (2 * j + ih) % 2
                        ev_gu_ld[wb_][0] = dma('DGW', (ev_guf[wb_] if isinstance(ev_guf[wb_], list) else [ev_guf[wb_]]) + [ev_posi],
                                               [(lambda j_=j, ih_=ih, wb2=wb_, k_=k2: eng.indirect_dma_start(
                                                   out=wgr[wb2][:, k_, :], out_offset=None,
                                                   in_=rtg_flat,
                                                   in_offset=bass.IndirectOffsetOnAxis(widxQ[:, j_:j_ + 1], 0),
                                                   element_offset=k_ * 128 * 2048 + ih_ * 1024))
                                                for k2 in range(NK)])
                        ev_gu_ld[wb_][1] = dma('DGW', [],
                                               [(lambda j_=j, ih_=ih, wb2=wb_, k_=k2: eng.indirect_dma_start(
                                                   out=wur[wb2][:, k_, :], out_offset=None,
                                                   in_=rtu_flat,
                                                   in_offset=bass.IndirectOffsetOnAxis(widxQ[:, j_:j_ + 1], 0),
                                                   element_offset=k_ * 128 * 2048 + ih_ * 1024))
                                                for k2 in range(NK)])
                        for m_loc in range(8):
                            m = ih * 8 + m_loc
                            gp2 = gi % 2
                            pg_, pu_ = ps[0 + 2 * gp2], ps[1 + 2 * gp2]
                            ev_mm = tens([ev_gu_ld[wb_][1], ev_sxT[jb],
                                          ev_act_r.get(gi - 2), ev_sil_r.get(gi - 2),
                                          ev_sh_sil if gi < 2 else None, ev_sh_act if gi < 2 else None],
                                         lambda m_=m_loc, wb2=wb_, jb_=jb, pg__=pg_, pu__=pu_: (
                                             [eng.matmul(out=pg__[:],
                                                         lhsT=wgr[wb2][:, k, m_ * 128:m_ * 128 + 128],
                                                         rhs=sxT[jb_][:, k, :],
                                                         start=(k == 0), stop=(k == NK - 1)) for k in range(NK)],
                                             [eng.matmul(out=pu__[:],
                                                         lhsT=wur[wb2][:, k, m_ * 128:m_ * 128 + 128],
                                                         rhs=sxT[jb_][:, k, :],
                                                         start=(k == 0), stop=(k == NK - 1)) for k in range(NK)])[-1][-1])
                            ev_gu_r[gi] = ev_mm
                            ev_sil2 = sca([ev_mm, ev_act_r.get(gi - 2)],
                                          lambda gp_=gp2, pg__=pg_: eng.activation(
                                              out=sgb_r[gp_][:], in_=pg__[:], func=AF.Silu))
                            ev_sil_r[gi] = ev_sil2
                            ev_act2 = vec([ev_sil2, ev_dn_r.get(di - 1) if m == 0 and j > 0 else None,
                                           ev_sh_stt if gi < 2 else None],
                                          lambda m__=m, gp_=gp2, pu__=pu_: eng.tensor_tensor(
                                              out=actT_r[:, m__, :], in0=sgb_r[gp_][:], in1=pu__[:],
                                              op=ALU.mult))
                            ev_act_r[gi] = ev_act2
                            gi += 1
                        ev_guf[wb_] = ev_gu_r[gi - 1]
                    ev_all_act2 = ev_act_r[gi - 1]
                    ev_wd_ldr = dma('DGD', (ev_wdf if isinstance(ev_wdf, list) else [ev_wdf]) + [ev_posi],
                                    [(lambda j_=j, m_=m2: eng.indirect_dma_start(
                                        out=wdr[:, m_, :], out_offset=None,
                                        in_=rtd_flat,
                                        in_offset=bass.IndirectOffsetOnAxis(widxD[:, j_:j_ + 1], 0),
                                        element_offset=m_ * 128 * 1024))
                                     for m2 in range(NM)])
                    for ct in range(4):
                        for hh in range(2):
                            dp2 = di % 2
                            pd_ = (ps[4], ps[6])[dp2]
                            ev_dmm = tens([ev_wd_ldr, ev_all_act2, ev_ev_r.get(di - 2),
                                           ev_sh_stt if di < 2 else None],
                                          lambda ct_=ct, hh_=hh, pd__=pd_: [
                                              eng.matmul(out=pd__[:], lhsT=actT_r[:, m, ct_ * 128:ct_ * 128 + 128],
                                                         rhs=wdr[:, m, hh_ * 512:hh_ * 512 + 512],
                                                         start=(m == 0), stop=(m == NM - 1))
                                              for m in range(NM)][-1])
                            ev_dn_r[di] = ev_dmm
                            ev_ev = vec([ev_dmm, ev_st_r.get(di - 8), ev_sh_stt if di < 8 else None,
                                         ev_sh_st if di < 8 else None],
                                        lambda ct_=ct, hh_=hh, pd__=pd_: eng.tensor_copy(
                                            out=dstg[:, ct_, hh_ * 512:hh_ * 512 + 512], in_=pd__[:]))
                            ev_ev_r[di] = ev_ev
                            ev_st_r[di] = ev_ev
                            di += 1
                    ev_wdf = ev_dn_r[di - 1]
                    ev_sxfree[jb] = ev_gu_r[gi - 1]
                    ev_sost = dma('DST', [ev_ev_r[di - 1]],
                                  lambda j_=j: eng.dma_start(
                                      out=sout_scr[512 * j_:512 * (j_ + 1), :].rearrange("(a p) d -> p a d", p=128),
                                      in_=dstg[:]))
                    for dd in range(di - 8, di):
                        ev_st_r[dd] = ev_sost
                if DEBUG_SPARSE:
                    dma('DST', [ev_posi], lambda: eng.dma_start(out=dbg1, in_=pos_s[:]))
                    dma('DST', [ev_posi], lambda: eng.dma_start(out=dbg2, in_=rows_s[0:1, :]))
                if DEBUG_SPARSE and do_scatter:
                    ev_d3l = dma('DSB', [ev_scatter_done, ev_sh_st],
                                 lambda: eng.dma_start(out=sx_tm[0][:],
                                                       in_=sorted_scr[0:512, :].rearrange("(a p) d -> p a d", p=128)))
                    ev_d3c = vec([ev_d3l], lambda: eng.tensor_copy(out=dstg[:], in_=sx_tm[0][:]))
                    dma('DST', [ev_d3c], lambda: eng.dma_start(
                        out=dbg3.rearrange("(a p) d -> p a d", p=128), in_=dstg[:]))
                evB_done = ("DST", cnt["DST"])
                ev_ga = None
                ev_add = None
                def _ret_tile(rp_):
                    if rp_ == 0:
                        return dstg[:]
                    return actT_r[:].rearrange("p m t -> p (m t)").bitcast(F32).rearrange(
                        "p (a d) -> p a d", a=4)

                ev_adds = [None, None]
                for q4 in range(NCH // 4 if do_return else 0):
                    rp = q4 % 2
                    ev_ga = None
                    for qq in range(4):
                        cc3 = q4 * 4 + qq
                        ev_ga = dma('DGS', [evB_done, ev_posi] + ([ev_adds[rp]] if qq == 0 else []),
                                    lambda c_=cc3, qq_=qq, rp_=rp: eng.indirect_dma_start(
                                        out=_ret_tile(rp_)[:, qq_, :],
                                        out_offset=None, in_=sout_scr,
                                        in_offset=bass.IndirectOffsetOnAxis(posi_s[:, c_:c_ + 1], 0)))
                    b3, c3 = (q4 * 4) // (SEQ // CH), (q4 * 4) % (SEQ // CH)
                    ev_adds[rp] = dma('DGA', [ev_ga],
                                      lambda b_=b3, c_=c3, rp_=rp: eng.dma_start(
                                          out=out_d[b_, c_ * CH:(c_ + 4) * CH, :].rearrange(
                                              "(a p) d -> p a d", p=128),
                                          in_=_ret_tile(rp_),
                                          accum_op=ALU.add))

            if who == "sync":
                eng.wait_ge(sems["DST"], cnt["DST"])
                eng.wait_ge(sems["DGA"], cnt["DGA"])

        @block.sync
        def _(eng):
            emit("sync", eng)

        @block.gpsimd
        def _(eng):
            emit("gpsimd", eng)

        @block.vector
        def _(eng):
            emit("vector", eng)

        @block.scalar
        def _(eng):
            emit("scalar", eng)

        @block.tensor
        def _(eng):
            emit("tensor", eng)

    return nc


def core_inputs(inputs, core):
    c = make_consts()
    return dict(
        x=np.ascontiguousarray(inputs["L0_attn_x"][2 * core:2 * core + 2]),
        wq=inputs["L0_attn_Wq"], wk=inputs["L0_attn_Wk"], wv=inputs["L0_attn_Wv"],
        wo=inputs["L0_attn_Wo"], wgate=inputs["L0_attn_Wg"],
        bgate=np.ascontiguousarray(inputs["L0_attn_bg"].reshape(1, NH)),
        gate_w=inputs["L0_ffn_gate_w"], shg=inputs["L0_ffn_sh_gate"],
        shu=inputs["L0_ffn_sh_up"], shd=inputs["L0_ffn_sh_down"],
        rtg=inputs["L0_ffn_rt_gate"], rtu=inputs["L0_ffn_rt_up"], rtd=inputs["L0_ffn_rt_down"],
        **c,
    )


def kernel(**inputs):
    inputs = {k: np.asarray(v, dtype=np.float32) for k, v in inputs.items()}
    nc = build_nc()
    in_maps = [core_inputs(inputs, core) for core in range(8)]
    res = run_bass_kernel_spmd(nc, in_maps, core_ids=list(range(8)))
    return np.concatenate([r["out"] for r in res.results], axis=0)


# version 17
# speedup vs baseline: 1.0748x; 1.0748x over previous
import numpy as np
import ml_dtypes
import concourse.bass as bass
import concourse.mybir as mybir
from concourse.bass_utils import run_bass_kernel_spmd
from contextlib import ExitStack

F32 = mybir.dt.float32
BF16 = mybir.dt.bfloat16
U8 = mybir.dt.uint8
I32 = mybir.dt.int32
AF = mybir.ActivationFunctionType
ALU = mybir.AluOpType
AX = mybir.AxisListType

NB = 2
SEQ = 2048
D = 1024
NH = 16
DH = 64
NK = 8
CH = 128
NCH = NB * (SEQ // CH)
QSCALE = 0.125
EPS = 1e-5
NEGBIG = 1.0e9
I_FF = 2048
NM = 16
NE = 8
SWTOK = 1024
NSW = NB * SEQ // SWTOK

PHASE_A = True
PHASE_B = True
SPARSE = True
DEBUG_SPARSE = False
SPARSE_STAGE = 4
TN = 15


def make_consts():
    ident_bf = np.eye(128, dtype=ml_dtypes.bfloat16)
    ident_f32 = np.eye(128, dtype=np.float32)
    utri = np.triu(np.ones((128, 128), dtype=np.float32))
    onesk1 = np.ones((1, 128), dtype=np.float32)
    oh127 = np.zeros((128, 1), dtype=np.float32)
    oh127[127, 0] = 1.0
    iota8 = np.broadcast_to(np.arange(8, dtype=np.float32), (128, 8)).copy()
    big8 = iota8 - 256.0
    negtri = np.where(np.triu(np.ones((128, 128), dtype=bool)), 0.0, NEGBIG).astype(np.float32)
    utri_strict = np.triu(np.ones((128, 128), dtype=np.float32), 1)
    iota_row = np.arange(128, dtype=np.float32).reshape(1, 128)
    iota_col = np.arange(128, dtype=np.float32).reshape(128, 1)
    onescol = np.ones((128, 1), dtype=np.float32)
    actc = np.zeros((128, 8), dtype=np.float32)
    actc[:] = np.array([1e-5, 1.0, -1.0, -0.5, 0.125, -0.125, 1.0 / 1024.0, 0.0], dtype=np.float32)
    return dict(c_actconst=actc, c_utri_strict=utri_strict, c_iota_row=iota_row,
                c_iota_col=iota_col,c_ident_bf=ident_bf, c_ident_f32=ident_f32, c_utri=utri,
                c_onesk1=onesk1, c_oh127=oh127, c_iota8=iota8, c_big8=big8,
                c_negtri=negtri, c_onescol=onescol)


class Arena:
    def __init__(self, nc, name, nbytes):
        self.t = nc.alloc_sbuf_tensor(name, [128, nbytes], U8)
        self.off = 0
        self.marks = []

    def take(self, shape, dtype, name=""):
        esz = {F32: 4, BF16: 2, U8: 1}[dtype]
        n = int(np.prod(shape[1:])) * esz
        off = (self.off + 127) // 128 * 128
        self.off = off + n
        assert self.off <= self.t.shape[1], f"arena overflow at {name}: {self.off}"
        ap = self.t.ap()[:, off:off + n].bitcast(dtype)
        if len(shape) > 2:
            dims = " ".join(f"d{i}" for i in range(1, len(shape)))
            ap = ap.rearrange(f"p ({dims}) -> p {dims}",
                              **{f"d{i}": shape[i] for i in range(1, len(shape) - 1)})
        if shape[0] != 128:
            ap = ap[:shape[0]]
        return ap


def build_nc():
    nc = bass.Bass("TRN2", target_bir_lowering=False, debug=False)

    x_d = nc.dram_tensor("x", [NB, SEQ, D], F32, kind="ExternalInput").ap()
    wq_d = nc.dram_tensor("wq", [D, D], F32, kind="ExternalInput").ap()
    wk_d = nc.dram_tensor("wk", [D, D], F32, kind="ExternalInput").ap()
    wv_d = nc.dram_tensor("wv", [D, D], F32, kind="ExternalInput").ap()
    wo_d = nc.dram_tensor("wo", [D, D], F32, kind="ExternalInput").ap()
    wg_d = nc.dram_tensor("wgate", [NH, D], F32, kind="ExternalInput").ap()
    bg_d = nc.dram_tensor("bgate", [1, NH], F32, kind="ExternalInput").ap()
    gw_d = nc.dram_tensor("gate_w", [D, NE], F32, kind="ExternalInput").ap()
    shg_d = nc.dram_tensor("shg", [D, I_FF], F32, kind="ExternalInput").ap()
    shu_d = nc.dram_tensor("shu", [D, I_FF], F32, kind="ExternalInput").ap()
    shd_d = nc.dram_tensor("shd", [I_FF, D], F32, kind="ExternalInput").ap()
    rtg_d = nc.dram_tensor("rtg", [NE, D, I_FF], F32, kind="ExternalInput").ap()
    rtu_d = nc.dram_tensor("rtu", [NE, D, I_FF], F32, kind="ExternalInput").ap()
    rtd_d = nc.dram_tensor("rtd", [NE, I_FF, D], F32, kind="ExternalInput").ap()
    cvals = make_consts()
    cst = {k: nc.dram_tensor(k, list(v.shape), F32 if v.dtype == np.float32 else BF16,
                             kind="ExternalInput").ap() for k, v in cvals.items()}
    out_d = nc.dram_tensor("out", [NB, SEQ, D], F32, kind="ExternalOutput").ap()
    if DEBUG_SPARSE:
        dbg1 = nc.dram_tensor("dbg1", [128, 32], F32, kind="ExternalOutput").ap()
        dbg2 = nc.dram_tensor("dbg2", [1, 128], F32, kind="ExternalOutput").ap()
        dbg3 = nc.dram_tensor("dbg3", [512, D], F32, kind="ExternalOutput").ap()

    xsT_scr = nc.dram_tensor("xsT_scr", [NCH, 128, NK, CH], BF16, kind="Internal").ap()
    lnT_scr = nc.dram_tensor("lnT_scr", [NCH, 128, NK, CH], BF16, kind="Internal").ap()
    msk_scr = nc.dram_tensor("msk_scr", [NCH, 128, NE], F32, kind="Internal").ap()
    L_scr = nc.dram_tensor("L_scr", [NCH, NH, CH], F32, kind="Internal").ap()
    xs_scr = nc.dram_tensor("xs_scr", [NCH, 128, D], BF16, kind="Internal").ap()
    sorted_scr = nc.dram_tensor("sorted_scr", [TN * 512, D], BF16, kind="Internal").ap()
    sout_scr = nc.dram_tensor("sout_scr", [TN * 512, D], F32, kind="Internal").ap()
    srtA = nc.dram_tensor("srtA", [1, 256], F32, kind="Internal").ap()
    srtB = nc.dram_tensor("srtB", [32, 8], F32, kind="Internal").ap()
    srtC = nc.dram_tensor("srtC", [1, 128], F32, kind="Internal").ap()

    with ExitStack() as stk:
        e = stk.enter_context

        identb = nc.alloc_sbuf_tensor("identb", [128, 128], BF16)
        identf = nc.alloc_sbuf_tensor("identf", [128, 128], F32)
        utri = nc.alloc_sbuf_tensor("utri", [128, 128], F32)
        onesk1 = nc.alloc_sbuf_tensor("onesk1", [1, 128], F32)
        oh127 = nc.alloc_sbuf_tensor("oh127", [128, 1], F32)
        iota8 = nc.alloc_sbuf_tensor("iota8", [128, 8], F32)
        big8 = nc.alloc_sbuf_tensor("big8", [128, 8], F32)
        negtri = nc.alloc_sbuf_tensor("negtri", [128, 128], F32)
        onescol = nc.alloc_sbuf_tensor("onescol", [128, 1], F32)
        bgB = nc.alloc_sbuf_tensor("bgB", [128, NH], F32)
        bgrow = nc.alloc_sbuf_tensor("bgrow", [1, NH], F32)
        wgT = nc.alloc_sbuf_tensor("wgT", [128, NK, NH], BF16)
        gwS = nc.alloc_sbuf_tensor("gwS", [128, NK, NE], BF16)
        wgtmp = nc.alloc_sbuf_tensor("wgtmp", [16, D], BF16)
        actc = nc.alloc_sbuf_tensor("actc", [128, 8], F32)
        utriS = nc.alloc_sbuf_tensor("utriS", [128, 128], F32)
        iotar = nc.alloc_sbuf_tensor("iotar", [1, 128], F32)
        iotac = nc.alloc_sbuf_tensor("iotac", [128, 1], F32)
        mk_s = nc.alloc_sbuf_tensor("mk_s", [128, 32, 8], F32)
        rk_s = nc.alloc_sbuf_tensor("rk_s", [128, 32, 8], F32)
        srow = nc.alloc_sbuf_tensor("srow", [1, 256], F32)
        car_s = nc.alloc_sbuf_tensor("car_s", [32, 8], F32)
        rows_s = nc.alloc_sbuf_tensor("rows_s", [1, 128], F32)
        rowi_s = nc.alloc_sbuf_tensor("rowi_s", [1, 16], I32)
        carb_s = nc.alloc_sbuf_tensor("carb_s", [128, 32, 8], F32)
        rowb_s = nc.alloc_sbuf_tensor("rowb_s", [128, 128], F32)
        pos_s = nc.alloc_sbuf_tensor("pos_s", [128, 32], F32)
        posi_s = nc.alloc_sbuf_tensor("posi_s", [128, 32], I32)
        widxQ = nc.alloc_sbuf_tensor("widxQ", [128, 16], I32)
        widxD = nc.alloc_sbuf_tensor("widxD", [128, 16], I32)
        zt_s = nc.alloc_sbuf_tensor("zt_s", [128, 1024], BF16)
        for _i, _v in enumerate([1e-5, 1.0, -1.0, -0.5, 0.125, -0.125, 1.0 / 1024.0, 256.0]):
            nc.const_aps.register(F32, _v, actc[:, _i:_i + 1])

        OVB = 194 * 1024
        ar = Arena(nc, "ov", OVB)

        off0 = ar.off
        wqT = ar.take([128, NK, D], BF16, "wqT")
        wkT = ar.take([128, NK, D], BF16, "wkT")
        wvT = ar.take([128, NK, D], BF16, "wvT")
        woT = ar.take([128, NK, D], BF16, "woT")
        wtmp = ar.take([128, NK, D], BF16, "wtmp")
        x_in = [ar.take([128, D], F32, "x0"), ar.take([128, D], F32, "x1")]
        r1 = ar.take([128, D], F32, "r1")
        e1 = ar.take([128, D], F32, "e1")
        e1a = ar.take([128, 512], F32, "e1a")
        def _pp2(shape, dt, nm):
            return [ar.take(shape, dt, nm + "0"), ar.take(shape, dt, nm + "1")]
        T_bc = ar.take([128, NH * CH], F32, "T_bc")
        LTs = ar.take([16, 128], F32, "LTs")
        LCrow = ar.take([1, NH], F32, "LCrow")
        ACrow = ar.take([1, NH], F32, "ACrow")
        def _pp(shape, dt, nm):
            return [ar.take(shape, dt, nm + "0"), ar.take(shape, dt, nm + "1")]
        ln1 = _pp([128, D], BF16, "ln1")
        ln1T = _pp([128, NK, CH], BF16, "ln1T")
        qf0 = ar.take([128, D], BF16, "qf")
        qs0 = ar.take([128, D], BF16, "qs")
        kf0 = ar.take([128, D], BF16, "kf")
        qf, qs, kf = [qf0, qf0], [qs0, qs0], [kf0, kf0]
        dmp = wtmp[:, 0, :].bitcast(BF16) if False else wtmp.rearrange("p k d -> p (k d)")[:, 0:D]
        kh = _pp([128, NH, DH], BF16, "kh")
        vv = _pp([128, NH, DH], BF16, "vv")
        qfT = _pp([128, NK, CH], BF16, "qfT")
        qsT = _pp([128, NK, CH], BF16, "qsT")
        kfT = _pp([128, NK, CH], BF16, "kfT")
        zb = _pp([128, NH], F32, "zb")
        za = _pp([128, NH], F32, "za")
        zl = _pp([128, NH], F32, "zl")
        zm = _pp([128, NH], F32, "zm")
        lcol = _pp([128, NH], F32, "lcol")
        Lm = _pp([128, NH], F32, "Lm")
        Ecol = _pp([128, NH], F32, "Ecol")
        Acol = _pp([128, NH], F32, "Acol")
        ACcol = _pp([128, NH], F32, "ACcol")
        Dbuf = _pp([128, NH, CH], BF16, "Dbuf")
        Pt = _pp([128, NH, CH], BF16, "Pt")
        yT16 = _pp([128, NK, CH], BF16, "yT16")
        Sf = _pp([128, NK, DH], F32, "Sf")
        S16 = _pp([128, NK, DH], BF16, "S16")
        x_new = _pp([128, D], F32, "x_new")
        ln2 = _pp([128, D], BF16, "ln2")
        ln2T = _pp([128, NK, CH], BF16, "ln2T")
        xs = _pp([128, D], BF16, "xs")
        eqm = _pp([128, NE], F32, "eqm")
        cand = _pp([128, NE], F32, "cand")
        masks8 = _pp([128, NE], F32, "masks8")
        scol = _pp([128, 24], F32, "scol")
        A_BYTES = ar.off
        ar.off = off0

        (SC_SUM, SC_M, SC_M2, SC_VAR, SC_LNV, SC_RSTD, SC_SQ, SC_MAX,
         SC_ESC, SC_DEN, SC_SCR, SC_SR2, SC_SEL, SC_NMR, SC_NR2,
         SC_M1, SC_M21, SC_VAR1, SC_LNV1, SC_RSTD1, SC_SQ1, SC_NMR1,
         SC_NR1) = range(23)

        xsT_s = ar.take([128, NK, SWTOK], BF16, "xsT_s")
        lnT_s = ar.take([128, NK, SWTOK], BF16, "lnT_s")
        acc = ar.take([128, SWTOK // 128, D], F32, "acc")
        actT = ar.take([128, NM, SWTOK], BF16, "actT")
        wgu = [[ar.take([128, NK, 256], BF16, f"wg{i}"), ar.take([128, NK, 256], BF16, f"wu{i}")]
               for i in range(2)]
        wdb = [ar.take([128, NM, 512], BF16, f"wd{i}") for i in range(2)]
        sgb = [ar.take([128, 512], F32, f"sg{i}") for i in range(2)]
        masks_s = ar.take([128, SWTOK // 128, NE], F32, "masks_s")
        B_BYTES = ar.off
        ar.off = off0
        sx_tm = [ar.take([128, 4, D], BF16, "sx0"), ar.take([128, 4, D], BF16, "sx1")]
        sxT = [ar.take([128, NK, 512], BF16, "sxT0"), ar.take([128, NK, 512], BF16, "sxT1")]
        wgr = [ar.take([128, NK, 1024], BF16, "wgr0"), ar.take([128, NK, 1024], BF16, "wgr1")]
        wur = [ar.take([128, NK, 1024], BF16, "wur0"), ar.take([128, NK, 1024], BF16, "wur1")]
        wdr = ar.take([128, NM, 1024], BF16, "wdr")
        actT_r = ar.take([128, NM, 512], BF16, "actT_r")
        dstg = ar.take([128, 4, D], F32, "dstg")
        rst_s = ar.take([128, D], F32, "rst_s")
        rst_s2 = ar.take([128, D], F32, "rst_s2")
        sgb_r = [ar.take([128, 512], F32, "sgr0"), ar.take([128, 512], F32, "sgr1")]
        C_BYTES = ar.off
        assert max(A_BYTES, B_BYTES, C_BYTES) <= OVB, (A_BYTES, B_BYTES, C_BYTES)

        ps = [nc.alloc_psum_tensor(f"ps{i}", [128, 512], F32) for i in range(3)] + [None, None, None]
        ps[4] = nc.alloc_psum_tensor("ps4", [128, 512], F32)
        ps.append(nc.alloc_psum_tensor("ps6", [128, 512], F32))
        small1 = nc.alloc_psum_tensor("small1", [128, 512], F32)
        ps[3] = small1
        pbs = [nc.alloc_psum_tensor("pbt0", [128, 1024], BF16),
               nc.alloc_psum_tensor("pbt1", [128, 1024], BF16)]
        SMALLs = [ps[6], small1]

        sems = {s: e(nc.semaphore(s)) for s in
                ("PE", "VE", "SE", "GP", "DSC", "DXL", "DST", "DSS", "DSB",
                 "DGP", "DGL", "DGW", "DGD", "DGS", "DGA")}
        block = e(nc.Block())

        def emit(who, eng):
            cnt = {k: 0 for k in sems}
            lastw = {}
            DMA_OWNER = {"DSC": "sync", "DXL": "sync", "DST": "sync", "DSS": "sync",
                         "DSB": "sync", "DGP": "gpsimd", "DGL": "gpsimd",
                         "DGW": "gpsimd", "DGD": "gpsimd", "DGS": "gpsimd", "DGA": "gpsimd"}

            def inc(instr, s_name, n=1):
                cnt[s_name] += n
                if instr is not None:
                    instr.then_inc(sems[s_name], n)
                return (s_name, cnt[s_name])

            def wait(ev):
                if ev is None:
                    return
                s_name, v = ev
                if v <= 0 or lastw.get(s_name, 0) >= v:
                    return
                lastw[s_name] = v
                eng.wait_ge(sems[s_name], v)

            def run(owner, s_name, deps, f, dn=1):
                if who == owner:
                    for d_ in deps:
                        wait(d_)
                    return inc(f(), s_name, dn)
                return inc(None, s_name, dn)

            def tens(deps, f):
                return run("tensor", "PE", deps, f)

            def vec(deps, f):
                return run("vector", "VE", deps, f)

            def sca(deps, f):
                return run("scalar", "SE", deps, f)

            def gp(deps, f):
                return run("gpsimd", "GP", deps, f)

            def dma(sem, deps, fs):
                if not isinstance(fs, (list, tuple)):
                    fs = [fs]
                if who == DMA_OWNER[sem]:
                    wait((sem, cnt[sem]))
                    for d_ in deps:
                        wait(d_)
                    for f_ in fs:
                        inc(f_(), sem, 16)
                else:
                    for f_ in fs:
                        inc(None, sem, 16)
                return (sem, cnt[sem])

            ev_c = []
            for tile, src in ((identb, cst["c_ident_bf"]), (identf, cst["c_ident_f32"]),
                              (utri, cst["c_utri"]), (onesk1, cst["c_onesk1"]),
                              (oh127, cst["c_oh127"]), (iota8, cst["c_iota8"]),
                              (big8, cst["c_big8"]), (negtri, cst["c_negtri"]),
                              (onescol, cst["c_onescol"]), (actc, cst["c_actconst"]),
                              (utriS, cst["c_utri_strict"]), (iotar, cst["c_iota_row"]),
                              (iotac, cst["c_iota_col"]), (bgrow, bg_d)):
                ev_c.append(dma('DSC', [], lambda t_=tile, s_=src: eng.dma_start(out=t_[:], in_=s_)))
            ev_iota = ev_c[5]
            ev_big = ev_c[6]
            ev_cst_all = ev_c[-1]
            if who in ("scalar", "vector", "tensor"):
                wait(ev_cst_all)
            ev_gw = dma('DGP', [], lambda: eng.dma_start(out=gwS[:], in_=gw_d.rearrange("(k p) e -> p k e", p=128)))
            ev_wgl = dma('DGP', [], lambda: eng.dma_start(out=wgtmp[:], in_=wg_d))
            ev_bgb_mm = tens([ev_cst_all], lambda: eng.matmul(out=ps[0][:, 0:NH], lhsT=onesk1[:],
                                                              rhs=bgrow[:], start=True, stop=True))
            ev_bgb = vec([ev_bgb_mm], lambda: eng.tensor_copy(out=bgB[:], in_=ps[0][:, 0:NH]))

            wT_tiles = {"wq": wqT, "wk": wkT, "wv": wvT, "wo": woT}
            w_srcs = {"wq": wq_d, "wk": wk_d, "wv": wv_d, "wo": wo_d}
            ev_wT = {}
            ev_wtmp_free = None
            ev_pb_free = None
            for wn in ("wq", "wk", "wv", "wo"):
                ev_l = dma('DGP', [ev_wtmp_free] if ev_wtmp_free else [],
                             lambda wn_=wn: eng.dma_start(out=wtmp[:],
                                                          in_=w_srcs[wn_].rearrange("(nk p) d -> p nk d", p=128)))
                last_ev = None
                for g in range(8):
                    def tf(g_=g):
                        last = None
                        for dk in range(8):
                            last = eng.transpose(out=pbs[0][:, dk * 128:(dk + 1) * 128],
                                                 in_=wtmp[:, g_, dk * 128:(dk + 1) * 128],
                                                 identity=identb[:])
                        return last
                    ev_t = tens([ev_l, ev_c[0], ev_pb_free], tf)

                    def vf(wn_=wn, g_=g):
                        return eng.tensor_copy(
                            out=wT_tiles[wn_][:, :, g_ * 128:(g_ + 1) * 128],
                            in_=pbs[0][:].rearrange("p (k t) -> p k t", k=8))
                    ev_pb_free = vec([ev_t], vf)
                    last_ev = ev_pb_free
                ev_wT[wn] = last_ev
                ev_wtmp_free = last_ev
            ev_wgT_t = tens([ev_wgl, ev_pb_free],
                            lambda: [eng.transpose(out=pbs[0][:, dk * 16:(dk + 1) * 16],
                                                   in_=wgtmp[0:16, dk * 128:(dk + 1) * 128],
                                                   identity=identb[0:16, 0:16]) for dk in range(8)][-1])
            ev_wgT = vec([ev_wgT_t], lambda: eng.tensor_copy(
                out=wgT[:].rearrange("p k h -> p (k h)"), in_=pbs[0][:, 0:128]))
            ev_pb_free = ev_wgT

            ev_xfree = [None, None]
            pc = {}
            ev_Sready = [None, None]
            ev_small_free = [ev_bgb, ev_bgb]
            ev_pA = [ev_bgb, None]
            ev_dmp = [None]
            ev_y = None
            evd_scr = None
            ev_Lw_prev = None
            ev_pbf = [ev_pb_free, None]
            pbi = 0

            if PHASE_A:
                stH = {}
                exl_pend = [None, None]

                def chunk_h1a(ci):
                    nonlocal pbi, evd_scr, ev_Lw_prev, ev_y
                    st = {}
                    c, b = ci // 2, ci % 2
                    cc = b * (SEQ // CH) + c
                    par = ci % 2
                    x_ = x_in[par]
                    ln1_, ln1T_, qf_, qs_, kf_, kh_, vv_ = (ln1[par], ln1T[par], qf[par],
                                                            qs[par], kf[par], kh[par], vv[par])
                    qfT_, qsT_, kfT_ = qfT[par], qsT[par], kfT[par]
                    zb_, za_, zl_, zm_, lcol_, Lm_ = zb[par], za[par], zl[par], zm[par], lcol[par], Lm[par]
                    Ecol_, Acol_, ACcol_ = Ecol[par], Acol[par], ACcol[par]
                    Dbuf_, Pt_, yT16_ = Dbuf[par], Pt[par], yT16[par]
                    x_new_, ln2_, ln2T_, xs_ = x_new[par], ln2[par], ln2T[par], xs[par]
                    eqm_, cand_, masks8_, scol_ = eqm[par], cand[par], masks8[par], scol[par]
                    Sf_, S16_ = Sf[b], S16[b]
                    SMALL = SMALLs[par]
                    if c == 0:
                        ev_Sready[b] = vec([], lambda Sf__=Sf_, S16__=S16_: (
                            eng.memset(Sf__[:].rearrange("p k j -> p (k j)"), 0.0),
                            eng.memset(S16__[:].rearrange("p k j -> p (k j)"), 0.0))[-1])
                    exl = exl_pend[par]
                    ev_ms = sca([exl, pc.get(("scol1", par)), ev_dmp[0],
                                 ev_wT["wo"] if ci == 0 else None],
                                lambda x__=x_, s_=scol_: eng.activation(
                                    out=dmp[:], in_=x__[:], func=AF.Identity, scale=1.0 / D,
                                    accum_out=s_[:, SC_M1:SC_M1 + 1]))
                    ev_sq = sca([exl, ev_ms],
                                lambda x__=x_, s_=scol_: eng.activation(
                                    out=dmp[:], in_=x__[:], func=AF.Square, scale=D ** -0.5,
                                    accum_out=s_[:, SC_SQ1:SC_SQ1 + 1]))
                    ev_dmp[0] = ev_sq
                    ev_m2 = sca([ev_ms], lambda s_=scol_: eng.activation(
                        out=s_[:, SC_M21:SC_M21 + 1], in_=s_[:, SC_M1:SC_M1 + 1], func=AF.Square))
                    ev_var = gp([ev_sq, ev_m2], lambda s_=scol_: eng.tensor_tensor(
                        out=s_[:, SC_VAR1:SC_VAR1 + 1], in0=s_[:, SC_SQ1:SC_SQ1 + 1],
                        in1=s_[:, SC_M21:SC_M21 + 1], op=ALU.subtract))
                    ev_lnv = sca([ev_var], lambda s_=scol_: eng.activation(
                        out=s_[:, SC_LNV1:SC_LNV1 + 1], in_=s_[:, SC_VAR1:SC_VAR1 + 1],
                        func=AF.Ln, bias=EPS))
                    ev_rstd = sca([ev_lnv], lambda s_=scol_: eng.activation(
                        out=s_[:, SC_RSTD1:SC_RSTD1 + 1], in_=s_[:, SC_LNV1:SC_LNV1 + 1],
                        func=AF.Exp, scale=-0.5))
                    ev_nr = sca([ev_rstd], lambda s_=scol_: eng.activation(
                        out=s_[:, SC_NR1:SC_NR1 + 1], in_=s_[:, SC_RSTD1:SC_RSTD1 + 1],
                        func=AF.Identity, scale=-1.0))
                    ev_nmr = sca([ev_nr], lambda s_=scol_: eng.activation(
                        out=s_[:, SC_NMR1:SC_NMR1 + 1], in_=s_[:, SC_M1:SC_M1 + 1],
                        func=AF.Identity, scale=s_[:, SC_NR1:SC_NR1 + 1]))
                    ev_ln1 = sca([ev_nmr, pc.get(("ln1", par))],
                                 lambda x__=x_, l_=ln1_, s_=scol_: eng.activation(
                                     out=l_[:], in_=x__[:], func=AF.Identity,
                                     scale=s_[:, SC_RSTD1:SC_RSTD1 + 1],
                                     bias=s_[:, SC_NMR1:SC_NMR1 + 1]))
                    pc[("scol1", par)] = ev_ln1
                    ev_xcp = gp([exl, pc.get(("x_new", par)), pc.get(("x_new_dma", par)), pc.get(("ln2", par))],
                                lambda x__=x_, xn=x_new[par]: eng.tensor_copy(out=xn[:], in_=x__[:]))
                    st["ev_xcp"] = ev_xcp
                    ev_xfree[par] = [ev_xcp, ev_ln1, ev_sq, ev_ms]
                    if ci + 1 < NCH:
                        b2, c2, p2 = (ci + 1) % 2, (ci + 1) // 2, (ci + 1) % 2
                        exl_pend[p2] = dma('DXL', list(ev_xfree[p2] or []),
                                           lambda b_=b2, c_=c2, p_=p2: eng.dma_start(
                                               out=x_in[p_][:], in_=x_d[b_, c_ * CH:(c_ + 1) * CH, :]))
                    ev_t = tens([ev_ln1, ev_pbf[pbi]],
                                lambda l_=ln1_, pbx=pbs[pbi]: [eng.transpose(
                                    out=pbx[:, j * 128:(j + 1) * 128], in_=l_[:, j * 128:(j + 1) * 128],
                                    identity=identb[:]) for j in range(8)][-1])
                    ev_ln1T = vec([ev_t, pc.get(("ln1T", par))],
                                  lambda d_=ln1T_, pbx=pbs[pbi]: eng.tensor_copy(
                                      out=d_[:].rearrange("p k t -> p (k t)"), in_=pbx[:]))
                    ev_pbf[pbi] = ev_ln1T
                    pbi ^= 1

                    ev_zmm = tens([ev_ln1T, ev_small_free[par], ev_wgT],
                                  lambda l_=ln1T_: [eng.matmul(out=SMALL[:, 0:NH], lhsT=l_[:, k, :],
                                                               rhs=wgT[:, k, :],
                                                               start=(k == 0), stop=(k == NK - 1))
                                                    for k in range(NK)][-1])
                    ev_zb = vec([ev_zmm, ev_bgb, pc.get(("zb", par)), pc.get(("za", par))],
                                lambda z_=zb_: eng.tensor_tensor(out=z_[:], in0=SMALL[:, 0:NH],
                                                                 in1=bgB[:], op=ALU.add))
                    ev_za = sca([ev_zb, pc.get(("za", par))],
                                lambda z_=zb_, a_=za_: eng.activation(out=a_[:], in_=z_[:], func=AF.Abs))
                    ev_ze = sca([ev_za], lambda a_=za_: eng.activation(out=a_[:], in_=a_[:], func=AF.Exp, scale=-1.0))
                    ev_zl = sca([ev_ze], lambda a_=za_, l_=zl_: eng.activation(out=l_[:], in_=a_[:],
                                                                               func=AF.Ln, bias=1.0))
                    ev_zm = vec([ev_zb, pc.get(("zm", par))],
                                lambda z_=zb_, m_=zm_: eng.tensor_scalar(out=m_[:], in0=z_[:], scalar1=0.0,
                                                                         scalar2=None, op0=ALU.min))
                    ev_l = vec([ev_zm, ev_zl, pc.get(("lcol", par))],
                               lambda m_=zm_, l_=zl_, o_=lcol_: eng.tensor_tensor(out=o_[:], in0=m_[:],
                                                                                  in1=l_[:], op=ALU.subtract))
                    ev_Lmm = tens([ev_l, ev_zb, ev_c[2]],
                                  lambda o_=lcol_: eng.matmul(out=SMALL[:, 16:16 + NH], lhsT=utri[:],
                                                              rhs=o_[:], start=True, stop=True))
                    ev_L = vec([ev_Lmm, pc.get(("Lm", par))],
                               lambda L__=Lm_: eng.tensor_copy(out=L__[:], in_=SMALL[:, 16:16 + NH]))
                    ev_LCmm = tens([ev_L], lambda L__=Lm_: eng.matmul(out=SMALL[0:1, 32:32 + NH], lhsT=oh127[:],
                                                                      rhs=L__[:], start=True, stop=True))
                    ev_LC = vec([ev_LCmm, pc.get(("LCrow", 0))],
                                lambda: eng.tensor_copy(out=LCrow[:], in_=SMALL[0:1, 32:32 + NH]))
                    ev_LCb = tens([ev_LC], lambda: eng.matmul(out=SMALL[:, 48:48 + NH], lhsT=onesk1[:],
                                                              rhs=LCrow[:], start=True, stop=True))
                    ev_Ep = vec([ev_LCb, pc.get(("Ecol", par))],
                                lambda L__=Lm_, E_=Ecol_: eng.tensor_tensor(
                                    out=E_[:], in0=SMALL[:, 48:48 + NH], in1=L__[:], op=ALU.subtract))
                    ev_E = sca([ev_Ep], lambda E_=Ecol_: eng.activation(out=E_[:], in_=E_[:], func=AF.Exp))
                    ev_A = sca([ev_L, pc.get(("Acol", par))],
                               lambda L__=Lm_, A__=Acol_: eng.activation(out=A__[:], in_=L__[:], func=AF.Exp))
                    ev_ACr = sca([ev_LC], lambda: eng.activation(out=ACrow[:], in_=LCrow[:], func=AF.Exp))
                    ev_ACb = tens([ev_ACr, ev_Ep], lambda: eng.matmul(out=SMALL[:, 64:64 + NH], lhsT=onesk1[:],
                                                                      rhs=ACrow[:], start=True, stop=True))
                    ev_AC = vec([ev_ACb, pc.get(("ACcol", par))],
                                lambda A__=ACcol_: eng.tensor_copy(out=A__[:], in_=SMALL[:, 64:64 + NH]))
                    ev_LTt = tens([ev_L, ev_AC, ev_c[1]],
                                  lambda L__=Lm_: eng.transpose(out=SMALL[:, 80:208][0:16], in_=L__[:],
                                                                identity=identf[:]))
                    ev_LTs = vec([ev_LTt, ev_Lw_prev],
                                 lambda: eng.tensor_copy(out=LTs[:], in_=SMALL[0:16, 80:208]))
                    ev_small_free[par] = ev_LTs
                    ev_Lw = dma('DGL', [ev_LTs], lambda cc_=cc: eng.dma_start(out=L_scr[cc_], in_=LTs[:]))
                    ev_Lw_prev = ev_Lw
                    ev_Lr = dma('DGL', [evd_scr],
                                lambda cc_=cc: eng.dma_start(
                                    out=T_bc[:],
                                    in_=L_scr[cc_].rearrange("h t -> (h t)").unsqueeze(0).broadcast_to([128, NH * CH])))
                    ev_d1 = gp([ev_Lr, ev_L],
                               lambda L__=Lm_: eng.tensor_tensor(
                                   out=T_bc[:].rearrange("p (h t) -> p h t", h=NH),
                                   in0=T_bc[:].rearrange("p (h t) -> p h t", h=NH),
                                   in1=L__[:].unsqueeze(2).broadcast_to([128, NH, 128]),
                                   op=ALU.subtract))
                    ev_d2 = gp([ev_d1, ev_c[7]], lambda: eng.tensor_tensor(
                        out=T_bc[:].rearrange("p (h t) -> p h t", h=NH),
                        in0=T_bc[:].rearrange("p (h t) -> p h t", h=NH),
                        in1=negtri[:].unsqueeze(1).broadcast_to([128, NH, 128]), op=ALU.subtract))
                    ev_Dx = sca([ev_d2, pc.get(("Dbuf", par))],
                                lambda D__=Dbuf_: eng.activation(
                                    out=D__[:].rearrange("p h t -> p (h t)"),
                                    in_=T_bc[:], func=AF.Exp))
                    ev_D_piece = [ev_Dx, ev_Dx, ev_Dx, ev_Dx]
                    evd_scr = ev_Dx

                    st.update(dict(exl=exl, ev_ln1T=ev_ln1T, ev_E=ev_E, ev_A=ev_A,
                                   ev_AC=ev_AC, ev_sq=ev_sq, ev_t=ev_t, ev_l=ev_l, ev_zl=ev_zl,
                                   ev_Lmm=ev_Lmm, ev_ACr=ev_ACr, ev_rstd=ev_rstd,
                                   ev_D_piece=ev_D_piece))
                    return st

                def chunk_h1b(ci, st):
                    nonlocal pbi, evd_scr, ev_Lw_prev, ev_y
                    c, b = ci // 2, ci % 2
                    cc = b * (SEQ // CH) + c
                    par = ci % 2
                    x_ = x_in[par]
                    ln1_, ln1T_, qf_, qs_, kf_, kh_, vv_ = (ln1[par], ln1T[par], qf[par],
                                                            qs[par], kf[par], kh[par], vv[par])
                    qfT_, qsT_, kfT_ = qfT[par], qsT[par], kfT[par]
                    Ecol_, Acol_ = Ecol[par], Acol[par]
                    ev_ln1T = st["ev_ln1T"]
                    ev_E = st["ev_E"]
                    ev_A = st["ev_A"]
                    ev_sq = st["ev_sq"]
                    def proj(wT_t, ev_w, scale_pm, out_f, out_scaled3, scale_ap, ev_dep_scale,
                             war_r1, war_f, war_scaled):
                        ev_half = [None, None]
                        for hf in range(2):
                            bank = ps[hf]
                            ev_mm = tens([ev_ln1T, ev_w, ev_pA[hf]],
                                         lambda hf_=hf, bank_=bank, w_=wT_t, l_=ln1T_: [
                                             eng.matmul(out=bank_[:], lhsT=l_[:, k, :],
                                                        rhs=w_[:, k, hf_ * 512:(hf_ + 1) * 512],
                                                        start=(k == 0), stop=(k == NK - 1)) for k in range(NK)][-1])
                            ev_r = sca([ev_mm] + ([war_r1] if hf == 0 else []),
                                       lambda hf_=hf, bank_=bank: eng.activation(
                                           out=r1[:, hf_ * 512:(hf_ + 1) * 512], in_=bank_[:],
                                           func=AF.Relu, scale=scale_pm))
                            ev_n = sca([ev_r], lambda bank_=bank: eng.activation(out=e1a[:], in_=bank_[:],
                                                                                 func=AF.Relu, scale=-scale_pm))
                            ev_e = sca([ev_n], lambda hf_=hf: eng.activation(
                                out=e1[:, hf_ * 512:(hf_ + 1) * 512], in_=e1a[:], func=AF.Exp, scale=-1.0))
                            ev_pA[hf] = ev_e
                            ev_half[hf] = ev_e
                        ev_f = gp([ev_half[0], ev_half[1], war_f],
                                  lambda o_=out_f: eng.tensor_tensor(out=o_[:], in0=r1[:], in1=e1[:], op=ALU.add))
                        ev_s2 = gp([ev_f, ev_dep_scale, war_scaled],
                                   lambda o_=out_f, os_=out_scaled3, sap=scale_ap: eng.tensor_tensor(
                                       out=os_, in0=o_[:].rearrange("p (h j) -> p h j", h=NH),
                                       in1=sap, op=ALU.mult))
                        return ev_f, ev_s2

                    A_ap = Acol_[:].unsqueeze(2).broadcast_to([128, NH, DH])
                    E_ap = Ecol_[:].unsqueeze(2).broadcast_to([128, NH, DH])
                    ev_qf, ev_qs = proj(wqT, ev_wT["wq"], QSCALE, qf_,
                                        qs_[:].rearrange("p (h j) -> p h j", h=NH), A_ap, ev_A,
                                        pc.get(("r1", 1 - par)), pc.get(("qf", 1 - par)), pc.get(("qs", 1 - par)))
                    ev_kf, ev_kh = proj(wkT, ev_wT["wk"], 1.0, kf_, kh_[:], E_ap, ev_E,
                                        ev_qf, pc.get(("kf", par)), pc.get(("kh", par)))
                    ev_v = None
                    for hf in range(2):
                        bank = ps[hf]
                        ev_mm = tens([ev_ln1T, ev_wT["wv"], ev_pA[hf]],
                                     lambda hf_=hf, bank_=bank, l_=ln1T_: [
                                         eng.matmul(out=bank_[:], lhsT=l_[:, k, :],
                                                    rhs=wvT[:, k, hf_ * 512:(hf_ + 1) * 512],
                                                    start=(k == 0), stop=(k == NK - 1)) for k in range(NK)][-1])
                        ev_v = vec([ev_mm] + ([pc.get(("vv", par))] if hf == 0 else []),
                                   lambda hf_=hf, bank_=bank, v_=vv_: eng.tensor_copy(
                                       out=v_[:].rearrange("p h j -> p (h j)")[:, hf_ * 512:(hf_ + 1) * 512],
                                       in_=bank_[:]))
                        ev_pA[hf] = ev_v
                    ev_tr = {}
                    for src_t, dst, evp, nm in ((qf_, qfT_, ev_qf, "qfT"), (qs_, qsT_, ev_qs, "qsT"),
                                                (kf_, kfT_, ev_kf, "kfT")):
                        ev_t2 = tens([evp, ev_pbf[pbi]],
                                     lambda s_=src_t, pbx=pbs[pbi]: [eng.transpose(
                                         out=pbx[:, j * 128:(j + 1) * 128],
                                         in_=s_[:, j * 128:(j + 1) * 128],
                                         identity=identb[:]) for j in range(8)][-1])
                        ev_pbf[pbi] = vec([ev_t2, pc.get((nm, par))],
                                          lambda d_=dst, pbx=pbs[pbi]: eng.tensor_copy(
                                              out=d_[:].rearrange("p k t -> p (k t)"), in_=pbx[:]))
                        ev_tr[nm] = ev_pbf[pbi]
                        pbi ^= 1
                    st.update(dict(ev_tr=ev_tr, ev_v=ev_v, ev_kh=ev_kh,
                                   ev_qf=ev_qf, ev_qs=ev_qs, ev_kf=ev_kf))
                    return st

                    st.update(dict(ev_tr=ev_tr, ev_v=ev_v, ev_kh=ev_kh,
                                   ev_qf=ev_qf, ev_qs=ev_qs, ev_kf=ev_kf))
                    return st

                def chunk_h2a(ci, st):
                    nonlocal pbi, evd_scr, ev_Lw_prev, ev_y
                    c, b = ci // 2, ci % 2
                    cc = b * (SEQ // CH) + c
                    par = ci % 2
                    qfT_, qsT_, kfT_ = qfT[par], qsT[par], kfT[par]
                    kh_, vv_ = kh[par], vv[par]
                    Dbuf_, Pt_, yT16_ = Dbuf[par], Pt[par], yT16[par]
                    ACcol_ = ACcol[par]
                    Sf_, S16_ = Sf[b], S16[b]
                    SMALL = SMALLs[par]
                    ev_tr = st["ev_tr"]
                    ev_v = st["ev_v"]
                    ev_kh = st["ev_kh"]
                    ev_D_piece = st["ev_D_piece"]
                    ev_AC = st["ev_AC"]
                    ev_Pe_last = None
                    ev_Pmm_last = None
                    ev_Sd = None
                    ev_yT = None
                    ev_ymm_w = None
                    ev_Pe_hist = {}
                    for wv_i in range(2):
                        for hh in range(8):
                            h = wv_i * 8 + hh
                            bank = ps[2]
                            ev_Pmm = tens([ev_tr["kfT"], ev_tr["qfT"],
                                           ev_Pe_hist.get(h - 4, pc.get(("Pbank", par)) if h < 4 else None)],
                                          lambda h_=h, bank_=bank, kT=kfT_, qT=qfT_: eng.matmul(
                                              out=bank_[:, 128 * (h_ % 4):128 * (h_ % 4) + 128],
                                              lhsT=kT[64 * (h_ % 2):64 * (h_ % 2) + 64, h_ // 2, :],
                                              rhs=qT[64 * (h_ % 2):64 * (h_ % 2) + 64, h_ // 2, :],
                                              start=True, stop=True))
                            ev_Pmm_last = ev_Pmm
                            ev_Pe = vec([ev_Pmm, ev_D_piece[h // 4]] + ([pc.get(("Pt", par))] if h == 0 else []),
                                        lambda h_=h, bank_=bank, P_=Pt_, D__=Dbuf_: eng.tensor_tensor(
                                            out=P_[:, h_, :], in0=bank_[:, 128 * (h_ % 4):128 * (h_ % 4) + 128],
                                            in1=D__[:, h_, :], op=ALU.mult))
                            ev_Pe_last = ev_Pe
                            ev_Pe_hist[h] = ev_Pe

                            def ymm(h_=h, S16__=S16_, qT=qsT_, v_=vv_, P_=Pt_):
                                base_ = 64 * (h_ % 2)
                                kc_ = h_ // 2
                                yoff_ = 128 * ((h_ // 2) % 4)
                                o = ps[4][base_:base_ + 64, yoff_:yoff_ + 128]
                                eng.matmul(out=o, lhsT=S16__[base_:base_ + 64, kc_, :],
                                           rhs=qT[base_:base_ + 64, kc_, :], start=True, stop=False,
                                           tile_position=(base_, base_))
                                return eng.matmul(out=o, lhsT=v_[:, h_, :], rhs=P_[:, h_, :],
                                                  start=False, stop=True, tile_position=(0, base_))
                            ev_ymm_w = tens([ev_Pe, ev_tr["qsT"], ev_v, ev_Sready[b], ev_y], ymm)

                            def sdmm(h_=h, k_=kh_, v_=vv_, SM=SMALL):
                                base_ = 64 * (h_ % 2)
                                return eng.matmul(out=SM[base_:base_ + 64, 64 * (h_ // 2):64 * (h_ // 2) + 64],
                                                  lhsT=k_[:, h_, :], rhs=v_[:, h_, :], start=True, stop=True,
                                                  tile_position=(0, base_))
                            ev_Sd = tens([ev_kh, ev_v] + ([ev_small_free[par], ev_Sready[b]] if h == 0 else []), sdmm)
                        ev_yT = vec([ev_ymm_w, pc.get(("yT16", par)) if wv_i == 0 else None],
                                    lambda g_=wv_i, y_=yT16_: eng.tensor_copy(
                                        out=y_[:, 4 * g_:4 * g_ + 4, :].rearrange("p k t -> p (k t)"),
                                        in_=ps[4][:]))
                        ev_y = ev_yT
                    st.update(dict(ev_Pe_last=ev_Pe_last, ev_Pmm_last=ev_Pmm_last,
                                   ev_Sd=ev_Sd, ev_yT=ev_yT, ev_ymm_w=ev_ymm_w))
                    return st

                def chunk_h2b(ci, st):
                    nonlocal pbi, evd_scr, ev_Lw_prev, ev_y
                    c, b = ci // 2, ci % 2
                    cc = b * (SEQ // CH) + c
                    par = ci % 2
                    x_ = x_in[par]
                    yT16_ = yT16[par]
                    x_new_, ln2_, ln2T_, xs_ = x_new[par], ln2[par], ln2T[par], xs[par]
                    eqm_, cand_, masks8_, scol_ = eqm[par], cand[par], masks8[par], scol[par]
                    Sf_, S16_ = Sf[b], S16[b]
                    ACcol_ = ACcol[par]
                    SMALL = SMALLs[par]
                    ev_AC = st["ev_AC"]
                    exl = st["exl"]
                    ev_tr = st["ev_tr"]
                    ev_v = st["ev_v"]
                    ev_kh = st["ev_kh"]
                    ev_qf = st["ev_qf"]
                    ev_qs = st["ev_qs"]
                    ev_kf = st["ev_kf"]
                    ev_t = st["ev_t"]
                    ev_l = st["ev_l"]
                    ev_zl = st["ev_zl"]
                    ev_Lmm = st["ev_Lmm"]
                    ev_A = st["ev_A"]
                    ev_ACr = st["ev_ACr"]
                    ev_rstd = st["ev_rstd"]
                    ev_sq = st["ev_sq"]
                    ev_xcp = st["ev_xcp"]
                    ev_Pe_last = st["ev_Pe_last"]
                    ev_Pmm_last = st["ev_Pmm_last"]
                    ev_Sd = st["ev_Sd"]
                    ev_yT = st["ev_yT"]
                    ev_ymm_w = st["ev_ymm_w"]
                    ev_Su = None
                    for h in range(NH):
                        ev_Su = vec([ev_Sd, ev_AC] + ([ev_yT] if h == 0 else []),
                                    lambda h_=h, Sf__=Sf_, A__=ACcol_, SM=SMALL: eng.scalar_tensor_tensor(
                                        out=Sf__[64 * (h_ % 2):64 * (h_ % 2) + 64, h_ // 2, :],
                                        in0=Sf__[64 * (h_ % 2):64 * (h_ % 2) + 64, h_ // 2, :],
                                        scalar=A__[64 * (h_ % 2):64 * (h_ % 2) + 64, h_:h_ + 1],
                                        in1=SM[64 * (h_ % 2):64 * (h_ % 2) + 64,
                                               64 * (h_ // 2):64 * (h_ // 2) + 64],
                                        op0=ALU.mult, op1=ALU.add))
                    ev_small_free[par] = ev_Su
                    ev_S16 = vec([ev_Su], lambda Sf__=Sf_, S16__=S16_: eng.tensor_copy(
                        out=S16__[:].rearrange("p k j -> p (k j)"),
                        in_=Sf__[:].rearrange("p k j -> p (k j)")))
                    ev_Sready[b] = ev_S16
                    ev_res = None
                    for hf in range(2):
                        bank = ps[hf]
                        ev_mm = tens([ev_yT, ev_wT["wo"], ev_pA[hf]],
                                     lambda hf_=hf, bank_=bank, y_=yT16_: [
                                         eng.matmul(out=bank_[:], lhsT=y_[:, k, :],
                                                    rhs=woT[:, k, hf_ * 512:(hf_ + 1) * 512],
                                                    start=(k == 0), stop=(k == NK - 1)) for k in range(NK)][-1])
                        ev_omm_last = ev_mm
                        ev_res = vec([ev_mm] + ([ev_xcp] if hf == 0 else []),
                                     lambda hf_=hf, bank_=bank, xn=x_new_: eng.tensor_tensor(
                                         out=xn[:, hf_ * 512:(hf_ + 1) * 512], in0=bank_[:],
                                         in1=xn[:, hf_ * 512:(hf_ + 1) * 512], op=ALU.add))
                        ev_pA[hf] = ev_res
                    ev_xn_st = dma('DST', [ev_res], lambda b_=b, c_=c, xn=x_new_: eng.dma_start(
                        out=out_d[b_, c_ * CH:(c_ + 1) * CH, :], in_=xn[:]))
                    ev_ms2 = sca([ev_res, ev_dmp[0], pc.get(("ln2sca", par))],
                                 lambda xn=x_new_, s_=scol_: eng.activation(
                                     out=dmp[:], in_=xn[:], func=AF.Identity, scale=1.0 / D,
                                     accum_out=s_[:, SC_M:SC_M + 1]))
                    ev_sq2 = sca([ev_res, ev_ms2], lambda xn=x_new_, s_=scol_: eng.activation(
                        out=dmp[:], in_=xn[:], func=AF.Square, scale=D ** -0.5,
                        accum_out=s_[:, SC_SQ:SC_SQ + 1]))
                    ev_dmp[0] = ev_sq2
                    ev_m22 = sca([ev_ms2], lambda s_=scol_: eng.activation(
                        out=s_[:, SC_M2:SC_M2 + 1], in_=s_[:, SC_M:SC_M + 1], func=AF.Square))
                    ev_var2 = gp([ev_sq2, ev_m22], lambda s_=scol_: eng.tensor_tensor(
                        out=s_[:, SC_VAR:SC_VAR + 1], in0=s_[:, SC_SQ:SC_SQ + 1],
                        in1=s_[:, SC_M2:SC_M2 + 1], op=ALU.subtract))
                    ev_lnv2 = sca([ev_var2], lambda s_=scol_: eng.activation(
                        out=s_[:, SC_LNV:SC_LNV + 1], in_=s_[:, SC_VAR:SC_VAR + 1], func=AF.Ln, bias=EPS))
                    ev_rstd2 = sca([ev_lnv2], lambda s_=scol_: eng.activation(
                        out=s_[:, SC_RSTD:SC_RSTD + 1], in_=s_[:, SC_LNV:SC_LNV + 1], func=AF.Exp, scale=-0.5))
                    ev_nr2 = sca([ev_rstd2], lambda s_=scol_: eng.activation(
                        out=s_[:, SC_NR2:SC_NR2 + 1], in_=s_[:, SC_RSTD:SC_RSTD + 1],
                        func=AF.Identity, scale=-1.0))
                    ev_nmr2 = sca([ev_nr2], lambda s_=scol_: eng.activation(
                        out=s_[:, SC_NMR:SC_NMR + 1], in_=s_[:, SC_M:SC_M + 1],
                        func=AF.Identity, scale=s_[:, SC_NR2:SC_NR2 + 1]))
                    ev_ln2 = sca([ev_nmr2, pc.get(("ln2", par))],
                                 lambda xn=x_new_, l_=ln2_, s_=scol_: eng.activation(
                                     out=l_[:], in_=xn[:], func=AF.Identity,
                                     scale=s_[:, SC_RSTD:SC_RSTD + 1], bias=s_[:, SC_NMR:SC_NMR + 1]))
                    ev_t3 = tens([ev_ln2, ev_pbf[pbi]],
                                 lambda l_=ln2_, pbx=pbs[pbi]: [eng.transpose(
                                     out=pbx[:, j * 128:(j + 1) * 128], in_=l_[:, j * 128:(j + 1) * 128],
                                     identity=identb[:]) for j in range(8)][-1])
                    ev_ln2T = vec([ev_t3, pc.get(("ln2T", par)), pc.get(("ln2T_pe", par))],
                                  lambda d_=ln2T_, pbx=pbs[pbi]: eng.tensor_copy(
                                      out=d_[:].rearrange("p k t -> p (k t)"), in_=pbx[:]))
                    ev_pbf[pbi] = ev_ln2T
                    pbi ^= 1
                    ev_lg = tens([ev_ln2T, ev_gw, ev_small_free[par]],
                                 lambda l_=ln2T_: [eng.matmul(out=SMALL[:, 0:NE], lhsT=l_[:, k, :],
                                                              rhs=gwS[:, k, :],
                                                              start=(k == 0), stop=(k == NK - 1))
                                                   for k in range(NK)][-1])
                    ev_mx = vec([ev_lg], lambda s_=scol_: eng.reduce_max(
                        out=s_[:, SC_MAX:SC_MAX + 1], in_=SMALL[:, 0:NE], axis=AX.X))
                    ev_eq = vec([ev_mx, pc.get(("eqm", par))],
                                lambda e_=eqm_, s_=scol_: eng.tensor_scalar(
                                    out=e_[:], in0=SMALL[:, 0:NE], scalar1=s_[:, SC_MAX:SC_MAX + 1],
                                    scalar2=None, op0=ALU.is_ge))
                    ev_small_free[par] = ev_eq
                    ev_cd = vec([ev_eq, ev_big, pc.get(("cand", par))],
                                lambda e_=eqm_, c_=cand_: eng.tensor_tensor(out=c_[:], in0=e_[:],
                                                                            in1=big8[:], op=ALU.mult))
                    ev_cd2 = vec([ev_cd], lambda c_=cand_: eng.tensor_scalar(
                        out=c_[:], in0=c_[:], scalar1=256.0, scalar2=None, op0=ALU.add))
                    ev_sel = vec([ev_cd2], lambda c_=cand_, s_=scol_: eng.tensor_reduce(
                        out=s_[:, SC_SEL:SC_SEL + 1], in_=c_[:], axis=AX.X, op=ALU.min))
                    ev_msk = vec([ev_sel, pc.get(("masks8", par))],
                                 lambda m_=masks8_, s_=scol_: eng.tensor_tensor(
                                     out=m_[:], in0=s_[:, SC_SEL:SC_SEL + 1].broadcast_to([128, NE]),
                                     in1=iota8[:], op=ALU.is_equal))
                    ev_esc = sca([ev_mx], lambda s_=scol_: eng.activation(
                        out=s_[:, SC_ESC:SC_ESC + 1], in_=s_[:, SC_MAX:SC_MAX + 1], func=AF.Exp, scale=-1.0))
                    ev_den = vec([ev_esc], lambda s_=scol_: eng.tensor_scalar(
                        out=s_[:, SC_DEN:SC_DEN + 1], in0=s_[:, SC_ESC:SC_ESC + 1],
                        scalar1=1.0, scalar2=None, op0=ALU.add))
                    ev_scr = vec([ev_den], lambda s_=scol_: eng.reciprocal(
                        out=s_[:, SC_SCR:SC_SCR + 1], in_=s_[:, SC_DEN:SC_DEN + 1]))
                    ev_sr2 = vec([ev_scr, ev_rstd2], lambda s_=scol_: eng.tensor_tensor(
                        out=s_[:, SC_SR2:SC_SR2 + 1], in0=s_[:, SC_SCR:SC_SCR + 1],
                        in1=s_[:, SC_RSTD:SC_RSTD + 1], op=ALU.mult))
                    ev_xs = vec([ev_sr2, pc.get(("xs", par))],
                                lambda xn=x_new_, x2=xs_, s_=scol_: eng.tensor_scalar(
                                    out=x2[:], in0=xn[:], scalar1=s_[:, SC_M:SC_M + 1],
                                    scalar2=s_[:, SC_SR2:SC_SR2 + 1], op0=ALU.subtract, op1=ALU.mult))
                    ev_st1 = dma('DSS', [ev_ln2T], lambda cc_=cc, d_=ln2T_: eng.dma_start(out=lnT_scr[cc_], in_=d_[:]))
                    ev_st2 = dma('DSS', [ev_xs], lambda cc_=cc, d_=xs_: eng.dma_start(out=xs_scr[cc_], in_=d_[:]))
                    ev_st3 = dma('DSS', [ev_msk], lambda cc_=cc, m_=masks8_: eng.dma_start(out=msk_scr[cc_], in_=m_[:]))

                    for k_, v_ in {
                        "scol": ev_xs, "scolg": ev_sq2, "sc_sca": ev_rstd2, "r1": ev_qf, "ln1": ev_t,
                        "ln1T": ev_v, "zb": ev_l, "za": ev_zl, "zm": ev_l, "lcol": ev_Lmm,
                        "Lm": ev_A, "LCrow": ev_ACr, "Ecol": ev_kh, "Acol": ev_qs, "ACcol": ev_Su,
                        "Dbuf": ev_Pe_last,
                        "qf": ev_tr["qfT"], "qs": ev_tr["qsT"], "kf": ev_tr["kfT"],
                        "kh": ev_Sd, "vv": ev_Sd, "Pt": ev_yT, "yT16": ev_omm_last,
                        "x_new": ev_xs, "x_new_dma": ev_xn_st, "ln2": ev_t3, "ln2T": ev_st1,
                        "ln2sca": ev_ln2,
                        "xs": ev_st2, "eqm": ev_cd, "cand": ev_msk,
                        "masks8": ev_st3, "qfT": ev_Pmm_last, "qsT": ev_ymm_w,
                        "kfT": ev_Pmm_last, "ln2T_pe": ev_lg, "Pbank": ev_Pe_last,
                    }.items():
                        pc[(k_, par)] = v_



                exl_pend[0] = dma('DXL', [], lambda: eng.dma_start(out=x_in[0][:], in_=x_d[0, 0:CH, :]))
                for ci in range(NCH):
                    stH[ci] = chunk_h1a(ci)
                    if ci >= 1:
                        chunk_h2a(ci - 1, stH[ci - 1])
                    chunk_h1b(ci, stH[ci])
                    if ci >= 1:
                        chunk_h2b(ci - 1, stH.pop(ci - 1))
                chunk_h2a(NCH - 1, stH[NCH - 1])
                chunk_h2b(NCH - 1, stH.pop(NCH - 1))

            evA_done = [(k, cnt[k]) for k in ("DST", "DSS", "DGL", "VE", "SE", "PE", "GP")]

            if PHASE_B and SPARSE:
                ev_mk = dma('DSB', evA_done, lambda: eng.dma_start(
                    out=mk_s[:], in_=msk_scr[:].rearrange("j p e -> p j e")))
                ev_rz = vec([], lambda: eng.memset(rows_s[0:1, :], 0.0))
                ev_rkmm = tens([ev_mk, ev_c[2]] + evA_done,
                               lambda: [eng.matmul(out=ps[2][:, j * 8:(j + 1) * 8], lhsT=utri[:],
                                                   rhs=mk_s[:, j, :], start=True, stop=True)
                                        for j in range(32)][-1])
                ev_rk = vec([ev_rkmm], lambda: eng.tensor_copy(
                    out=rk_s[:].rearrange("p j e -> p (j e)"), in_=ps[2][:, 0:256]))
                ev_totmm = tens([ev_rk], lambda: eng.matmul(
                    out=ps[3][0:1, 0:256], lhsT=oh127[:], rhs=rk_s[:].rearrange("p j e -> p (j e)"),
                    start=True, stop=True))
                ev_tot = vec([ev_totmm], lambda: eng.tensor_copy(out=srow[:], in_=ps[3][0:1, 0:256]))
                ev_totw = dma('DGL', [ev_tot], lambda: eng.dma_start(out=srtA, in_=srow[:]))
                ev_totr = dma('DGL', [ev_totw], lambda: eng.dma_start(
                    out=car_s[:], in_=srtA.rearrange("a (j e) -> (a j) e", e=8)))
                ev_carmm = tens([ev_totr], lambda: eng.matmul(
                    out=ps[3][0:32, 256:264], lhsT=utriS[0:32, 0:32], rhs=car_s[:],
                    start=True, stop=True))
                ev_cntmm = tens([ev_totr], lambda: eng.matmul(
                    out=ps[3][0:1, 264:272], lhsT=utri[0:32, 127:128], rhs=car_s[:],
                    start=True, stop=True))
                ev_car = vec([ev_carmm, ev_cntmm], lambda: eng.tensor_copy(out=car_s[:], in_=ps[3][0:32, 256:264]))
                ev_cnt = vec([ev_cntmm, ev_car, ev_rz], lambda: eng.tensor_copy(out=rows_s[0:1, 72:80],
                                                                         in_=ps[3][0:1, 264:272]))
                ev_q1 = vec([ev_cnt], lambda: eng.memset(rows_s[0:1, 64:72], 0.0))
                ev_ctf = ev_q1
                for m8 in range(8):
                    ev_ctf = vec([ev_ctf], lambda m_=m8: eng.scalar_tensor_tensor(
                        out=rows_s[0:1, 64:72], in0=rows_s[0:1, 72:80], scalar=512.0 * m_,
                        in1=rows_s[0:1, 64:72], op0=ALU.is_gt, op1=ALU.add))
                ev_cc1 = vec([ev_ctf], lambda: eng.tensor_copy(out=rows_s[0:1, 8:16], in_=rows_s[0:1, 64:72]))
                ev_cs = ev_cc1
                for dlev in (1, 2, 4):
                    ev_cs = vec([ev_cs], lambda d_=dlev: eng.tensor_tensor(
                        out=rows_s[0:1, 8 + d_:16], in0=rows_s[0:1, 8 + d_:16],
                        in1=rows_s[0:1, 8:16 - d_], op=ALU.add))
                ev_px1 = vec([ev_cs], lambda: eng.tensor_tensor(out=rows_s[0:1, 0:8],
                                                                in0=rows_s[0:1, 8:16],
                                                                in1=rows_s[0:1, 64:72], op=ALU.subtract))
                ev_px = vec([ev_px1], lambda: eng.tensor_scalar(out=rows_s[0:1, 0:8], in0=rows_s[0:1, 0:8],
                                                                scalar1=512.0, scalar2=-1.0,
                                                                op0=ALU.mult, op1=ALU.add))
                ev_t2z = vec([ev_px], lambda: eng.memset(rows_s[0:1, 16:32], 0.0))
                ev_t2 = ev_t2z
                for e8 in range(8):
                    ev_t2 = vec([ev_t2], lambda e_=e8: eng.scalar_tensor_tensor(
                        out=rows_s[0:1, 16:32], in0=iotar[0:1, 0:16],
                        scalar=rows_s[0:1, 8 + e_:9 + e_], in1=rows_s[0:1, 16:32],
                        op0=ALU.is_ge, op1=ALU.add))
                ev_t2c = vec([ev_t2], lambda: eng.tensor_scalar(out=rows_s[0:1, 16:32], in0=rows_s[0:1, 16:32],
                                                                scalar1=7.0, scalar2=None, op0=ALU.min))
                ev_woq = vec([ev_t2c], lambda: eng.tensor_scalar(out=rows_s[0:1, 32:48], in0=rows_s[0:1, 16:32],
                                                                 scalar1=1024.0, scalar2=None, op0=ALU.mult))
                ev_wod = vec([ev_woq], lambda: eng.tensor_scalar(out=rows_s[0:1, 48:64], in0=rows_s[0:1, 16:32],
                                                                 scalar1=2048.0, scalar2=None, op0=ALU.mult))
                ev_rw1 = dma('DGL', [ev_car], lambda: eng.dma_start(out=srtB, in_=car_s[:]))
                ev_rw2 = dma('DGL', [ev_wod, ev_px], lambda: eng.dma_start(out=srtC, in_=rows_s[:]))
                ev_rb1 = dma('DGL', [ev_rw1], lambda: eng.dma_start(
                    out=carb_s[:].rearrange("p j e -> p (j e)"),
                    in_=srtB.rearrange("j e -> (j e)").unsqueeze(0).broadcast_to([128, 256])))
                ev_rb2 = dma('DGL', [ev_rw2], lambda: eng.dma_start(
                    out=rowb_s[:],
                    in_=srtC.rearrange("r c -> (r c)").unsqueeze(0).broadcast_to([128, 128])))
                ev_wq1 = vec([ev_rb2], lambda: eng.tensor_tensor(
                    out=pos_s[:, 0:16], in0=rowb_s[:, 32:48],
                    in1=iotac[:, 0:1].broadcast_to([128, 16]), op=ALU.add))
                ev_wq2 = vec([ev_wq1], lambda: eng.tensor_copy(out=widxQ[:], in_=pos_s[:, 0:16]))
                ev_wd1 = vec([ev_wq2], lambda: eng.tensor_tensor(
                    out=pos_s[:, 0:16], in0=rowb_s[:, 48:64],
                    in1=iotac[:, 0:1].broadcast_to([128, 16]), op=ALU.add))
                ev_wd2 = vec([ev_wd1], lambda: eng.tensor_copy(out=widxD[:], in_=pos_s[:, 0:16]))
                ev_p1 = vec([ev_rb1, ev_rk, ev_wd2], lambda: eng.tensor_tensor(
                    out=rk_s[:].rearrange("p j e -> p (j e)"),
                    in0=rk_s[:].rearrange("p j e -> p (j e)"),
                    in1=carb_s[:].rearrange("p j e -> p (j e)"), op=ALU.add))
                ev_p2 = vec([ev_p1, ev_rb2], lambda: eng.tensor_tensor(
                    out=rk_s[:], in0=rk_s[:],
                    in1=rowb_s[:, 0:8].unsqueeze(1).broadcast_to([128, 32, 8]), op=ALU.add))
                ev_pos = ev_p2
                for j32 in range(32):
                    ev_pos = vec([ev_pos], lambda j_=j32: eng.scalar_tensor_tensor(
                        out=carb_s[:, j_, :], in0=rk_s[:, j_, :], scalar=1.0,
                        in1=mk_s[:, j_, :], op0=ALU.mult, op1=ALU.mult,
                        accum_out=pos_s[:, j_:j_ + 1]))
                ev_posi = vec([ev_pos], lambda: eng.tensor_copy(out=posi_s[:], in_=pos_s[:]))

                do_scatter = SPARSE_STAGE >= 2
                do_routed = SPARSE_STAGE >= 3
                do_return = SPARSE_STAGE >= 4
                ev_zz = vec([], lambda: eng.memset(zt_s[:], 0.0))
                scat_state = {"ev": None}

                def scatter_gen():
                    ev_z = None
                    for a0 in range(TN * 512 // 128):
                        ev_z = dma('DGS', [ev_zz] if a0 == 0 else [],
                                   lambda a_=a0: eng.dma_start(
                                       out=sorted_scr[a_ * 128:(a_ + 1) * 128, :], in_=zt_s[:]))
                        if a0 % 4 == 3:
                            yield
                    ev_sc = ev_z
                    for cc2 in range(NCH if do_scatter else 0):
                        ev_ldx = dma('DXL', [ev_sc, ev_zz],
                                     lambda c_=cc2: eng.dma_start(out=zt_s[:], in_=xs_scr[c_]))
                        ev_sc = dma('DGS', [ev_ldx, ev_posi, ev_z],
                                    lambda c_=cc2: eng.indirect_dma_start(
                                        out=sorted_scr,
                                        out_offset=bass.IndirectOffsetOnAxis(posi_s[:, c_:c_ + 1], 0),
                                        in_=zt_s[:], in_offset=None))
                        scat_state["ev"] = ev_sc
                        yield
                    while True:
                        yield

                scat_pump = scatter_gen()
                scat_units = [0]

                def pump_scatter(n):
                    for _ in range(n):
                        next(scat_pump)
                        scat_units[0] += 1

            if PHASE_B:
                NTT = SWTOK // 128
                NT5 = SWTOK // 512
                ev_wgu_free = [None, None]
                ev_wgu_ld = [None, None]
                ev_wd_free = [None, None]
                ev_wd_ld = [None, None]
                ev_acc_st = None
                ev_in_free = None
                gidx = 0
                didx = 0
                widx = 0
                ev_gu_grp = {}
                ev_act_grp = {}
                ev_sil_grp = {}
                ev_down_grp = {}
                ev_stt_grp = {}
                ev_actT_free = None

                passes = ["sh"] if SPARSE else (["sh"] + list(range(NE)))
                for s in range(NSW):
                    b, half = s // 2, s % 2
                    cc0 = b * (SEQ // CH) + half * NTT
                    t0 = half * SWTOK
                    if not SPARSE:
                        ev_xsl = dma('DSB', evA_done + [ev_in_free],
                                     lambda cc0_=cc0: eng.dma_start(
                                         out=xsT_s[:].rearrange("p k (j t) -> p k j t", j=NTT),
                                         in_=xsT_scr[cc0_:cc0_ + NTT].rearrange("j p k t -> p k j t")))
                        ev_mkl = dma('DSB', [ev_acc_st],
                                     lambda cc0_=cc0: eng.dma_start(
                                         out=masks_s[:], in_=msk_scr[cc0_:cc0_ + NTT].rearrange("j p e -> p j e")))
                    else:
                        ev_xsl = None
                        ev_mkl = None
                    ev_lnl = dma('DSB', evA_done + [ev_in_free],
                                   lambda cc0_=cc0: eng.dma_start(
                                       out=lnT_s[:].rearrange("p k (j t) -> p k j t", j=NTT),
                                       in_=lnT_scr[cc0_:cc0_ + NTT].rearrange("j p k t -> p k j t")))
                    ev_accl = dma('DSB', [ev_acc_st],
                                    lambda b_=b, t0_=t0: eng.dma_start(
                                        out=acc[:].rearrange("p j d -> p j d"),
                                        in_=out_d[b_, t0_:t0_ + SWTOK, :].rearrange("(j p) d -> p j d", p=128)))
                    ev_stt_last = None
                    for ex in passes:
                        inT = lnT_s if ex == "sh" else xsT_s
                        ev_inT = ev_lnl if ex == "sh" else ev_xsl
                        g_src = shg_d if ex == "sh" else rtg_d[ex]
                        u_src = shu_d if ex == "sh" else rtu_d[ex]
                        d_src = shd_d if ex == "sh" else rtd_d[ex]
                        for j in range(NK):
                            jp = j % 2
                            ev_wgu_ld[jp] = dma(
                                'DGW', [ev_wgu_free[jp]] + (evA_done if gidx == 0 else []),
                                [lambda j_=j, jp_=jp, g_s=g_src: eng.dma_start(
                                    out=wgu[jp_][0][:],
                                    in_=g_s.rearrange("(k p) i -> p k i", p=128)[:, :, 256 * j_:256 * j_ + 256]),
                                 lambda j_=j, jp_=jp, u_s=u_src: eng.dma_start(
                                    out=wgu[jp_][1][:],
                                    in_=u_s.rearrange("(k p) i -> p k i", p=128)[:, :, 256 * j_:256 * j_ + 256])])
                            for m_loc in range(2):
                                m = 2 * j + m_loc
                                for tt in range(NT5):
                                    gp = gidx % 2
                                    pg_, pu_ = ps[0 + 2 * gp], ps[1 + 2 * gp]
                                    extra = (evA_done + ([ev_rk, ev_car, ev_cnt] if SPARSE else [])) if gidx < 2 else []
                                    ev_mm = tens([ev_wgu_ld[jp], ev_inT,
                                                  ev_act_grp.get(gidx - 2), ev_sil_grp.get(gidx - 2)] + extra,
                                                 lambda m_=m_loc, tt_=tt, jp_=jp, pg__=pg_, pu__=pu_, inT_=inT: (
                                                     [eng.matmul(out=pg__[:],
                                                                 lhsT=wgu[jp_][0][:, k, m_ * 128:m_ * 128 + 128],
                                                                 rhs=inT_[:, k, tt_ * 512:tt_ * 512 + 512],
                                                                 start=(k == 0), stop=(k == NK - 1)) for k in range(NK)],
                                                     [eng.matmul(out=pu__[:],
                                                                 lhsT=wgu[jp_][1][:, k, m_ * 128:m_ * 128 + 128],
                                                                 rhs=inT_[:, k, tt_ * 512:tt_ * 512 + 512],
                                                                 start=(k == 0), stop=(k == NK - 1)) for k in range(NK)])[-1][-1])
                                    ev_gu_grp[gidx] = ev_mm
                                    ev_sil = sca([ev_mm, ev_act_grp.get(gidx - 2)] + (evA_done if gidx < 2 else []),
                                                 lambda gp_=gp, pg__=pg_: eng.activation(out=sgb[gp_][:], in_=pg__[:],
                                                                                         func=AF.Silu))
                                    ev_sil_grp[gidx] = ev_sil
                                    ev_act = vec([ev_sil, ev_actT_free] + (evA_done if gidx < 2 else []),
                                                 lambda m__=m, tt_=tt, gp_=gp, pu__=pu_: eng.tensor_tensor(
                                                     out=actT[:, m__, tt_ * 512:tt_ * 512 + 512],
                                                     in0=sgb[gp_][:], in1=pu__[:], op=ALU.mult))
                                    ev_act_grp[gidx] = ev_act
                                    gidx += 1
                            ev_wgu_free[jp] = ev_gu_grp[gidx - 1]
                            if SPARSE and (s >= 1 or scat_units[0] < 15):
                                pump_scatter(2)
                        ev_all_act = ev_act_grp[gidx - 1]
                        for hh in range(2):
                            wp = widx % 2
                            ev_wd_ld[wp] = dma('DGD', [ev_wd_free[wp]] + (evA_done if widx < 2 else []),
                                                 lambda hh_=hh, wp_=wp, d_s=d_src: eng.dma_start(
                                                     out=wdb[wp_][:],
                                                     in_=d_s.rearrange("(m p) h -> p m h", p=128)[:, :, hh_ * 512:hh_ * 512 + 512]))
                            for ct in range(NTT):
                                dp = didx % 2
                                pd_ = (ps[4], ps[6])[dp]
                                extra = evA_done if didx < 2 else []
                                ev_dmm = tens([ev_wd_ld[wp], ev_all_act, ev_stt_grp.get(didx - 2)] + extra,
                                              lambda ct_=ct, wp_=wp, pd__=pd_: [
                                                  eng.matmul(out=pd__[:], lhsT=actT[:, m, ct_ * 128:ct_ * 128 + 128],
                                                             rhs=wdb[wp_][:, m, :], start=(m == 0), stop=(m == NM - 1))
                                                  for m in range(NM)][-1])
                                ev_down_grp[didx] = ev_dmm
                                mc = onescol[:, 0:1] if ex == "sh" else masks_s[:, ct, ex:ex + 1]
                                ev_stt = vec([ev_dmm, ev_accl, ev_mkl],
                                             lambda ct_=ct, hh_=hh, pd__=pd_, mc_=mc: eng.scalar_tensor_tensor(
                                                 out=acc[:, ct_, hh_ * 512:hh_ * 512 + 512],
                                                 in0=pd__[:], scalar=mc_,
                                                 in1=acc[:, ct_, hh_ * 512:hh_ * 512 + 512],
                                                 op0=ALU.mult, op1=ALU.add))
                                ev_stt_grp[didx] = ev_stt
                                ev_stt_last = ev_stt
                                didx += 1
                            ev_wd_free[wp] = ev_down_grp[didx - 1]
                            widx += 1
                        ev_actT_free = ev_down_grp[didx - 1]
                    ev_acc_st = dma('DST', [ev_stt_last],
                                      lambda b_=b, t0_=t0: eng.dma_start(
                                          out=out_d[b_, t0_:t0_ + SWTOK, :].rearrange("(j p) d -> p j d", p=128),
                                          in_=acc[:]))
                    ev_in_free = ev_gu_grp[gidx - 1]

                pump_scatter(15 + NCH + 4 - scat_units[0])
                ev_scatter_done = scat_state["ev"]
                ev_sh_pe = ev_gu_grp[gidx - 1]
                ev_sh_sil = ev_sil_grp[gidx - 1]
                ev_sh_act = ev_act_grp[gidx - 1]
                ev_sh_stt = ev_stt_grp[didx - 1]
                ev_sh_st = ev_acc_st
                rtg_flat = rtg_d.rearrange("e h i -> (e h) i")
                rtu_flat = rtu_d.rearrange("e h i -> (e h) i")
                rtd_flat = rtd_d.rearrange("e i h -> (e i) h")
                ev_sxld = [None, None]
                ev_sxT = [None, None]
                ev_gu_ld = [[None, None], [None, None]]
                ev_wd_ldr = None
                ev_guf = [[ev_sh_pe, ev_sh_st], [ev_sh_pe, ev_sh_st]]
                ev_wdf = [ev_sh_pe, ev_sh_st]
                ev_act_r = {}
                ev_sil_r = {}
                ev_gu_r = {}
                ev_dn_r = {}
                ev_ev_r = {}
                ev_st_r = {}
                gi = 0
                di = 0
                ev_pbfr = [ev_pbf[0], ev_pbf[1]]
                ev_sxfree = [ev_sh_pe, ev_sh_pe]
                for j in range(TN if do_routed else 0):
                    jb = j % 2
                    ev_sxld[jb] = dma('DSB', [ev_scatter_done, ev_sh_st] + ([ev_sxfree[jb]] if not isinstance(ev_sxfree[jb], list) else ev_sxfree[jb]),
                                      lambda j_=j, jb_=jb: eng.dma_start(
                                          out=sx_tm[jb_][:],
                                          in_=sorted_scr[512 * j_:512 * (j_ + 1), :].rearrange(
                                              "(a p) d -> p a d", p=128)))
                    ev_tx = None
                    for a4 in range(4):
                        ev_tt = tens([ev_sxld[jb], ev_pbfr[a4 % 2]],
                                     lambda jb_=jb, a_=a4, pbx=pbs[a4 % 2]: [
                                         eng.transpose(out=pbx[:, jj * 128:(jj + 1) * 128],
                                                       in_=sx_tm[jb_][:, a_, jj * 128:(jj + 1) * 128],
                                                       identity=identb[:]) for jj in range(8)][-1])
                        ev_tx = vec([ev_tt] + ([ev_sxT[jb]] if a4 == 0 else []),
                                    lambda jb_=jb, a_=a4, pbx=pbs[a4 % 2]: eng.tensor_copy(
                                        out=sxT[jb_][:, :, a_ * 128:(a_ + 1) * 128],
                                        in_=pbx[:].rearrange("p (k t) -> p k t", k=8)))
                        ev_pbfr[a4 % 2] = ev_tx
                    ev_sxT[jb] = ev_tx
                    for ih in range(2):
                        wb_ = (2 * j + ih) % 2
                        ev_gu_ld[wb_][0] = dma('DGW', (ev_guf[wb_] if isinstance(ev_guf[wb_], list) else [ev_guf[wb_]]) + [ev_posi],
                                               [(lambda j_=j, ih_=ih, wb2=wb_, k_=k2: eng.indirect_dma_start(
                                                   out=wgr[wb2][:, k_, :], out_offset=None,
                                                   in_=rtg_flat,
                                                   in_offset=bass.IndirectOffsetOnAxis(widxQ[:, j_:j_ + 1], 0),
                                                   element_offset=k_ * 128 * 2048 + ih_ * 1024))
                                                for k2 in range(NK)])
                        ev_gu_ld[wb_][1] = dma('DGW', [],
                                               [(lambda j_=j, ih_=ih, wb2=wb_, k_=k2: eng.indirect_dma_start(
                                                   out=wur[wb2][:, k_, :], out_offset=None,
                                                   in_=rtu_flat,
                                                   in_offset=bass.IndirectOffsetOnAxis(widxQ[:, j_:j_ + 1], 0),
                                                   element_offset=k_ * 128 * 2048 + ih_ * 1024))
                                                for k2 in range(NK)])
                        for m_loc in range(8):
                            m = ih * 8 + m_loc
                            gp2 = gi % 2
                            pg_, pu_ = ps[0 + 2 * gp2], ps[1 + 2 * gp2]
                            ev_mm = tens([ev_gu_ld[wb_][1], ev_sxT[jb],
                                          ev_act_r.get(gi - 2), ev_sil_r.get(gi - 2),
                                          ev_sh_sil if gi < 2 else None, ev_sh_act if gi < 2 else None],
                                         lambda m_=m_loc, wb2=wb_, jb_=jb, pg__=pg_, pu__=pu_: (
                                             [eng.matmul(out=pg__[:],
                                                         lhsT=wgr[wb2][:, k, m_ * 128:m_ * 128 + 128],
                                                         rhs=sxT[jb_][:, k, :],
                                                         start=(k == 0), stop=(k == NK - 1)) for k in range(NK)],
                                             [eng.matmul(out=pu__[:],
                                                         lhsT=wur[wb2][:, k, m_ * 128:m_ * 128 + 128],
                                                         rhs=sxT[jb_][:, k, :],
                                                         start=(k == 0), stop=(k == NK - 1)) for k in range(NK)])[-1][-1])
                            ev_gu_r[gi] = ev_mm
                            ev_sil2 = sca([ev_mm, ev_act_r.get(gi - 2)],
                                          lambda gp_=gp2, pg__=pg_: eng.activation(
                                              out=sgb_r[gp_][:], in_=pg__[:], func=AF.Silu))
                            ev_sil_r[gi] = ev_sil2
                            ev_act2 = vec([ev_sil2, ev_dn_r.get(di - 1) if m == 0 and j > 0 else None,
                                           ev_sh_stt if gi < 2 else None],
                                          lambda m__=m, gp_=gp2, pu__=pu_: eng.tensor_tensor(
                                              out=actT_r[:, m__, :], in0=sgb_r[gp_][:], in1=pu__[:],
                                              op=ALU.mult))
                            ev_act_r[gi] = ev_act2
                            gi += 1
                        ev_guf[wb_] = ev_gu_r[gi - 1]
                    ev_all_act2 = ev_act_r[gi - 1]
                    ev_wd_ldr = dma('DGD', (ev_wdf if isinstance(ev_wdf, list) else [ev_wdf]) + [ev_posi],
                                    [(lambda j_=j, m_=m2: eng.indirect_dma_start(
                                        out=wdr[:, m_, :], out_offset=None,
                                        in_=rtd_flat,
                                        in_offset=bass.IndirectOffsetOnAxis(widxD[:, j_:j_ + 1], 0),
                                        element_offset=m_ * 128 * 1024))
                                     for m2 in range(NM)])
                    for ct in range(4):
                        for hh in range(2):
                            dp2 = di % 2
                            pd_ = (ps[4], ps[6])[dp2]
                            ev_dmm = tens([ev_wd_ldr, ev_all_act2, ev_ev_r.get(di - 2),
                                           ev_sh_stt if di < 2 else None],
                                          lambda ct_=ct, hh_=hh, pd__=pd_: [
                                              eng.matmul(out=pd__[:], lhsT=actT_r[:, m, ct_ * 128:ct_ * 128 + 128],
                                                         rhs=wdr[:, m, hh_ * 512:hh_ * 512 + 512],
                                                         start=(m == 0), stop=(m == NM - 1))
                                              for m in range(NM)][-1])
                            ev_dn_r[di] = ev_dmm
                            ev_ev = vec([ev_dmm, ev_st_r.get(di - 8), ev_sh_stt if di < 8 else None,
                                         ev_sh_st if di < 8 else None],
                                        lambda ct_=ct, hh_=hh, pd__=pd_: eng.tensor_copy(
                                            out=dstg[:, ct_, hh_ * 512:hh_ * 512 + 512], in_=pd__[:]))
                            ev_ev_r[di] = ev_ev
                            ev_st_r[di] = ev_ev
                            di += 1
                    ev_wdf = ev_dn_r[di - 1]
                    ev_sxfree[jb] = ev_gu_r[gi - 1]
                    ev_sost = dma('DST', [ev_ev_r[di - 1]],
                                  lambda j_=j: eng.dma_start(
                                      out=sout_scr[512 * j_:512 * (j_ + 1), :].rearrange("(a p) d -> p a d", p=128),
                                      in_=dstg[:]))
                    for dd in range(di - 8, di):
                        ev_st_r[dd] = ev_sost
                if DEBUG_SPARSE:
                    dma('DST', [ev_posi], lambda: eng.dma_start(out=dbg1, in_=pos_s[:]))
                    dma('DST', [ev_posi], lambda: eng.dma_start(out=dbg2, in_=rows_s[0:1, :]))
                if DEBUG_SPARSE and do_scatter:
                    ev_d3l = dma('DSB', [ev_scatter_done, ev_sh_st],
                                 lambda: eng.dma_start(out=sx_tm[0][:],
                                                       in_=sorted_scr[0:512, :].rearrange("(a p) d -> p a d", p=128)))
                    ev_d3c = vec([ev_d3l], lambda: eng.tensor_copy(out=dstg[:], in_=sx_tm[0][:]))
                    dma('DST', [ev_d3c], lambda: eng.dma_start(
                        out=dbg3.rearrange("(a p) d -> p a d", p=128), in_=dstg[:]))
                evB_done = ("DST", cnt["DST"])
                ev_ga = None
                ev_add = None
                def _ret_tile(rp_):
                    if rp_ == 0:
                        return dstg[:]
                    return actT_r[:].rearrange("p m t -> p (m t)").bitcast(F32).rearrange(
                        "p (a d) -> p a d", a=4)

                ev_adds = [None, None]
                for q4 in range(NCH // 4 if do_return else 0):
                    rp = q4 % 2
                    ev_ga = None
                    for qq in range(4):
                        cc3 = q4 * 4 + qq
                        ev_ga = dma('DGS', [evB_done, ev_posi] + ([ev_adds[rp]] if qq == 0 else []),
                                    lambda c_=cc3, qq_=qq, rp_=rp: eng.indirect_dma_start(
                                        out=_ret_tile(rp_)[:, qq_, :],
                                        out_offset=None, in_=sout_scr,
                                        in_offset=bass.IndirectOffsetOnAxis(posi_s[:, c_:c_ + 1], 0)))
                    b3, c3 = (q4 * 4) // (SEQ // CH), (q4 * 4) % (SEQ // CH)
                    ev_adds[rp] = dma('DGA', [ev_ga],
                                      lambda b_=b3, c_=c3, rp_=rp: eng.dma_start(
                                          out=out_d[b_, c_ * CH:(c_ + 4) * CH, :].rearrange(
                                              "(a p) d -> p a d", p=128),
                                          in_=_ret_tile(rp_),
                                          accum_op=ALU.add))

            if who == "sync":
                eng.wait_ge(sems["DST"], cnt["DST"])
                eng.wait_ge(sems["DGA"], cnt["DGA"])

        @block.sync
        def _(eng):
            emit("sync", eng)

        @block.gpsimd
        def _(eng):
            emit("gpsimd", eng)

        @block.vector
        def _(eng):
            emit("vector", eng)

        @block.scalar
        def _(eng):
            emit("scalar", eng)

        @block.tensor
        def _(eng):
            emit("tensor", eng)

    return nc


def core_inputs(inputs, core):
    c = make_consts()
    return dict(
        x=np.ascontiguousarray(inputs["L0_attn_x"][2 * core:2 * core + 2]),
        wq=inputs["L0_attn_Wq"], wk=inputs["L0_attn_Wk"], wv=inputs["L0_attn_Wv"],
        wo=inputs["L0_attn_Wo"], wgate=inputs["L0_attn_Wg"],
        bgate=np.ascontiguousarray(inputs["L0_attn_bg"].reshape(1, NH)),
        gate_w=inputs["L0_ffn_gate_w"], shg=inputs["L0_ffn_sh_gate"],
        shu=inputs["L0_ffn_sh_up"], shd=inputs["L0_ffn_sh_down"],
        rtg=inputs["L0_ffn_rt_gate"], rtu=inputs["L0_ffn_rt_up"], rtd=inputs["L0_ffn_rt_down"],
        **c,
    )


def kernel(**inputs):
    inputs = {k: np.asarray(v, dtype=np.float32) for k, v in inputs.items()}
    nc = build_nc()
    in_maps = [core_inputs(inputs, core) for core in range(8)]
    res = run_bass_kernel_spmd(nc, in_maps, core_ids=list(range(8)))
    return np.concatenate([r["out"] for r in res.results], axis=0)


# version 18 (final)
# speedup vs baseline: 1.0829x; 1.0075x over previous
import numpy as np
import ml_dtypes
import concourse.bass as bass
import concourse.mybir as mybir
from concourse.bass_utils import run_bass_kernel_spmd
from contextlib import ExitStack

F32 = mybir.dt.float32
BF16 = mybir.dt.bfloat16
U8 = mybir.dt.uint8
I32 = mybir.dt.int32
AF = mybir.ActivationFunctionType
ALU = mybir.AluOpType
AX = mybir.AxisListType

NB = 2
SEQ = 2048
D = 1024
NH = 16
DH = 64
NK = 8
CH = 128
NCH = NB * (SEQ // CH)
QSCALE = 0.125
EPS = 1e-5
NEGBIG = 1.0e9
I_FF = 2048
NM = 16
NE = 8
SWTOK = 1024
NSW = NB * SEQ // SWTOK

PHASE_A = True
PHASE_B = True
SPARSE = True
DEBUG_SPARSE = False
SPARSE_STAGE = 4
TN = 15


def make_consts():
    ident_bf = np.eye(128, dtype=ml_dtypes.bfloat16)
    ident_f32 = np.eye(128, dtype=np.float32)
    utri = np.triu(np.ones((128, 128), dtype=np.float32))
    onesk1 = np.ones((1, 128), dtype=np.float32)
    oh127 = np.zeros((128, 1), dtype=np.float32)
    oh127[127, 0] = 1.0
    iota8 = np.broadcast_to(np.arange(8, dtype=np.float32), (128, 8)).copy()
    big8 = iota8 - 256.0
    negtri = np.where(np.triu(np.ones((128, 128), dtype=bool)), 0.0, NEGBIG).astype(np.float32)
    utri_strict = np.triu(np.ones((128, 128), dtype=np.float32), 1)
    iota_row = np.arange(128, dtype=np.float32).reshape(1, 128)
    iota_col = np.arange(128, dtype=np.float32).reshape(128, 1)
    onescol = np.ones((128, 1), dtype=np.float32)
    actc = np.zeros((128, 8), dtype=np.float32)
    actc[:] = np.array([1e-5, 1.0, -1.0, -0.5, 0.125, -0.125, 1.0 / 1024.0, 0.0], dtype=np.float32)
    return dict(c_actconst=actc, c_utri_strict=utri_strict, c_iota_row=iota_row,
                c_iota_col=iota_col,c_ident_bf=ident_bf, c_ident_f32=ident_f32, c_utri=utri,
                c_onesk1=onesk1, c_oh127=oh127, c_iota8=iota8, c_big8=big8,
                c_negtri=negtri, c_onescol=onescol)


class Arena:
    def __init__(self, nc, name, nbytes):
        self.t = nc.alloc_sbuf_tensor(name, [128, nbytes], U8)
        self.off = 0
        self.marks = []

    def take(self, shape, dtype, name=""):
        esz = {F32: 4, BF16: 2, U8: 1}[dtype]
        n = int(np.prod(shape[1:])) * esz
        off = (self.off + 127) // 128 * 128
        self.off = off + n
        assert self.off <= self.t.shape[1], f"arena overflow at {name}: {self.off}"
        ap = self.t.ap()[:, off:off + n].bitcast(dtype)
        if len(shape) > 2:
            dims = " ".join(f"d{i}" for i in range(1, len(shape)))
            ap = ap.rearrange(f"p ({dims}) -> p {dims}",
                              **{f"d{i}": shape[i] for i in range(1, len(shape) - 1)})
        if shape[0] != 128:
            ap = ap[:shape[0]]
        return ap


def build_nc():
    nc = bass.Bass("TRN2", target_bir_lowering=False, debug=False)

    x_d = nc.dram_tensor("x", [NB, SEQ, D], F32, kind="ExternalInput").ap()
    wq_d = nc.dram_tensor("wq", [D, D], F32, kind="ExternalInput").ap()
    wk_d = nc.dram_tensor("wk", [D, D], F32, kind="ExternalInput").ap()
    wv_d = nc.dram_tensor("wv", [D, D], F32, kind="ExternalInput").ap()
    wo_d = nc.dram_tensor("wo", [D, D], F32, kind="ExternalInput").ap()
    wg_d = nc.dram_tensor("wgate", [NH, D], F32, kind="ExternalInput").ap()
    bg_d = nc.dram_tensor("bgate", [1, NH], F32, kind="ExternalInput").ap()
    gw_d = nc.dram_tensor("gate_w", [D, NE], F32, kind="ExternalInput").ap()
    shg_d = nc.dram_tensor("shg", [D, I_FF], F32, kind="ExternalInput").ap()
    shu_d = nc.dram_tensor("shu", [D, I_FF], F32, kind="ExternalInput").ap()
    shd_d = nc.dram_tensor("shd", [I_FF, D], F32, kind="ExternalInput").ap()
    rtg_d = nc.dram_tensor("rtg", [NE, D, I_FF], F32, kind="ExternalInput").ap()
    rtu_d = nc.dram_tensor("rtu", [NE, D, I_FF], F32, kind="ExternalInput").ap()
    rtd_d = nc.dram_tensor("rtd", [NE, I_FF, D], F32, kind="ExternalInput").ap()
    cvals = make_consts()
    cst = {k: nc.dram_tensor(k, list(v.shape), F32 if v.dtype == np.float32 else BF16,
                             kind="ExternalInput").ap() for k, v in cvals.items()}
    out_d = nc.dram_tensor("out", [NB, SEQ, D], F32, kind="ExternalOutput").ap()
    if DEBUG_SPARSE:
        dbg1 = nc.dram_tensor("dbg1", [128, 32], F32, kind="ExternalOutput").ap()
        dbg2 = nc.dram_tensor("dbg2", [1, 128], F32, kind="ExternalOutput").ap()
        dbg3 = nc.dram_tensor("dbg3", [512, D], F32, kind="ExternalOutput").ap()

    xsT_scr = nc.dram_tensor("xsT_scr", [NCH, 128, NK, CH], BF16, kind="Internal").ap()
    lnT_scr = nc.dram_tensor("lnT_scr", [NCH, 128, NK, CH], BF16, kind="Internal").ap()
    msk_scr = nc.dram_tensor("msk_scr", [NCH, 128, NE], F32, kind="Internal").ap()
    L_scr = nc.dram_tensor("L_scr", [NCH, NH, CH], F32, kind="Internal").ap()
    xs_scr = nc.dram_tensor("xs_scr", [NCH, 128, D], BF16, kind="Internal").ap()
    sorted_scr = nc.dram_tensor("sorted_scr", [TN * 512, D], BF16, kind="Internal").ap()
    sout_scr = nc.dram_tensor("sout_scr", [TN * 512, D], F32, kind="Internal").ap()
    srtA = nc.dram_tensor("srtA", [1, 256], F32, kind="Internal").ap()
    srtB = nc.dram_tensor("srtB", [32, 8], F32, kind="Internal").ap()
    srtC = nc.dram_tensor("srtC", [1, 128], F32, kind="Internal").ap()

    with ExitStack() as stk:
        e = stk.enter_context

        identb = nc.alloc_sbuf_tensor("identb", [128, 128], BF16)
        identf = nc.alloc_sbuf_tensor("identf", [128, 128], F32)
        utri = nc.alloc_sbuf_tensor("utri", [128, 128], F32)
        onesk1 = nc.alloc_sbuf_tensor("onesk1", [1, 128], F32)
        oh127 = nc.alloc_sbuf_tensor("oh127", [128, 1], F32)
        iota8 = nc.alloc_sbuf_tensor("iota8", [128, 8], F32)
        big8 = nc.alloc_sbuf_tensor("big8", [128, 8], F32)
        negtri = nc.alloc_sbuf_tensor("negtri", [128, 128], F32)
        onescol = nc.alloc_sbuf_tensor("onescol", [128, 1], F32)
        bgB = nc.alloc_sbuf_tensor("bgB", [128, NH], F32)
        bgrow = nc.alloc_sbuf_tensor("bgrow", [1, NH], F32)
        wgT = nc.alloc_sbuf_tensor("wgT", [128, NK, NH], BF16)
        gwS = nc.alloc_sbuf_tensor("gwS", [128, NK, NE], BF16)
        wgtmp = nc.alloc_sbuf_tensor("wgtmp", [16, D], BF16)
        actc = nc.alloc_sbuf_tensor("actc", [128, 8], F32)
        utriS = nc.alloc_sbuf_tensor("utriS", [128, 128], F32)
        iotar = nc.alloc_sbuf_tensor("iotar", [1, 128], F32)
        iotac = nc.alloc_sbuf_tensor("iotac", [128, 1], F32)
        mk_s = nc.alloc_sbuf_tensor("mk_s", [128, 32, 8], F32)
        rk_s = nc.alloc_sbuf_tensor("rk_s", [128, 32, 8], F32)
        srow = nc.alloc_sbuf_tensor("srow", [1, 256], F32)
        car_s = nc.alloc_sbuf_tensor("car_s", [32, 8], F32)
        rows_s = nc.alloc_sbuf_tensor("rows_s", [1, 128], F32)
        rowi_s = nc.alloc_sbuf_tensor("rowi_s", [1, 16], I32)
        carb_s = nc.alloc_sbuf_tensor("carb_s", [128, 32, 8], F32)
        rowb_s = nc.alloc_sbuf_tensor("rowb_s", [128, 128], F32)
        pos_s = nc.alloc_sbuf_tensor("pos_s", [128, 32], F32)
        posi_s = nc.alloc_sbuf_tensor("posi_s", [128, 32], I32)
        widxQ = nc.alloc_sbuf_tensor("widxQ", [128, 16], I32)
        widxD = nc.alloc_sbuf_tensor("widxD", [128, 16], I32)
        zt_s = nc.alloc_sbuf_tensor("zt_s", [128, 1024], BF16)
        for _i, _v in enumerate([1e-5, 1.0, -1.0, -0.5, 0.125, -0.125, 1.0 / 1024.0, 256.0]):
            nc.const_aps.register(F32, _v, actc[:, _i:_i + 1])

        OVB = 194 * 1024
        ar = Arena(nc, "ov", OVB)

        off0 = ar.off
        wqT = ar.take([128, NK, D], BF16, "wqT")
        wkT = ar.take([128, NK, D], BF16, "wkT")
        wvT = ar.take([128, NK, D], BF16, "wvT")
        woT = ar.take([128, NK, D], BF16, "woT")
        wtmp = ar.take([128, NK, D], BF16, "wtmp")
        x_in = [ar.take([128, D], F32, "x0"), ar.take([128, D], F32, "x1")]
        r1 = ar.take([128, D], F32, "r1")
        e1 = ar.take([128, D], F32, "e1")
        e1a = ar.take([128, 512], F32, "e1a")
        def _pp2(shape, dt, nm):
            return [ar.take(shape, dt, nm + "0"), ar.take(shape, dt, nm + "1")]
        T_bc = ar.take([128, NH * CH], F32, "T_bc")
        LTs = ar.take([16, 128], F32, "LTs")
        LCrow = ar.take([1, NH], F32, "LCrow")
        ACrow = ar.take([1, NH], F32, "ACrow")
        def _pp(shape, dt, nm):
            return [ar.take(shape, dt, nm + "0"), ar.take(shape, dt, nm + "1")]
        ln1 = _pp([128, D], BF16, "ln1")
        ln1T = _pp([128, NK, CH], BF16, "ln1T")
        qf0 = ar.take([128, D], BF16, "qf")
        qs0 = ar.take([128, D], BF16, "qs")
        kf0 = ar.take([128, D], BF16, "kf")
        qf, qs, kf = [qf0, qf0], [qs0, qs0], [kf0, kf0]
        dmp = wtmp[:, 0, :].bitcast(BF16) if False else wtmp.rearrange("p k d -> p (k d)")[:, 0:D]
        kh = _pp([128, NH, DH], BF16, "kh")
        vv = _pp([128, NH, DH], BF16, "vv")
        qfT = _pp([128, NK, CH], BF16, "qfT")
        qsT = _pp([128, NK, CH], BF16, "qsT")
        kfT = _pp([128, NK, CH], BF16, "kfT")
        zb = _pp([128, NH], F32, "zb")
        za = _pp([128, NH], F32, "za")
        zl = _pp([128, NH], F32, "zl")
        zm = _pp([128, NH], F32, "zm")
        lcol = _pp([128, NH], F32, "lcol")
        Lm = _pp([128, NH], F32, "Lm")
        Ecol = _pp([128, NH], F32, "Ecol")
        Acol = _pp([128, NH], F32, "Acol")
        ACcol = _pp([128, NH], F32, "ACcol")
        Dbuf = _pp([128, NH, CH], BF16, "Dbuf")
        Pt = _pp([128, NH, CH], BF16, "Pt")
        yT16 = _pp([128, NK, CH], BF16, "yT16")
        Sf = _pp([128, NK, DH], F32, "Sf")
        S16 = _pp([128, NK, DH], BF16, "S16")
        x_new = _pp([128, D], F32, "x_new")
        ln2 = _pp([128, D], BF16, "ln2")
        ln2T = _pp([128, NK, CH], BF16, "ln2T")
        xs = _pp([128, D], BF16, "xs")
        eqm = _pp([128, NE], F32, "eqm")
        cand = _pp([128, NE], F32, "cand")
        masks8 = _pp([128, NE], F32, "masks8")
        scol = _pp([128, 24], F32, "scol")
        A_BYTES = ar.off
        ar.off = off0

        (SC_SUM, SC_M, SC_M2, SC_VAR, SC_LNV, SC_RSTD, SC_SQ, SC_MAX,
         SC_ESC, SC_DEN, SC_SCR, SC_SR2, SC_SEL, SC_NMR, SC_NR2,
         SC_M1, SC_M21, SC_VAR1, SC_LNV1, SC_RSTD1, SC_SQ1, SC_NMR1,
         SC_NR1) = range(23)

        xsT_s = ar.take([128, NK, SWTOK], BF16, "xsT_s")
        lnT_s = ar.take([128, NK, SWTOK], BF16, "lnT_s")
        acc = ar.take([128, SWTOK // 128, D], F32, "acc")
        actT = ar.take([128, NM, SWTOK], BF16, "actT")
        wgu = [[ar.take([128, NK, 256], BF16, f"wg{i}"), ar.take([128, NK, 256], BF16, f"wu{i}")]
               for i in range(2)]
        wdb = [ar.take([128, NM, 512], BF16, f"wd{i}") for i in range(2)]
        sgb = [ar.take([128, 512], F32, f"sg{i}") for i in range(2)]
        masks_s = ar.take([128, SWTOK // 128, NE], F32, "masks_s")
        B_BYTES = ar.off
        ar.off = off0
        sx_tm = [ar.take([128, 4, D], BF16, "sx0"), ar.take([128, 4, D], BF16, "sx1")]
        sxT = [ar.take([128, NK, 512], BF16, "sxT0"), ar.take([128, NK, 512], BF16, "sxT1")]
        wgr = [ar.take([128, NK, 1024], BF16, "wgr0"), ar.take([128, NK, 1024], BF16, "wgr1")]
        wur = [ar.take([128, NK, 1024], BF16, "wur0"), ar.take([128, NK, 1024], BF16, "wur1")]
        wdr = ar.take([128, NM, 1024], BF16, "wdr")
        actT_r = ar.take([128, NM, 512], BF16, "actT_r")
        dstg = ar.take([128, 4, D], F32, "dstg")
        rst_s = ar.take([128, D], F32, "rst_s")
        rst_s2 = ar.take([128, D], F32, "rst_s2")
        sgb_r = [ar.take([128, 512], F32, "sgr0"), ar.take([128, 512], F32, "sgr1")]
        C_BYTES = ar.off
        assert max(A_BYTES, B_BYTES, C_BYTES) <= OVB, (A_BYTES, B_BYTES, C_BYTES)

        ps = [nc.alloc_psum_tensor(f"ps{i}", [128, 512], F32) for i in range(3)] + [None, None, None]
        ps[4] = nc.alloc_psum_tensor("ps4", [128, 512], F32)
        ps.append(nc.alloc_psum_tensor("ps6", [128, 512], F32))
        small1 = nc.alloc_psum_tensor("small1", [128, 512], F32)
        ps[3] = small1
        pbs = [nc.alloc_psum_tensor("pbt0", [128, 1024], BF16),
               nc.alloc_psum_tensor("pbt1", [128, 1024], BF16)]
        SMALLs = [ps[6], small1]

        sems = {s: e(nc.semaphore(s)) for s in
                ("PE", "VE", "SE", "GP", "DSC", "DXL", "DST", "DSS", "DSB",
                 "DGP", "DGL", "DGW", "DGD", "DGS", "DGA")}
        block = e(nc.Block())

        def emit(who, eng):
            cnt = {k: 0 for k in sems}
            lastw = {}
            DMA_OWNER = {"DSC": "sync", "DXL": "sync", "DST": "sync", "DSS": "sync",
                         "DSB": "sync", "DGP": "gpsimd", "DGL": "gpsimd",
                         "DGW": "gpsimd", "DGD": "gpsimd", "DGS": "gpsimd", "DGA": "gpsimd"}

            def inc(instr, s_name, n=1):
                cnt[s_name] += n
                if instr is not None:
                    instr.then_inc(sems[s_name], n)
                return (s_name, cnt[s_name])

            def wait(ev):
                if ev is None:
                    return
                s_name, v = ev
                if v <= 0 or lastw.get(s_name, 0) >= v:
                    return
                lastw[s_name] = v
                eng.wait_ge(sems[s_name], v)

            def run(owner, s_name, deps, f, dn=1):
                if who == owner:
                    for d_ in deps:
                        wait(d_)
                    return inc(f(), s_name, dn)
                return inc(None, s_name, dn)

            def tens(deps, f):
                return run("tensor", "PE", deps, f)

            def vec(deps, f):
                return run("vector", "VE", deps, f)

            def sca(deps, f):
                return run("scalar", "SE", deps, f)

            def gp(deps, f):
                return run("gpsimd", "GP", deps, f)

            def dma(sem, deps, fs):
                if not isinstance(fs, (list, tuple)):
                    fs = [fs]
                if who == DMA_OWNER[sem]:
                    wait((sem, cnt[sem]))
                    for d_ in deps:
                        wait(d_)
                    for f_ in fs:
                        inc(f_(), sem, 16)
                else:
                    for f_ in fs:
                        inc(None, sem, 16)
                return (sem, cnt[sem])

            ev_c = []
            for tile, src in ((identb, cst["c_ident_bf"]), (identf, cst["c_ident_f32"]),
                              (utri, cst["c_utri"]), (onesk1, cst["c_onesk1"]),
                              (oh127, cst["c_oh127"]), (iota8, cst["c_iota8"]),
                              (big8, cst["c_big8"]), (negtri, cst["c_negtri"]),
                              (onescol, cst["c_onescol"]), (actc, cst["c_actconst"]),
                              (utriS, cst["c_utri_strict"]), (iotar, cst["c_iota_row"]),
                              (iotac, cst["c_iota_col"]), (bgrow, bg_d)):
                ev_c.append(dma('DSC', [], lambda t_=tile, s_=src: eng.dma_start(out=t_[:], in_=s_)))
            ev_iota = ev_c[5]
            ev_big = ev_c[6]
            ev_cst_all = ev_c[-1]
            if who in ("scalar", "vector", "tensor"):
                wait(ev_cst_all)
            ev_gw = dma('DGP', [], lambda: eng.dma_start(out=gwS[:], in_=gw_d.rearrange("(k p) e -> p k e", p=128)))
            ev_wgl = dma('DGP', [], lambda: eng.dma_start(out=wgtmp[:], in_=wg_d))
            ev_bgb_mm = tens([ev_cst_all], lambda: eng.matmul(out=ps[0][:, 0:NH], lhsT=onesk1[:],
                                                              rhs=bgrow[:], start=True, stop=True))
            ev_bgb = vec([ev_bgb_mm], lambda: eng.tensor_copy(out=bgB[:], in_=ps[0][:, 0:NH]))

            wT_tiles = {"wq": wqT, "wk": wkT, "wv": wvT, "wo": woT}
            w_srcs = {"wq": wq_d, "wk": wk_d, "wv": wv_d, "wo": wo_d}
            ev_wT = {}
            ev_wtmp_free = None
            ev_pb_free = None
            for wn in ("wq", "wk", "wv", "wo"):
                ev_l = dma('DGP', [ev_wtmp_free] if ev_wtmp_free else [],
                             lambda wn_=wn: eng.dma_start(out=wtmp[:],
                                                          in_=w_srcs[wn_].rearrange("(nk p) d -> p nk d", p=128)))
                last_ev = None
                for g in range(8):
                    def tf(g_=g):
                        last = None
                        for dk in range(8):
                            last = eng.transpose(out=pbs[0][:, dk * 128:(dk + 1) * 128],
                                                 in_=wtmp[:, g_, dk * 128:(dk + 1) * 128],
                                                 identity=identb[:])
                        return last
                    ev_t = tens([ev_l, ev_c[0], ev_pb_free], tf)

                    def vf(wn_=wn, g_=g):
                        return eng.tensor_copy(
                            out=wT_tiles[wn_][:, :, g_ * 128:(g_ + 1) * 128],
                            in_=pbs[0][:].rearrange("p (k t) -> p k t", k=8))
                    ev_pb_free = vec([ev_t], vf)
                    last_ev = ev_pb_free
                ev_wT[wn] = last_ev
                ev_wtmp_free = last_ev
            ev_wgT_t = tens([ev_wgl, ev_pb_free],
                            lambda: [eng.transpose(out=pbs[0][:, dk * 16:(dk + 1) * 16],
                                                   in_=wgtmp[0:16, dk * 128:(dk + 1) * 128],
                                                   identity=identb[0:16, 0:16]) for dk in range(8)][-1])
            ev_wgT = vec([ev_wgT_t], lambda: eng.tensor_copy(
                out=wgT[:].rearrange("p k h -> p (k h)"), in_=pbs[0][:, 0:128]))
            ev_pb_free = ev_wgT

            ev_xfree = [None, None]
            pc = {}
            ev_Sready = [None, None]
            ev_small_free = [ev_bgb, ev_bgb]
            ev_pA = [ev_bgb, None]
            ev_dmp = [None]
            ev_y = None
            evd_scr = None
            ev_Lw_prev = None
            ev_pbf = [ev_pb_free, None]
            pbi = 0

            if PHASE_A:
                stH = {}
                exl_pend = [None, None]

                def chunk_h1a(ci):
                    nonlocal pbi, evd_scr, ev_Lw_prev, ev_y
                    st = {}
                    c, b = ci // 2, ci % 2
                    cc = b * (SEQ // CH) + c
                    par = ci % 2
                    x_ = x_in[par]
                    ln1_, ln1T_, qf_, qs_, kf_, kh_, vv_ = (ln1[par], ln1T[par], qf[par],
                                                            qs[par], kf[par], kh[par], vv[par])
                    qfT_, qsT_, kfT_ = qfT[par], qsT[par], kfT[par]
                    zb_, za_, zl_, zm_, lcol_, Lm_ = zb[par], za[par], zl[par], zm[par], lcol[par], Lm[par]
                    Ecol_, Acol_, ACcol_ = Ecol[par], Acol[par], ACcol[par]
                    Dbuf_, Pt_, yT16_ = Dbuf[par], Pt[par], yT16[par]
                    x_new_, ln2_, ln2T_, xs_ = x_new[par], ln2[par], ln2T[par], xs[par]
                    eqm_, cand_, masks8_, scol_ = eqm[par], cand[par], masks8[par], scol[par]
                    Sf_, S16_ = Sf[b], S16[b]
                    SMALL = SMALLs[par]
                    if c == 0:
                        ev_Sready[b] = vec([], lambda Sf__=Sf_, S16__=S16_: (
                            eng.memset(Sf__[:].rearrange("p k j -> p (k j)"), 0.0),
                            eng.memset(S16__[:].rearrange("p k j -> p (k j)"), 0.0))[-1])
                    exl = exl_pend[par]
                    ev_ms = sca([exl, pc.get(("scol1", par)), ev_dmp[0],
                                 ev_wT["wo"] if ci == 0 else None],
                                lambda x__=x_, s_=scol_: eng.activation(
                                    out=dmp[:], in_=x__[:], func=AF.Identity, scale=1.0 / D,
                                    accum_out=s_[:, SC_M1:SC_M1 + 1]))
                    ev_sq = sca([exl, ev_ms],
                                lambda x__=x_, s_=scol_: eng.activation(
                                    out=dmp[:], in_=x__[:], func=AF.Square, scale=D ** -0.5,
                                    accum_out=s_[:, SC_SQ1:SC_SQ1 + 1]))
                    ev_dmp[0] = ev_sq
                    ev_m2 = sca([ev_ms], lambda s_=scol_: eng.activation(
                        out=s_[:, SC_M21:SC_M21 + 1], in_=s_[:, SC_M1:SC_M1 + 1], func=AF.Square))
                    ev_var = gp([ev_sq, ev_m2], lambda s_=scol_: eng.tensor_tensor(
                        out=s_[:, SC_VAR1:SC_VAR1 + 1], in0=s_[:, SC_SQ1:SC_SQ1 + 1],
                        in1=s_[:, SC_M21:SC_M21 + 1], op=ALU.subtract))
                    ev_lnv = sca([ev_var], lambda s_=scol_: eng.activation(
                        out=s_[:, SC_LNV1:SC_LNV1 + 1], in_=s_[:, SC_VAR1:SC_VAR1 + 1],
                        func=AF.Ln, bias=EPS))
                    ev_rstd = sca([ev_lnv], lambda s_=scol_: eng.activation(
                        out=s_[:, SC_RSTD1:SC_RSTD1 + 1], in_=s_[:, SC_LNV1:SC_LNV1 + 1],
                        func=AF.Exp, scale=-0.5))
                    ev_nr = sca([ev_rstd], lambda s_=scol_: eng.activation(
                        out=s_[:, SC_NR1:SC_NR1 + 1], in_=s_[:, SC_RSTD1:SC_RSTD1 + 1],
                        func=AF.Identity, scale=-1.0))
                    ev_nmr = sca([ev_nr], lambda s_=scol_: eng.activation(
                        out=s_[:, SC_NMR1:SC_NMR1 + 1], in_=s_[:, SC_M1:SC_M1 + 1],
                        func=AF.Identity, scale=s_[:, SC_NR1:SC_NR1 + 1]))
                    ev_ln1 = sca([ev_nmr, pc.get(("ln1", par))],
                                 lambda x__=x_, l_=ln1_, s_=scol_: eng.activation(
                                     out=l_[:], in_=x__[:], func=AF.Identity,
                                     scale=s_[:, SC_RSTD1:SC_RSTD1 + 1],
                                     bias=s_[:, SC_NMR1:SC_NMR1 + 1]))
                    pc[("scol1", par)] = ev_ln1
                    ev_xcp = gp([exl, pc.get(("x_new", par)), pc.get(("x_new_dma", par)), pc.get(("ln2", par))],
                                lambda x__=x_, xn=x_new[par]: eng.tensor_copy(out=xn[:], in_=x__[:]))
                    st["ev_xcp"] = ev_xcp
                    ev_xfree[par] = [ev_xcp, ev_ln1, ev_sq, ev_ms]
                    if ci + 1 < NCH:
                        b2, c2, p2 = (ci + 1) % 2, (ci + 1) // 2, (ci + 1) % 2
                        exl_pend[p2] = dma('DXL', list(ev_xfree[p2] or []),
                                           lambda b_=b2, c_=c2, p_=p2: eng.dma_start(
                                               out=x_in[p_][:], in_=x_d[b_, c_ * CH:(c_ + 1) * CH, :]))
                    ev_t = tens([ev_ln1, ev_pbf[pbi]],
                                lambda l_=ln1_, pbx=pbs[pbi]: [eng.transpose(
                                    out=pbx[:, j * 128:(j + 1) * 128], in_=l_[:, j * 128:(j + 1) * 128],
                                    identity=identb[:]) for j in range(8)][-1])
                    ev_ln1T = vec([ev_t, pc.get(("ln1T", par))],
                                  lambda d_=ln1T_, pbx=pbs[pbi]: eng.tensor_copy(
                                      out=d_[:].rearrange("p k t -> p (k t)"), in_=pbx[:]))
                    ev_pbf[pbi] = ev_ln1T
                    pbi ^= 1

                    ev_zmm = tens([ev_ln1T, ev_small_free[par], ev_wgT],
                                  lambda l_=ln1T_: [eng.matmul(out=SMALL[:, 0:NH], lhsT=l_[:, k, :],
                                                               rhs=wgT[:, k, :],
                                                               start=(k == 0), stop=(k == NK - 1))
                                                    for k in range(NK)][-1])
                    ev_zb = vec([ev_zmm, ev_bgb, pc.get(("zb", par)), pc.get(("za", par))],
                                lambda z_=zb_: eng.tensor_tensor(out=z_[:], in0=SMALL[:, 0:NH],
                                                                 in1=bgB[:], op=ALU.add))
                    ev_za = sca([ev_zb, pc.get(("za", par))],
                                lambda z_=zb_, a_=za_: eng.activation(out=a_[:], in_=z_[:], func=AF.Abs))
                    ev_ze = sca([ev_za], lambda a_=za_: eng.activation(out=a_[:], in_=a_[:], func=AF.Exp, scale=-1.0))
                    ev_zl = sca([ev_ze], lambda a_=za_, l_=zl_: eng.activation(out=l_[:], in_=a_[:],
                                                                               func=AF.Ln, bias=1.0))
                    ev_zm = vec([ev_zb, pc.get(("zm", par))],
                                lambda z_=zb_, m_=zm_: eng.tensor_scalar(out=m_[:], in0=z_[:], scalar1=0.0,
                                                                         scalar2=None, op0=ALU.min))
                    ev_l = vec([ev_zm, ev_zl, pc.get(("lcol", par))],
                               lambda m_=zm_, l_=zl_, o_=lcol_: eng.tensor_tensor(out=o_[:], in0=m_[:],
                                                                                  in1=l_[:], op=ALU.subtract))
                    ev_Lmm = tens([ev_l, ev_zb, ev_c[2]],
                                  lambda o_=lcol_: eng.matmul(out=SMALL[:, 16:16 + NH], lhsT=utri[:],
                                                              rhs=o_[:], start=True, stop=True))
                    ev_L = vec([ev_Lmm, pc.get(("Lm", par))],
                               lambda L__=Lm_: eng.tensor_copy(out=L__[:], in_=SMALL[:, 16:16 + NH]))
                    ev_LCmm = tens([ev_L], lambda L__=Lm_: eng.matmul(out=SMALL[0:1, 32:32 + NH], lhsT=oh127[:],
                                                                      rhs=L__[:], start=True, stop=True))
                    ev_LC = vec([ev_LCmm, pc.get(("LCrow", 0))],
                                lambda: eng.tensor_copy(out=LCrow[:], in_=SMALL[0:1, 32:32 + NH]))
                    ev_LCb = tens([ev_LC], lambda: eng.matmul(out=SMALL[:, 48:48 + NH], lhsT=onesk1[:],
                                                              rhs=LCrow[:], start=True, stop=True))
                    ev_Ep = vec([ev_LCb, pc.get(("Ecol", par))],
                                lambda L__=Lm_, E_=Ecol_: eng.tensor_tensor(
                                    out=E_[:], in0=SMALL[:, 48:48 + NH], in1=L__[:], op=ALU.subtract))
                    ev_E = sca([ev_Ep], lambda E_=Ecol_: eng.activation(out=E_[:], in_=E_[:], func=AF.Exp))
                    ev_A = sca([ev_L, pc.get(("Acol", par))],
                               lambda L__=Lm_, A__=Acol_: eng.activation(out=A__[:], in_=L__[:], func=AF.Exp))
                    ev_ACr = sca([ev_LC], lambda: eng.activation(out=ACrow[:], in_=LCrow[:], func=AF.Exp))
                    ev_ACb = tens([ev_ACr, ev_Ep], lambda: eng.matmul(out=SMALL[:, 64:64 + NH], lhsT=onesk1[:],
                                                                      rhs=ACrow[:], start=True, stop=True))
                    ev_AC = vec([ev_ACb, pc.get(("ACcol", par))],
                                lambda A__=ACcol_: eng.tensor_copy(out=A__[:], in_=SMALL[:, 64:64 + NH]))
                    ev_LTt = tens([ev_L, ev_AC, ev_c[1]],
                                  lambda L__=Lm_: eng.transpose(out=SMALL[:, 80:208][0:16], in_=L__[:],
                                                                identity=identf[:]))
                    ev_LTs = vec([ev_LTt, ev_Lw_prev],
                                 lambda: eng.tensor_copy(out=LTs[:], in_=SMALL[0:16, 80:208]))
                    ev_small_free[par] = ev_LTs
                    ev_Lw = dma('DGL', [ev_LTs], lambda cc_=cc: eng.dma_start(out=L_scr[cc_], in_=LTs[:]))
                    ev_Lw_prev = ev_Lw
                    ev_Lr = dma('DGL', [evd_scr],
                                lambda cc_=cc: eng.dma_start(
                                    out=T_bc[:],
                                    in_=L_scr[cc_].rearrange("h t -> (h t)").unsqueeze(0).broadcast_to([128, NH * CH])))
                    ev_d1 = gp([ev_Lr, ev_L],
                               lambda L__=Lm_: eng.tensor_tensor(
                                   out=T_bc[:].rearrange("p (h t) -> p h t", h=NH),
                                   in0=T_bc[:].rearrange("p (h t) -> p h t", h=NH),
                                   in1=L__[:].unsqueeze(2).broadcast_to([128, NH, 128]),
                                   op=ALU.subtract))
                    ev_d2 = gp([ev_d1, ev_c[7]], lambda: eng.tensor_tensor(
                        out=T_bc[:].rearrange("p (h t) -> p h t", h=NH),
                        in0=T_bc[:].rearrange("p (h t) -> p h t", h=NH),
                        in1=negtri[:].unsqueeze(1).broadcast_to([128, NH, 128]), op=ALU.subtract))
                    ev_Dx = sca([ev_d2, pc.get(("Dbuf", par))],
                                lambda D__=Dbuf_: eng.activation(
                                    out=D__[:].rearrange("p h t -> p (h t)"),
                                    in_=T_bc[:], func=AF.Exp))
                    ev_D_piece = [ev_Dx, ev_Dx, ev_Dx, ev_Dx]
                    evd_scr = ev_Dx

                    st.update(dict(exl=exl, ev_ln1T=ev_ln1T, ev_E=ev_E, ev_A=ev_A,
                                   ev_AC=ev_AC, ev_sq=ev_sq, ev_t=ev_t, ev_l=ev_l, ev_zl=ev_zl,
                                   ev_Lmm=ev_Lmm, ev_ACr=ev_ACr, ev_rstd=ev_rstd,
                                   ev_D_piece=ev_D_piece))
                    return st

                def chunk_h1b(ci, st):
                    nonlocal pbi, evd_scr, ev_Lw_prev, ev_y
                    c, b = ci // 2, ci % 2
                    cc = b * (SEQ // CH) + c
                    par = ci % 2
                    x_ = x_in[par]
                    ln1_, ln1T_, qf_, qs_, kf_, kh_, vv_ = (ln1[par], ln1T[par], qf[par],
                                                            qs[par], kf[par], kh[par], vv[par])
                    qfT_, qsT_, kfT_ = qfT[par], qsT[par], kfT[par]
                    Ecol_, Acol_ = Ecol[par], Acol[par]
                    ev_ln1T = st["ev_ln1T"]
                    ev_E = st["ev_E"]
                    ev_A = st["ev_A"]
                    ev_sq = st["ev_sq"]
                    def proj(wT_t, ev_w, scale_pm, out_f, out_scaled3, scale_ap, ev_dep_scale,
                             war_r1, war_f, war_scaled):
                        ev_half = [None, None]
                        for hf in range(2):
                            bank = ps[hf]
                            ev_mm = tens([ev_ln1T, ev_w, ev_pA[hf]],
                                         lambda hf_=hf, bank_=bank, w_=wT_t, l_=ln1T_: [
                                             eng.matmul(out=bank_[:], lhsT=l_[:, k, :],
                                                        rhs=w_[:, k, hf_ * 512:(hf_ + 1) * 512],
                                                        start=(k == 0), stop=(k == NK - 1)) for k in range(NK)][-1])
                            ev_r = sca([ev_mm] + ([war_r1] if hf == 0 else []),
                                       lambda hf_=hf, bank_=bank: eng.activation(
                                           out=r1[:, hf_ * 512:(hf_ + 1) * 512], in_=bank_[:],
                                           func=AF.Relu, scale=scale_pm))
                            ev_n = sca([ev_r], lambda bank_=bank: eng.activation(out=e1a[:], in_=bank_[:],
                                                                                 func=AF.Relu, scale=-scale_pm))
                            ev_e = sca([ev_n], lambda hf_=hf: eng.activation(
                                out=e1[:, hf_ * 512:(hf_ + 1) * 512], in_=e1a[:], func=AF.Exp, scale=-1.0))
                            ev_pA[hf] = ev_e
                            ev_half[hf] = ev_e
                        ev_f = gp([ev_half[0], ev_half[1], war_f],
                                  lambda o_=out_f: eng.tensor_tensor(out=o_[:], in0=r1[:], in1=e1[:], op=ALU.add))
                        ev_s2 = gp([ev_f, ev_dep_scale, war_scaled],
                                   lambda o_=out_f, os_=out_scaled3, sap=scale_ap: eng.tensor_tensor(
                                       out=os_, in0=o_[:].rearrange("p (h j) -> p h j", h=NH),
                                       in1=sap, op=ALU.mult))
                        return ev_f, ev_s2

                    A_ap = Acol_[:].unsqueeze(2).broadcast_to([128, NH, DH])
                    E_ap = Ecol_[:].unsqueeze(2).broadcast_to([128, NH, DH])
                    ev_qf, ev_qs = proj(wqT, ev_wT["wq"], QSCALE, qf_,
                                        qs_[:].rearrange("p (h j) -> p h j", h=NH), A_ap, ev_A,
                                        pc.get(("r1", 1 - par)), pc.get(("qf", 1 - par)), pc.get(("qs", 1 - par)))
                    ev_kf, ev_kh = proj(wkT, ev_wT["wk"], 1.0, kf_, kh_[:], E_ap, ev_E,
                                        ev_qf, pc.get(("kf", par)), pc.get(("kh", par)))
                    ev_v = None
                    for hf in range(2):
                        bank = ps[hf]
                        ev_mm = tens([ev_ln1T, ev_wT["wv"], ev_pA[hf]],
                                     lambda hf_=hf, bank_=bank, l_=ln1T_: [
                                         eng.matmul(out=bank_[:], lhsT=l_[:, k, :],
                                                    rhs=wvT[:, k, hf_ * 512:(hf_ + 1) * 512],
                                                    start=(k == 0), stop=(k == NK - 1)) for k in range(NK)][-1])
                        ev_v = vec([ev_mm] + ([pc.get(("vv", par))] if hf == 0 else []),
                                   lambda hf_=hf, bank_=bank, v_=vv_: eng.tensor_copy(
                                       out=v_[:].rearrange("p h j -> p (h j)")[:, hf_ * 512:(hf_ + 1) * 512],
                                       in_=bank_[:]))
                        ev_pA[hf] = ev_v
                    ev_tr = {}
                    for src_t, dst, evp, nm in ((qf_, qfT_, ev_qf, "qfT"), (qs_, qsT_, ev_qs, "qsT"),
                                                (kf_, kfT_, ev_kf, "kfT")):
                        ev_t2 = tens([evp, ev_pbf[pbi]],
                                     lambda s_=src_t, pbx=pbs[pbi]: [eng.transpose(
                                         out=pbx[:, j * 128:(j + 1) * 128],
                                         in_=s_[:, j * 128:(j + 1) * 128],
                                         identity=identb[:]) for j in range(8)][-1])
                        ev_pbf[pbi] = vec([ev_t2, pc.get((nm, par))],
                                          lambda d_=dst, pbx=pbs[pbi]: eng.tensor_copy(
                                              out=d_[:].rearrange("p k t -> p (k t)"), in_=pbx[:]))
                        ev_tr[nm] = ev_pbf[pbi]
                        pbi ^= 1
                    st.update(dict(ev_tr=ev_tr, ev_v=ev_v, ev_kh=ev_kh,
                                   ev_qf=ev_qf, ev_qs=ev_qs, ev_kf=ev_kf))
                    return st

                    st.update(dict(ev_tr=ev_tr, ev_v=ev_v, ev_kh=ev_kh,
                                   ev_qf=ev_qf, ev_qs=ev_qs, ev_kf=ev_kf))
                    return st

                def chunk_h2a(ci, st):
                    nonlocal pbi, evd_scr, ev_Lw_prev, ev_y
                    c, b = ci // 2, ci % 2
                    cc = b * (SEQ // CH) + c
                    par = ci % 2
                    qfT_, qsT_, kfT_ = qfT[par], qsT[par], kfT[par]
                    kh_, vv_ = kh[par], vv[par]
                    Dbuf_, Pt_, yT16_ = Dbuf[par], Pt[par], yT16[par]
                    ACcol_ = ACcol[par]
                    Sf_, S16_ = Sf[b], S16[b]
                    SMALL = SMALLs[par]
                    ev_tr = st["ev_tr"]
                    ev_v = st["ev_v"]
                    ev_kh = st["ev_kh"]
                    ev_D_piece = st["ev_D_piece"]
                    ev_AC = st["ev_AC"]
                    ev_Pe_last = None
                    ev_Pmm_last = None
                    ev_Sd = None
                    ev_yT = None
                    ev_ymm_w = None
                    ev_Pe_hist = {}
                    for wv_i in range(2):
                        for hh in range(8):
                            h = wv_i * 8 + hh
                            bank = ps[2]
                            ev_Pmm = tens([ev_tr["kfT"], ev_tr["qfT"],
                                           ev_Pe_hist.get(h - 4, pc.get(("Pbank", par)) if h < 4 else None)],
                                          lambda h_=h, bank_=bank, kT=kfT_, qT=qfT_: eng.matmul(
                                              out=bank_[:, 128 * (h_ % 4):128 * (h_ % 4) + 128],
                                              lhsT=kT[64 * (h_ % 2):64 * (h_ % 2) + 64, h_ // 2, :],
                                              rhs=qT[64 * (h_ % 2):64 * (h_ % 2) + 64, h_ // 2, :],
                                              start=True, stop=True))
                            ev_Pmm_last = ev_Pmm
                            ev_Pe = vec([ev_Pmm, ev_D_piece[h // 4]] + ([pc.get(("Pt", par))] if h == 0 else []),
                                        lambda h_=h, bank_=bank, P_=Pt_, D__=Dbuf_: eng.tensor_tensor(
                                            out=P_[:, h_, :], in0=bank_[:, 128 * (h_ % 4):128 * (h_ % 4) + 128],
                                            in1=D__[:, h_, :], op=ALU.mult))
                            ev_Pe_last = ev_Pe
                            ev_Pe_hist[h] = ev_Pe

                            def ymm(h_=h, S16__=S16_, qT=qsT_, v_=vv_, P_=Pt_):
                                base_ = 64 * (h_ % 2)
                                kc_ = h_ // 2
                                yoff_ = 128 * ((h_ // 2) % 4)
                                o = ps[4][base_:base_ + 64, yoff_:yoff_ + 128]
                                eng.matmul(out=o, lhsT=S16__[base_:base_ + 64, kc_, :],
                                           rhs=qT[base_:base_ + 64, kc_, :], start=True, stop=False,
                                           tile_position=(base_, base_))
                                return eng.matmul(out=o, lhsT=v_[:, h_, :], rhs=P_[:, h_, :],
                                                  start=False, stop=True, tile_position=(0, base_))
                            ev_ymm_w = tens([ev_Pe, ev_tr["qsT"], ev_v, ev_Sready[b], ev_y], ymm)

                            def sdmm(h_=h, k_=kh_, v_=vv_, SM=SMALL):
                                base_ = 64 * (h_ % 2)
                                return eng.matmul(out=SM[base_:base_ + 64, 64 * (h_ // 2):64 * (h_ // 2) + 64],
                                                  lhsT=k_[:, h_, :], rhs=v_[:, h_, :], start=True, stop=True,
                                                  tile_position=(0, base_))
                            ev_Sd = tens([ev_kh, ev_v] + ([ev_small_free[par], ev_Sready[b]] if h == 0 else []), sdmm)
                        ev_yT = vec([ev_ymm_w, pc.get(("yT16", par)) if wv_i == 0 else None],
                                    lambda g_=wv_i, y_=yT16_: eng.tensor_copy(
                                        out=y_[:, 4 * g_:4 * g_ + 4, :].rearrange("p k t -> p (k t)"),
                                        in_=ps[4][:]))
                        ev_y = ev_yT
                    x_new_ = x_new[par]
                    ev_xcp = st["ev_xcp"]
                    ev_res = None
                    ev_omm_last = None
                    for hf in range(2):
                        bank = ps[hf]
                        ev_mm = tens([ev_yT, ev_wT["wo"], ev_pA[hf]],
                                     lambda hf_=hf, bank_=bank, y_=yT16_: [
                                         eng.matmul(out=bank_[:], lhsT=y_[:, k, :],
                                                    rhs=woT[:, k, hf_ * 512:(hf_ + 1) * 512],
                                                    start=(k == 0), stop=(k == NK - 1)) for k in range(NK)][-1])
                        ev_omm_last = ev_mm
                        ev_res = vec([ev_mm] + ([ev_xcp] if hf == 0 else []),
                                     lambda hf_=hf, bank_=bank, xn=x_new_: eng.tensor_tensor(
                                         out=xn[:, hf_ * 512:(hf_ + 1) * 512], in0=bank_[:],
                                         in1=xn[:, hf_ * 512:(hf_ + 1) * 512], op=ALU.add))
                        ev_pA[hf] = ev_res
                    ev_xn_st = dma('DST', [ev_res], lambda b_=b, c_=c, xn=x_new_: eng.dma_start(
                        out=out_d[b_, c_ * CH:(c_ + 1) * CH, :], in_=xn[:]))
                    st.update(dict(ev_Pe_last=ev_Pe_last, ev_Pmm_last=ev_Pmm_last,
                                   ev_Sd=ev_Sd, ev_yT=ev_yT, ev_ymm_w=ev_ymm_w,
                                   ev_res=ev_res, ev_omm_last=ev_omm_last, ev_xn_st=ev_xn_st))
                    return st

                def chunk_h2b(ci, st):
                    nonlocal pbi, evd_scr, ev_Lw_prev, ev_y
                    c, b = ci // 2, ci % 2
                    cc = b * (SEQ // CH) + c
                    par = ci % 2
                    x_ = x_in[par]
                    yT16_ = yT16[par]
                    x_new_, ln2_, ln2T_, xs_ = x_new[par], ln2[par], ln2T[par], xs[par]
                    eqm_, cand_, masks8_, scol_ = eqm[par], cand[par], masks8[par], scol[par]
                    Sf_, S16_ = Sf[b], S16[b]
                    ACcol_ = ACcol[par]
                    SMALL = SMALLs[par]
                    ev_AC = st["ev_AC"]
                    exl = st["exl"]
                    ev_tr = st["ev_tr"]
                    ev_v = st["ev_v"]
                    ev_kh = st["ev_kh"]
                    ev_qf = st["ev_qf"]
                    ev_qs = st["ev_qs"]
                    ev_kf = st["ev_kf"]
                    ev_t = st["ev_t"]
                    ev_l = st["ev_l"]
                    ev_zl = st["ev_zl"]
                    ev_Lmm = st["ev_Lmm"]
                    ev_A = st["ev_A"]
                    ev_ACr = st["ev_ACr"]
                    ev_rstd = st["ev_rstd"]
                    ev_sq = st["ev_sq"]
                    ev_xcp = st["ev_xcp"]
                    ev_Pe_last = st["ev_Pe_last"]
                    ev_Pmm_last = st["ev_Pmm_last"]
                    ev_Sd = st["ev_Sd"]
                    ev_yT = st["ev_yT"]
                    ev_ymm_w = st["ev_ymm_w"]
                    ev_Su = None
                    for h in range(NH):
                        ev_Su = vec([ev_Sd, ev_AC] + ([ev_yT] if h == 0 else []),
                                    lambda h_=h, Sf__=Sf_, A__=ACcol_, SM=SMALL: eng.scalar_tensor_tensor(
                                        out=Sf__[64 * (h_ % 2):64 * (h_ % 2) + 64, h_ // 2, :],
                                        in0=Sf__[64 * (h_ % 2):64 * (h_ % 2) + 64, h_ // 2, :],
                                        scalar=A__[64 * (h_ % 2):64 * (h_ % 2) + 64, h_:h_ + 1],
                                        in1=SM[64 * (h_ % 2):64 * (h_ % 2) + 64,
                                               64 * (h_ // 2):64 * (h_ // 2) + 64],
                                        op0=ALU.mult, op1=ALU.add))
                    ev_small_free[par] = ev_Su
                    ev_S16 = vec([ev_Su], lambda Sf__=Sf_, S16__=S16_: eng.tensor_copy(
                        out=S16__[:].rearrange("p k j -> p (k j)"),
                        in_=Sf__[:].rearrange("p k j -> p (k j)")))
                    ev_Sready[b] = ev_S16
                    ev_res = st["ev_res"]
                    ev_omm_last = st["ev_omm_last"]
                    ev_xn_st = st["ev_xn_st"]
                    ev_ms2 = sca([ev_res, ev_dmp[0], pc.get(("ln2sca", par))],
                                 lambda xn=x_new_, s_=scol_: eng.activation(
                                     out=dmp[:], in_=xn[:], func=AF.Identity, scale=1.0 / D,
                                     accum_out=s_[:, SC_M:SC_M + 1]))
                    ev_sq2 = sca([ev_res, ev_ms2], lambda xn=x_new_, s_=scol_: eng.activation(
                        out=dmp[:], in_=xn[:], func=AF.Square, scale=D ** -0.5,
                        accum_out=s_[:, SC_SQ:SC_SQ + 1]))
                    ev_dmp[0] = ev_sq2
                    ev_m22 = sca([ev_ms2], lambda s_=scol_: eng.activation(
                        out=s_[:, SC_M2:SC_M2 + 1], in_=s_[:, SC_M:SC_M + 1], func=AF.Square))
                    ev_var2 = gp([ev_sq2, ev_m22], lambda s_=scol_: eng.tensor_tensor(
                        out=s_[:, SC_VAR:SC_VAR + 1], in0=s_[:, SC_SQ:SC_SQ + 1],
                        in1=s_[:, SC_M2:SC_M2 + 1], op=ALU.subtract))
                    ev_lnv2 = sca([ev_var2], lambda s_=scol_: eng.activation(
                        out=s_[:, SC_LNV:SC_LNV + 1], in_=s_[:, SC_VAR:SC_VAR + 1], func=AF.Ln, bias=EPS))
                    ev_rstd2 = sca([ev_lnv2], lambda s_=scol_: eng.activation(
                        out=s_[:, SC_RSTD:SC_RSTD + 1], in_=s_[:, SC_LNV:SC_LNV + 1], func=AF.Exp, scale=-0.5))
                    ev_nr2 = sca([ev_rstd2], lambda s_=scol_: eng.activation(
                        out=s_[:, SC_NR2:SC_NR2 + 1], in_=s_[:, SC_RSTD:SC_RSTD + 1],
                        func=AF.Identity, scale=-1.0))
                    ev_nmr2 = sca([ev_nr2], lambda s_=scol_: eng.activation(
                        out=s_[:, SC_NMR:SC_NMR + 1], in_=s_[:, SC_M:SC_M + 1],
                        func=AF.Identity, scale=s_[:, SC_NR2:SC_NR2 + 1]))
                    ev_ln2 = sca([ev_nmr2, pc.get(("ln2", par))],
                                 lambda xn=x_new_, l_=ln2_, s_=scol_: eng.activation(
                                     out=l_[:], in_=xn[:], func=AF.Identity,
                                     scale=s_[:, SC_RSTD:SC_RSTD + 1], bias=s_[:, SC_NMR:SC_NMR + 1]))
                    ev_t3 = tens([ev_ln2, ev_pbf[pbi]],
                                 lambda l_=ln2_, pbx=pbs[pbi]: [eng.transpose(
                                     out=pbx[:, j * 128:(j + 1) * 128], in_=l_[:, j * 128:(j + 1) * 128],
                                     identity=identb[:]) for j in range(8)][-1])
                    ev_ln2T = vec([ev_t3, pc.get(("ln2T", par)), pc.get(("ln2T_pe", par))],
                                  lambda d_=ln2T_, pbx=pbs[pbi]: eng.tensor_copy(
                                      out=d_[:].rearrange("p k t -> p (k t)"), in_=pbx[:]))
                    ev_pbf[pbi] = ev_ln2T
                    pbi ^= 1
                    ev_lg = tens([ev_ln2T, ev_gw, ev_small_free[par]],
                                 lambda l_=ln2T_: [eng.matmul(out=SMALL[:, 0:NE], lhsT=l_[:, k, :],
                                                              rhs=gwS[:, k, :],
                                                              start=(k == 0), stop=(k == NK - 1))
                                                   for k in range(NK)][-1])
                    ev_mx = vec([ev_lg], lambda s_=scol_: eng.reduce_max(
                        out=s_[:, SC_MAX:SC_MAX + 1], in_=SMALL[:, 0:NE], axis=AX.X))
                    ev_eq = vec([ev_mx, pc.get(("eqm", par))],
                                lambda e_=eqm_, s_=scol_: eng.tensor_scalar(
                                    out=e_[:], in0=SMALL[:, 0:NE], scalar1=s_[:, SC_MAX:SC_MAX + 1],
                                    scalar2=None, op0=ALU.is_ge))
                    ev_small_free[par] = ev_eq
                    ev_cd = vec([ev_eq, ev_big, pc.get(("cand", par))],
                                lambda e_=eqm_, c_=cand_: eng.tensor_tensor(out=c_[:], in0=e_[:],
                                                                            in1=big8[:], op=ALU.mult))
                    ev_cd2 = vec([ev_cd], lambda c_=cand_: eng.tensor_scalar(
                        out=c_[:], in0=c_[:], scalar1=256.0, scalar2=None, op0=ALU.add))
                    ev_sel = vec([ev_cd2], lambda c_=cand_, s_=scol_: eng.tensor_reduce(
                        out=s_[:, SC_SEL:SC_SEL + 1], in_=c_[:], axis=AX.X, op=ALU.min))
                    ev_msk = vec([ev_sel, pc.get(("masks8", par))],
                                 lambda m_=masks8_, s_=scol_: eng.tensor_tensor(
                                     out=m_[:], in0=s_[:, SC_SEL:SC_SEL + 1].broadcast_to([128, NE]),
                                     in1=iota8[:], op=ALU.is_equal))
                    ev_esc = sca([ev_mx], lambda s_=scol_: eng.activation(
                        out=s_[:, SC_ESC:SC_ESC + 1], in_=s_[:, SC_MAX:SC_MAX + 1], func=AF.Exp, scale=-1.0))
                    ev_den = vec([ev_esc], lambda s_=scol_: eng.tensor_scalar(
                        out=s_[:, SC_DEN:SC_DEN + 1], in0=s_[:, SC_ESC:SC_ESC + 1],
                        scalar1=1.0, scalar2=None, op0=ALU.add))
                    ev_scr = vec([ev_den], lambda s_=scol_: eng.reciprocal(
                        out=s_[:, SC_SCR:SC_SCR + 1], in_=s_[:, SC_DEN:SC_DEN + 1]))
                    ev_sr2 = vec([ev_scr, ev_rstd2], lambda s_=scol_: eng.tensor_tensor(
                        out=s_[:, SC_SR2:SC_SR2 + 1], in0=s_[:, SC_SCR:SC_SCR + 1],
                        in1=s_[:, SC_RSTD:SC_RSTD + 1], op=ALU.mult))
                    ev_xs = vec([ev_sr2, pc.get(("xs", par))],
                                lambda xn=x_new_, x2=xs_, s_=scol_: eng.tensor_scalar(
                                    out=x2[:], in0=xn[:], scalar1=s_[:, SC_M:SC_M + 1],
                                    scalar2=s_[:, SC_SR2:SC_SR2 + 1], op0=ALU.subtract, op1=ALU.mult))
                    ev_st1 = dma('DSS', [ev_ln2T], lambda cc_=cc, d_=ln2T_: eng.dma_start(out=lnT_scr[cc_], in_=d_[:]))
                    ev_st2 = dma('DSS', [ev_xs], lambda cc_=cc, d_=xs_: eng.dma_start(out=xs_scr[cc_], in_=d_[:]))
                    ev_st3 = dma('DSS', [ev_msk], lambda cc_=cc, m_=masks8_: eng.dma_start(out=msk_scr[cc_], in_=m_[:]))

                    for k_, v_ in {
                        "scol": ev_xs, "scolg": ev_sq2, "sc_sca": ev_rstd2, "r1": ev_qf, "ln1": ev_t,
                        "ln1T": ev_v, "zb": ev_l, "za": ev_zl, "zm": ev_l, "lcol": ev_Lmm,
                        "Lm": ev_A, "LCrow": ev_ACr, "Ecol": ev_kh, "Acol": ev_qs, "ACcol": ev_Su,
                        "Dbuf": ev_Pe_last,
                        "qf": ev_tr["qfT"], "qs": ev_tr["qsT"], "kf": ev_tr["kfT"],
                        "kh": ev_Sd, "vv": ev_Sd, "Pt": ev_yT, "yT16": ev_omm_last,
                        "x_new": ev_xs, "x_new_dma": ev_xn_st, "ln2": ev_t3, "ln2T": ev_st1,
                        "ln2sca": ev_ln2,
                        "xs": ev_st2, "eqm": ev_cd, "cand": ev_msk,
                        "masks8": ev_st3, "qfT": ev_Pmm_last, "qsT": ev_ymm_w,
                        "kfT": ev_Pmm_last, "ln2T_pe": ev_lg, "Pbank": ev_Pe_last,
                    }.items():
                        pc[(k_, par)] = v_



                exl_pend[0] = dma('DXL', [], lambda: eng.dma_start(out=x_in[0][:], in_=x_d[0, 0:CH, :]))
                for ci in range(NCH):
                    stH[ci] = chunk_h1a(ci)
                    if ci >= 1:
                        chunk_h2a(ci - 1, stH[ci - 1])
                    chunk_h1b(ci, stH[ci])
                    if ci >= 1:
                        chunk_h2b(ci - 1, stH.pop(ci - 1))
                chunk_h2a(NCH - 1, stH[NCH - 1])
                chunk_h2b(NCH - 1, stH.pop(NCH - 1))

            evA_done = [(k, cnt[k]) for k in ("DST", "DSS", "DGL", "VE", "SE", "PE", "GP")]

            if PHASE_B and SPARSE:
                ev_mk = dma('DSB', evA_done, lambda: eng.dma_start(
                    out=mk_s[:], in_=msk_scr[:].rearrange("j p e -> p j e")))
                ev_rz = vec([], lambda: eng.memset(rows_s[0:1, :], 0.0))
                ev_rkmm = tens([ev_mk, ev_c[2]] + evA_done,
                               lambda: [eng.matmul(out=ps[2][:, j * 8:(j + 1) * 8], lhsT=utri[:],
                                                   rhs=mk_s[:, j, :], start=True, stop=True)
                                        for j in range(32)][-1])
                ev_rk = vec([ev_rkmm], lambda: eng.tensor_copy(
                    out=rk_s[:].rearrange("p j e -> p (j e)"), in_=ps[2][:, 0:256]))
                ev_totmm = tens([ev_rk], lambda: eng.matmul(
                    out=ps[3][0:1, 0:256], lhsT=oh127[:], rhs=rk_s[:].rearrange("p j e -> p (j e)"),
                    start=True, stop=True))
                ev_tot = vec([ev_totmm], lambda: eng.tensor_copy(out=srow[:], in_=ps[3][0:1, 0:256]))
                ev_totw = dma('DGL', [ev_tot], lambda: eng.dma_start(out=srtA, in_=srow[:]))
                ev_totr = dma('DGL', [ev_totw], lambda: eng.dma_start(
                    out=car_s[:], in_=srtA.rearrange("a (j e) -> (a j) e", e=8)))
                ev_carmm = tens([ev_totr], lambda: eng.matmul(
                    out=ps[3][0:32, 256:264], lhsT=utriS[0:32, 0:32], rhs=car_s[:],
                    start=True, stop=True))
                ev_cntmm = tens([ev_totr], lambda: eng.matmul(
                    out=ps[3][0:1, 264:272], lhsT=utri[0:32, 127:128], rhs=car_s[:],
                    start=True, stop=True))
                ev_car = vec([ev_carmm, ev_cntmm], lambda: eng.tensor_copy(out=car_s[:], in_=ps[3][0:32, 256:264]))
                ev_cnt = vec([ev_cntmm, ev_car, ev_rz], lambda: eng.tensor_copy(out=rows_s[0:1, 72:80],
                                                                         in_=ps[3][0:1, 264:272]))
                ev_q1 = vec([ev_cnt], lambda: eng.memset(rows_s[0:1, 64:72], 0.0))
                ev_ctf = ev_q1
                for m8 in range(8):
                    ev_ctf = vec([ev_ctf], lambda m_=m8: eng.scalar_tensor_tensor(
                        out=rows_s[0:1, 64:72], in0=rows_s[0:1, 72:80], scalar=512.0 * m_,
                        in1=rows_s[0:1, 64:72], op0=ALU.is_gt, op1=ALU.add))
                ev_cc1 = vec([ev_ctf], lambda: eng.tensor_copy(out=rows_s[0:1, 8:16], in_=rows_s[0:1, 64:72]))
                ev_cs = ev_cc1
                for dlev in (1, 2, 4):
                    ev_cs = vec([ev_cs], lambda d_=dlev: eng.tensor_tensor(
                        out=rows_s[0:1, 8 + d_:16], in0=rows_s[0:1, 8 + d_:16],
                        in1=rows_s[0:1, 8:16 - d_], op=ALU.add))
                ev_px1 = vec([ev_cs], lambda: eng.tensor_tensor(out=rows_s[0:1, 0:8],
                                                                in0=rows_s[0:1, 8:16],
                                                                in1=rows_s[0:1, 64:72], op=ALU.subtract))
                ev_px = vec([ev_px1], lambda: eng.tensor_scalar(out=rows_s[0:1, 0:8], in0=rows_s[0:1, 0:8],
                                                                scalar1=512.0, scalar2=-1.0,
                                                                op0=ALU.mult, op1=ALU.add))
                ev_t2z = vec([ev_px], lambda: eng.memset(rows_s[0:1, 16:32], 0.0))
                ev_t2 = ev_t2z
                for e8 in range(8):
                    ev_t2 = vec([ev_t2], lambda e_=e8: eng.scalar_tensor_tensor(
                        out=rows_s[0:1, 16:32], in0=iotar[0:1, 0:16],
                        scalar=rows_s[0:1, 8 + e_:9 + e_], in1=rows_s[0:1, 16:32],
                        op0=ALU.is_ge, op1=ALU.add))
                ev_t2c = vec([ev_t2], lambda: eng.tensor_scalar(out=rows_s[0:1, 16:32], in0=rows_s[0:1, 16:32],
                                                                scalar1=7.0, scalar2=None, op0=ALU.min))
                ev_woq = vec([ev_t2c], lambda: eng.tensor_scalar(out=rows_s[0:1, 32:48], in0=rows_s[0:1, 16:32],
                                                                 scalar1=1024.0, scalar2=None, op0=ALU.mult))
                ev_wod = vec([ev_woq], lambda: eng.tensor_scalar(out=rows_s[0:1, 48:64], in0=rows_s[0:1, 16:32],
                                                                 scalar1=2048.0, scalar2=None, op0=ALU.mult))
                ev_rw1 = dma('DGL', [ev_car], lambda: eng.dma_start(out=srtB, in_=car_s[:]))
                ev_rw2 = dma('DGL', [ev_wod, ev_px], lambda: eng.dma_start(out=srtC, in_=rows_s[:]))
                ev_rb1 = dma('DGL', [ev_rw1], lambda: eng.dma_start(
                    out=carb_s[:].rearrange("p j e -> p (j e)"),
                    in_=srtB.rearrange("j e -> (j e)").unsqueeze(0).broadcast_to([128, 256])))
                ev_rb2 = dma('DGL', [ev_rw2], lambda: eng.dma_start(
                    out=rowb_s[:],
                    in_=srtC.rearrange("r c -> (r c)").unsqueeze(0).broadcast_to([128, 128])))
                ev_wq1 = vec([ev_rb2], lambda: eng.tensor_tensor(
                    out=pos_s[:, 0:16], in0=rowb_s[:, 32:48],
                    in1=iotac[:, 0:1].broadcast_to([128, 16]), op=ALU.add))
                ev_wq2 = vec([ev_wq1], lambda: eng.tensor_copy(out=widxQ[:], in_=pos_s[:, 0:16]))
                ev_wd1 = vec([ev_wq2], lambda: eng.tensor_tensor(
                    out=pos_s[:, 0:16], in0=rowb_s[:, 48:64],
                    in1=iotac[:, 0:1].broadcast_to([128, 16]), op=ALU.add))
                ev_wd2 = vec([ev_wd1], lambda: eng.tensor_copy(out=widxD[:], in_=pos_s[:, 0:16]))
                ev_p1 = vec([ev_rb1, ev_rk, ev_wd2], lambda: eng.tensor_tensor(
                    out=rk_s[:].rearrange("p j e -> p (j e)"),
                    in0=rk_s[:].rearrange("p j e -> p (j e)"),
                    in1=carb_s[:].rearrange("p j e -> p (j e)"), op=ALU.add))
                ev_p2 = vec([ev_p1, ev_rb2], lambda: eng.tensor_tensor(
                    out=rk_s[:], in0=rk_s[:],
                    in1=rowb_s[:, 0:8].unsqueeze(1).broadcast_to([128, 32, 8]), op=ALU.add))
                ev_pos = ev_p2
                for j32 in range(32):
                    ev_pos = vec([ev_pos], lambda j_=j32: eng.scalar_tensor_tensor(
                        out=carb_s[:, j_, :], in0=rk_s[:, j_, :], scalar=1.0,
                        in1=mk_s[:, j_, :], op0=ALU.mult, op1=ALU.mult,
                        accum_out=pos_s[:, j_:j_ + 1]))
                ev_posi = vec([ev_pos], lambda: eng.tensor_copy(out=posi_s[:], in_=pos_s[:]))

                do_scatter = SPARSE_STAGE >= 2
                do_routed = SPARSE_STAGE >= 3
                do_return = SPARSE_STAGE >= 4
                ev_zz = vec([], lambda: eng.memset(zt_s[:], 0.0))
                scat_state = {"ev": None}

                def scatter_gen():
                    ev_z = None
                    for a0 in range(TN * 512 // 128):
                        ev_z = dma('DGS', [ev_zz] if a0 == 0 else [],
                                   lambda a_=a0: eng.dma_start(
                                       out=sorted_scr[a_ * 128:(a_ + 1) * 128, :], in_=zt_s[:]))
                        if a0 % 4 == 3:
                            yield
                    ev_sc = ev_z
                    for cc2 in range(NCH if do_scatter else 0):
                        ev_ldx = dma('DXL', [ev_sc, ev_zz],
                                     lambda c_=cc2: eng.dma_start(out=zt_s[:], in_=xs_scr[c_]))
                        ev_sc = dma('DGS', [ev_ldx, ev_posi, ev_z],
                                    lambda c_=cc2: eng.indirect_dma_start(
                                        out=sorted_scr,
                                        out_offset=bass.IndirectOffsetOnAxis(posi_s[:, c_:c_ + 1], 0),
                                        in_=zt_s[:], in_offset=None))
                        scat_state["ev"] = ev_sc
                        yield
                    while True:
                        yield

                scat_pump = scatter_gen()
                scat_units = [0]

                def pump_scatter(n):
                    for _ in range(n):
                        next(scat_pump)
                        scat_units[0] += 1

            if PHASE_B:
                NTT = SWTOK // 128
                NT5 = SWTOK // 512
                ev_wgu_free = [None, None]
                ev_wgu_ld = [None, None]
                ev_wd_free = [None, None]
                ev_wd_ld = [None, None]
                ev_acc_st = None
                ev_in_free = None
                gidx = 0
                didx = 0
                widx = 0
                ev_gu_grp = {}
                ev_act_grp = {}
                ev_sil_grp = {}
                ev_down_grp = {}
                ev_stt_grp = {}
                ev_actT_free = None

                passes = ["sh"] if SPARSE else (["sh"] + list(range(NE)))
                for s in range(NSW):
                    b, half = s // 2, s % 2
                    cc0 = b * (SEQ // CH) + half * NTT
                    t0 = half * SWTOK
                    if not SPARSE:
                        ev_xsl = dma('DSB', evA_done + [ev_in_free],
                                     lambda cc0_=cc0: eng.dma_start(
                                         out=xsT_s[:].rearrange("p k (j t) -> p k j t", j=NTT),
                                         in_=xsT_scr[cc0_:cc0_ + NTT].rearrange("j p k t -> p k j t")))
                        ev_mkl = dma('DSB', [ev_acc_st],
                                     lambda cc0_=cc0: eng.dma_start(
                                         out=masks_s[:], in_=msk_scr[cc0_:cc0_ + NTT].rearrange("j p e -> p j e")))
                    else:
                        ev_xsl = None
                        ev_mkl = None
                    ev_lnl = dma('DSB', evA_done + [ev_in_free],
                                   lambda cc0_=cc0: eng.dma_start(
                                       out=lnT_s[:].rearrange("p k (j t) -> p k j t", j=NTT),
                                       in_=lnT_scr[cc0_:cc0_ + NTT].rearrange("j p k t -> p k j t")))
                    ev_accl = dma('DSB', [ev_acc_st],
                                    lambda b_=b, t0_=t0: eng.dma_start(
                                        out=acc[:].rearrange("p j d -> p j d"),
                                        in_=out_d[b_, t0_:t0_ + SWTOK, :].rearrange("(j p) d -> p j d", p=128)))
                    ev_stt_last = None
                    for ex in passes:
                        inT = lnT_s if ex == "sh" else xsT_s
                        ev_inT = ev_lnl if ex == "sh" else ev_xsl
                        g_src = shg_d if ex == "sh" else rtg_d[ex]
                        u_src = shu_d if ex == "sh" else rtu_d[ex]
                        d_src = shd_d if ex == "sh" else rtd_d[ex]
                        for j in range(NK):
                            jp = j % 2
                            ev_wgu_ld[jp] = dma(
                                'DGW', [ev_wgu_free[jp]] + (evA_done if gidx == 0 else []),
                                [lambda j_=j, jp_=jp, g_s=g_src: eng.dma_start(
                                    out=wgu[jp_][0][:],
                                    in_=g_s.rearrange("(k p) i -> p k i", p=128)[:, :, 256 * j_:256 * j_ + 256]),
                                 lambda j_=j, jp_=jp, u_s=u_src: eng.dma_start(
                                    out=wgu[jp_][1][:],
                                    in_=u_s.rearrange("(k p) i -> p k i", p=128)[:, :, 256 * j_:256 * j_ + 256])])
                            for m_loc in range(2):
                                m = 2 * j + m_loc
                                for tt in range(NT5):
                                    gp = gidx % 2
                                    pg_, pu_ = ps[0 + 2 * gp], ps[1 + 2 * gp]
                                    extra = (evA_done + ([ev_rk, ev_car, ev_cnt] if SPARSE else [])) if gidx < 2 else []
                                    ev_mm = tens([ev_wgu_ld[jp], ev_inT,
                                                  ev_act_grp.get(gidx - 2), ev_sil_grp.get(gidx - 2)] + extra,
                                                 lambda m_=m_loc, tt_=tt, jp_=jp, pg__=pg_, pu__=pu_, inT_=inT: (
                                                     [eng.matmul(out=pg__[:],
                                                                 lhsT=wgu[jp_][0][:, k, m_ * 128:m_ * 128 + 128],
                                                                 rhs=inT_[:, k, tt_ * 512:tt_ * 512 + 512],
                                                                 start=(k == 0), stop=(k == NK - 1)) for k in range(NK)],
                                                     [eng.matmul(out=pu__[:],
                                                                 lhsT=wgu[jp_][1][:, k, m_ * 128:m_ * 128 + 128],
                                                                 rhs=inT_[:, k, tt_ * 512:tt_ * 512 + 512],
                                                                 start=(k == 0), stop=(k == NK - 1)) for k in range(NK)])[-1][-1])
                                    ev_gu_grp[gidx] = ev_mm
                                    ev_sil = sca([ev_mm, ev_act_grp.get(gidx - 2)] + (evA_done if gidx < 2 else []),
                                                 lambda gp_=gp, pg__=pg_: eng.activation(out=sgb[gp_][:], in_=pg__[:],
                                                                                         func=AF.Silu))
                                    ev_sil_grp[gidx] = ev_sil
                                    ev_act = vec([ev_sil, ev_actT_free] + (evA_done if gidx < 2 else []),
                                                 lambda m__=m, tt_=tt, gp_=gp, pu__=pu_: eng.tensor_tensor(
                                                     out=actT[:, m__, tt_ * 512:tt_ * 512 + 512],
                                                     in0=sgb[gp_][:], in1=pu__[:], op=ALU.mult))
                                    ev_act_grp[gidx] = ev_act
                                    gidx += 1
                            ev_wgu_free[jp] = ev_gu_grp[gidx - 1]
                            if SPARSE and (s >= 1 or scat_units[0] < 15):
                                pump_scatter(2)
                        ev_all_act = ev_act_grp[gidx - 1]
                        for hh in range(2):
                            wp = widx % 2
                            ev_wd_ld[wp] = dma('DGD', [ev_wd_free[wp]] + (evA_done if widx < 2 else []),
                                                 lambda hh_=hh, wp_=wp, d_s=d_src: eng.dma_start(
                                                     out=wdb[wp_][:],
                                                     in_=d_s.rearrange("(m p) h -> p m h", p=128)[:, :, hh_ * 512:hh_ * 512 + 512]))
                            for ct in range(NTT):
                                dp = didx % 2
                                pd_ = (ps[4], ps[6])[dp]
                                extra = evA_done if didx < 2 else []
                                ev_dmm = tens([ev_wd_ld[wp], ev_all_act, ev_stt_grp.get(didx - 2)] + extra,
                                              lambda ct_=ct, wp_=wp, pd__=pd_: [
                                                  eng.matmul(out=pd__[:], lhsT=actT[:, m, ct_ * 128:ct_ * 128 + 128],
                                                             rhs=wdb[wp_][:, m, :], start=(m == 0), stop=(m == NM - 1))
                                                  for m in range(NM)][-1])
                                ev_down_grp[didx] = ev_dmm
                                mc = onescol[:, 0:1] if ex == "sh" else masks_s[:, ct, ex:ex + 1]
                                ev_stt = vec([ev_dmm, ev_accl, ev_mkl],
                                             lambda ct_=ct, hh_=hh, pd__=pd_, mc_=mc: eng.scalar_tensor_tensor(
                                                 out=acc[:, ct_, hh_ * 512:hh_ * 512 + 512],
                                                 in0=pd__[:], scalar=mc_,
                                                 in1=acc[:, ct_, hh_ * 512:hh_ * 512 + 512],
                                                 op0=ALU.mult, op1=ALU.add))
                                ev_stt_grp[didx] = ev_stt
                                ev_stt_last = ev_stt
                                didx += 1
                            ev_wd_free[wp] = ev_down_grp[didx - 1]
                            widx += 1
                        ev_actT_free = ev_down_grp[didx - 1]
                    ev_acc_st = dma('DST', [ev_stt_last],
                                      lambda b_=b, t0_=t0: eng.dma_start(
                                          out=out_d[b_, t0_:t0_ + SWTOK, :].rearrange("(j p) d -> p j d", p=128),
                                          in_=acc[:]))
                    ev_in_free = ev_gu_grp[gidx - 1]

                pump_scatter(15 + NCH + 4 - scat_units[0])
                ev_scatter_done = scat_state["ev"]
                ev_sh_pe = ev_gu_grp[gidx - 1]
                ev_sh_sil = ev_sil_grp[gidx - 1]
                ev_sh_act = ev_act_grp[gidx - 1]
                ev_sh_stt = ev_stt_grp[didx - 1]
                ev_sh_st = ev_acc_st
                rtg_flat = rtg_d.rearrange("e h i -> (e h) i")
                rtu_flat = rtu_d.rearrange("e h i -> (e h) i")
                rtd_flat = rtd_d.rearrange("e i h -> (e i) h")
                ev_sxld = [None, None]
                ev_sxT = [None, None]
                ev_gu_ld = [[None, None], [None, None]]
                ev_wd_ldr = None
                ev_guf = [[ev_sh_pe, ev_sh_st], [ev_sh_pe, ev_sh_st]]
                ev_wdf = [ev_sh_pe, ev_sh_st]
                ev_act_r = {}
                ev_sil_r = {}
                ev_gu_r = {}
                ev_dn_r = {}
                ev_ev_r = {}
                ev_st_r = {}
                gi = 0
                di = 0
                ev_pbfr = [ev_pbf[0], ev_pbf[1]]
                ev_sxfree = [ev_sh_pe, ev_sh_pe]
                for j in range(TN if do_routed else 0):
                    jb = j % 2
                    ev_sxld[jb] = dma('DSB', [ev_scatter_done, ev_sh_st] + ([ev_sxfree[jb]] if not isinstance(ev_sxfree[jb], list) else ev_sxfree[jb]),
                                      lambda j_=j, jb_=jb: eng.dma_start(
                                          out=sx_tm[jb_][:],
                                          in_=sorted_scr[512 * j_:512 * (j_ + 1), :].rearrange(
                                              "(a p) d -> p a d", p=128)))
                    ev_tx = None
                    for a4 in range(4):
                        ev_tt = tens([ev_sxld[jb], ev_pbfr[a4 % 2]],
                                     lambda jb_=jb, a_=a4, pbx=pbs[a4 % 2]: [
                                         eng.transpose(out=pbx[:, jj * 128:(jj + 1) * 128],
                                                       in_=sx_tm[jb_][:, a_, jj * 128:(jj + 1) * 128],
                                                       identity=identb[:]) for jj in range(8)][-1])
                        ev_tx = vec([ev_tt] + ([ev_sxT[jb]] if a4 == 0 else []),
                                    lambda jb_=jb, a_=a4, pbx=pbs[a4 % 2]: eng.tensor_copy(
                                        out=sxT[jb_][:, :, a_ * 128:(a_ + 1) * 128],
                                        in_=pbx[:].rearrange("p (k t) -> p k t", k=8)))
                        ev_pbfr[a4 % 2] = ev_tx
                    ev_sxT[jb] = ev_tx
                    for ih in range(2):
                        wb_ = (2 * j + ih) % 2
                        ev_gu_ld[wb_][0] = dma('DGW', (ev_guf[wb_] if isinstance(ev_guf[wb_], list) else [ev_guf[wb_]]) + [ev_posi],
                                               [(lambda j_=j, ih_=ih, wb2=wb_, k_=k2: eng.indirect_dma_start(
                                                   out=wgr[wb2][:, k_, :], out_offset=None,
                                                   in_=rtg_flat,
                                                   in_offset=bass.IndirectOffsetOnAxis(widxQ[:, j_:j_ + 1], 0),
                                                   element_offset=k_ * 128 * 2048 + ih_ * 1024))
                                                for k2 in range(NK)])
                        ev_gu_ld[wb_][1] = dma('DGW', [],
                                               [(lambda j_=j, ih_=ih, wb2=wb_, k_=k2: eng.indirect_dma_start(
                                                   out=wur[wb2][:, k_, :], out_offset=None,
                                                   in_=rtu_flat,
                                                   in_offset=bass.IndirectOffsetOnAxis(widxQ[:, j_:j_ + 1], 0),
                                                   element_offset=k_ * 128 * 2048 + ih_ * 1024))
                                                for k2 in range(NK)])
                        for m_loc in range(8):
                            m = ih * 8 + m_loc
                            gp2 = gi % 2
                            pg_, pu_ = ps[0 + 2 * gp2], ps[1 + 2 * gp2]
                            ev_mm = tens([ev_gu_ld[wb_][1], ev_sxT[jb],
                                          ev_act_r.get(gi - 2), ev_sil_r.get(gi - 2),
                                          ev_sh_sil if gi < 2 else None, ev_sh_act if gi < 2 else None],
                                         lambda m_=m_loc, wb2=wb_, jb_=jb, pg__=pg_, pu__=pu_: (
                                             [eng.matmul(out=pg__[:],
                                                         lhsT=wgr[wb2][:, k, m_ * 128:m_ * 128 + 128],
                                                         rhs=sxT[jb_][:, k, :],
                                                         start=(k == 0), stop=(k == NK - 1)) for k in range(NK)],
                                             [eng.matmul(out=pu__[:],
                                                         lhsT=wur[wb2][:, k, m_ * 128:m_ * 128 + 128],
                                                         rhs=sxT[jb_][:, k, :],
                                                         start=(k == 0), stop=(k == NK - 1)) for k in range(NK)])[-1][-1])
                            ev_gu_r[gi] = ev_mm
                            ev_sil2 = sca([ev_mm, ev_act_r.get(gi - 2)],
                                          lambda gp_=gp2, pg__=pg_: eng.activation(
                                              out=sgb_r[gp_][:], in_=pg__[:], func=AF.Silu))
                            ev_sil_r[gi] = ev_sil2
                            ev_act2 = vec([ev_sil2, ev_dn_r.get(di - 1) if m == 0 and j > 0 else None,
                                           ev_sh_stt if gi < 2 else None],
                                          lambda m__=m, gp_=gp2, pu__=pu_: eng.tensor_tensor(
                                              out=actT_r[:, m__, :], in0=sgb_r[gp_][:], in1=pu__[:],
                                              op=ALU.mult))
                            ev_act_r[gi] = ev_act2
                            gi += 1
                        ev_guf[wb_] = ev_gu_r[gi - 1]
                    ev_all_act2 = ev_act_r[gi - 1]
                    ev_wd_ldr = dma('DGD', (ev_wdf if isinstance(ev_wdf, list) else [ev_wdf]) + [ev_posi],
                                    [(lambda j_=j, m_=m2: eng.indirect_dma_start(
                                        out=wdr[:, m_, :], out_offset=None,
                                        in_=rtd_flat,
                                        in_offset=bass.IndirectOffsetOnAxis(widxD[:, j_:j_ + 1], 0),
                                        element_offset=m_ * 128 * 1024))
                                     for m2 in range(NM)])
                    for ct in range(4):
                        for hh in range(2):
                            dp2 = di % 2
                            pd_ = (ps[4], ps[6])[dp2]
                            ev_dmm = tens([ev_wd_ldr, ev_all_act2, ev_ev_r.get(di - 2),
                                           ev_sh_stt if di < 2 else None],
                                          lambda ct_=ct, hh_=hh, pd__=pd_: [
                                              eng.matmul(out=pd__[:], lhsT=actT_r[:, m, ct_ * 128:ct_ * 128 + 128],
                                                         rhs=wdr[:, m, hh_ * 512:hh_ * 512 + 512],
                                                         start=(m == 0), stop=(m == NM - 1))
                                              for m in range(NM)][-1])
                            ev_dn_r[di] = ev_dmm
                            ev_ev = vec([ev_dmm, ev_st_r.get(di - 8), ev_sh_stt if di < 8 else None,
                                         ev_sh_st if di < 8 else None],
                                        lambda ct_=ct, hh_=hh, pd__=pd_: eng.tensor_copy(
                                            out=dstg[:, ct_, hh_ * 512:hh_ * 512 + 512], in_=pd__[:]))
                            ev_ev_r[di] = ev_ev
                            ev_st_r[di] = ev_ev
                            di += 1
                    ev_wdf = ev_dn_r[di - 1]
                    ev_sxfree[jb] = ev_gu_r[gi - 1]
                    ev_sost = dma('DST', [ev_ev_r[di - 1]],
                                  lambda j_=j: eng.dma_start(
                                      out=sout_scr[512 * j_:512 * (j_ + 1), :].rearrange("(a p) d -> p a d", p=128),
                                      in_=dstg[:]))
                    for dd in range(di - 8, di):
                        ev_st_r[dd] = ev_sost
                if DEBUG_SPARSE:
                    dma('DST', [ev_posi], lambda: eng.dma_start(out=dbg1, in_=pos_s[:]))
                    dma('DST', [ev_posi], lambda: eng.dma_start(out=dbg2, in_=rows_s[0:1, :]))
                if DEBUG_SPARSE and do_scatter:
                    ev_d3l = dma('DSB', [ev_scatter_done, ev_sh_st],
                                 lambda: eng.dma_start(out=sx_tm[0][:],
                                                       in_=sorted_scr[0:512, :].rearrange("(a p) d -> p a d", p=128)))
                    ev_d3c = vec([ev_d3l], lambda: eng.tensor_copy(out=dstg[:], in_=sx_tm[0][:]))
                    dma('DST', [ev_d3c], lambda: eng.dma_start(
                        out=dbg3.rearrange("(a p) d -> p a d", p=128), in_=dstg[:]))
                evB_done = ("DST", cnt["DST"])
                ev_ga = None
                ev_add = None
                def _ret_tile(rp_):
                    if rp_ == 0:
                        return dstg[:]
                    return actT_r[:].rearrange("p m t -> p (m t)").bitcast(F32).rearrange(
                        "p (a d) -> p a d", a=4)

                ev_adds = [None, None]
                for q4 in range(NCH // 4 if do_return else 0):
                    rp = q4 % 2
                    ev_ga = None
                    for qq in range(4):
                        cc3 = q4 * 4 + qq
                        ev_ga = dma('DGS', [evB_done, ev_posi] + ([ev_adds[rp]] if qq == 0 else []),
                                    lambda c_=cc3, qq_=qq, rp_=rp: eng.indirect_dma_start(
                                        out=_ret_tile(rp_)[:, qq_, :],
                                        out_offset=None, in_=sout_scr,
                                        in_offset=bass.IndirectOffsetOnAxis(posi_s[:, c_:c_ + 1], 0)))
                    b3, c3 = (q4 * 4) // (SEQ // CH), (q4 * 4) % (SEQ // CH)
                    ev_adds[rp] = dma('DGA', [ev_ga],
                                      lambda b_=b3, c_=c3, rp_=rp: eng.dma_start(
                                          out=out_d[b_, c_ * CH:(c_ + 4) * CH, :].rearrange(
                                              "(a p) d -> p a d", p=128),
                                          in_=_ret_tile(rp_),
                                          accum_op=ALU.add))

            if who == "sync":
                eng.wait_ge(sems["DST"], cnt["DST"])
                eng.wait_ge(sems["DGA"], cnt["DGA"])

        @block.sync
        def _(eng):
            emit("sync", eng)

        @block.gpsimd
        def _(eng):
            emit("gpsimd", eng)

        @block.vector
        def _(eng):
            emit("vector", eng)

        @block.scalar
        def _(eng):
            emit("scalar", eng)

        @block.tensor
        def _(eng):
            emit("tensor", eng)

    return nc


def core_inputs(inputs, core):
    c = make_consts()
    return dict(
        x=np.ascontiguousarray(inputs["L0_attn_x"][2 * core:2 * core + 2]),
        wq=inputs["L0_attn_Wq"], wk=inputs["L0_attn_Wk"], wv=inputs["L0_attn_Wv"],
        wo=inputs["L0_attn_Wo"], wgate=inputs["L0_attn_Wg"],
        bgate=np.ascontiguousarray(inputs["L0_attn_bg"].reshape(1, NH)),
        gate_w=inputs["L0_ffn_gate_w"], shg=inputs["L0_ffn_sh_gate"],
        shu=inputs["L0_ffn_sh_up"], shd=inputs["L0_ffn_sh_down"],
        rtg=inputs["L0_ffn_rt_gate"], rtu=inputs["L0_ffn_rt_up"], rtd=inputs["L0_ffn_rt_down"],
        **c,
    )


def kernel(**inputs):
    inputs = {k: np.asarray(v, dtype=np.float32) for k, v in inputs.items()}
    nc = build_nc()
    in_maps = [core_inputs(inputs, core) for core in range(8)]
    res = run_bass_kernel_spmd(nc, in_maps, core_ids=list(range(8)))
    return np.concatenate([r["out"] for r in res.results], axis=0)
